# version 03
# baseline (speedup 1.0000x reference)
import numpy as np
import ml_dtypes
from contextlib import ExitStack

import concourse.bass as bass
import concourse.mybir as mybir
from concourse.bass_utils import run_bass_kernel_spmd
from concourse.aot_env import allow_ici_all_to_all

F32 = mybir.dt.float32
BF16 = mybir.dt.bfloat16
U8 = mybir.dt.uint8
I32 = mybir.dt.int32
AF = mybir.ActivationFunctionType
ALU = mybir.AluOpType
AX = mybir.AxisListType.X
BF = ml_dtypes.bfloat16

NCORE = 8
TPC = 1024
D = 2048
MF = 16
SEQ = 4096
NOC_IN = 48
HID0 = 8192
IC0 = 64
HID1 = 5461
HID1P = 5504
IC1 = 43
EPS = 1e-5
RG = [list(range(NCORE))]
HV = [(0, 512), (512, 1024)]



def _slab3(w, pc=128, cc=128):
    K, N = w.shape
    kc, ncc = K // pc, N // cc
    return np.ascontiguousarray(
        w.reshape(kc, pc, ncc, cc).transpose(2, 1, 0, 3).reshape(ncc, pc, kc * cc))


def _prep(inputs):
    H = {}
    x = np.asarray(inputs["L0_attn_X"], np.float32).reshape(NCORE * TPC, D)
    H["x0"] = [np.ascontiguousarray(x[c * TPC:(c + 1) * TPC].T) for c in range(NCORE)]

    perm = np.concatenate([np.arange(0, 1024, 2), np.arange(1, 1024, 2), np.arange(1024, 2048)])
    W_in = np.asarray(inputs["L0_attn_W_in"], np.float32)
    b_in = np.asarray(inputs["L0_attn_b_in"], np.float32)
    W_in_p = np.concatenate([W_in[:, 0:D][:, perm], W_in[:, D:2 * D][:, perm],
                             W_in[:, 2 * D:3 * D][:, perm]], axis=1)
    b_in_p = np.concatenate([b_in[0:D][perm], b_in[D:2 * D][perm], b_in[2 * D:3 * D][perm]])
    W_out_p = np.asarray(inputs["L0_attn_W_out"], np.float32)[perm, :]
    H["w_in_t"] = _slab3(W_in_p)
    H["b_in_t"] = np.ascontiguousarray(b_in_p.reshape(NOC_IN, 128).T)
    H["w_out_t"] = _slab3(W_out_p)
    H["b_out_t"] = np.ascontiguousarray(np.asarray(inputs["L0_attn_b_out"], np.float32).reshape(MF, 128).T)
    H["gamma_t"] = np.ascontiguousarray(np.asarray(inputs["L0_attn_gamma"], np.float32).reshape(MF, 128).T)
    H["beta_t"] = np.ascontiguousarray(np.asarray(inputs["L0_attn_beta"], np.float32).reshape(MF, 128).T)
    invf = 10000.0 ** (-np.arange(0, 1024, 2, dtype=np.float64) / 1024.0)
    H["rot_cos"], H["rot_sin"] = [], []
    for c in range(NCORE):
        pos = np.arange(TPC, dtype=np.float64) + (c % 4) * TPC
        ang = invf[:, None] * pos[None, :]
        H["rot_cos"].append(np.ascontiguousarray(np.cos(ang).astype(np.float32)))
        H["rot_sin"].append(np.ascontiguousarray(np.sin(ang).astype(np.float32)))

    def u8(a):
        a = np.ascontiguousarray(np.asarray(a, np.int32))
        return np.ascontiguousarray(a.view(np.uint8).reshape(a.shape[0], a.shape[1], 4)[:, :, 0])
    H["gw_t"] = _slab3(u8(inputs["L0_ffn_gate_w"]).T)
    H["uw_t"] = _slab3(u8(inputs["L0_ffn_up_w"]).T)
    H["dw_t"] = _slab3(u8(inputs["L0_ffn_down_w"]).T)
    H["gsc_t"] = np.ascontiguousarray(np.asarray(inputs["L0_ffn_gate_scale"], np.float32).reshape(IC0, 128).T)
    H["usc_t"] = np.ascontiguousarray(np.asarray(inputs["L0_ffn_up_scale"], np.float32).reshape(IC0, 128).T)
    H["dsc_t"] = np.ascontiguousarray(np.asarray(inputs["L0_ffn_down_scale"], np.float32).reshape(MF, 128).T)

    wqs = np.asarray(inputs["L1_attn_wq_sem"], np.float32)
    wks = np.asarray(inputs["L1_attn_wk_sem"], np.float32)
    wqg = np.asarray(inputs["L1_attn_wq_geo"], np.float32)
    wkg = np.asarray(inputs["L1_attn_wk_geo"], np.float32)
    wv = np.asarray(inputs["L1_attn_wv"], np.float32)
    gl = np.asarray(inputs["L1_attn_gate_logit"], np.float32)
    H["wq_t"], H["wk_t"], H["wv_t"], H["gl"] = [], [], [], []
    for c in range(NCORE):
        hs = [2 * c, 2 * c + 1]
        q = np.concatenate([np.concatenate([wqg[:, h * 64:h * 64 + 64], wqs[:, h * 64:h * 64 + 64]], 1) for h in hs], 1)
        k = np.concatenate([np.concatenate([wkg[:, h * 64:h * 64 + 64], wks[:, h * 64:h * 64 + 64]], 1) for h in hs], 1)
        v = np.concatenate([wv[:, h * 128:h * 128 + 128] for h in hs], 1)
        H["wq_t"].append(np.ascontiguousarray(q.reshape(MF, 128, 256)))
        H["wk_t"].append(np.ascontiguousarray(k.reshape(MF, 128, 256)))
        H["wv_t"].append(np.ascontiguousarray(v.reshape(MF, 128, 256)))
        H["gl"].append(np.ascontiguousarray(gl[hs].reshape(2, 1)))
    H["wo_t"] = _slab3(np.asarray(inputs["L1_attn_wo"], np.float32))
    invg = 10000.0 ** (-np.arange(32, dtype=np.float64) / 32.0)
    ang = np.arange(SEQ, dtype=np.float64)[None, :] * invg[:, None]
    cg, sg = np.cos(ang), np.sin(ang)
    H["rope1"] = np.ascontiguousarray(np.concatenate([cg, cg, -sg, sg], 0).astype(np.float32))
    kk = np.arange(128)[:, None]
    qq = np.arange(512)[None, :]
    mk = [(kk <= qq - 128 * d_).astype(np.float32) for d_ in range(4)]
    mk.append(np.ones((128, 512), np.float32))
    H["masks"] = np.ascontiguousarray(np.stack(mk).astype(BF))

    w1T = np.zeros((D, HID1P), np.float32); w1T[:, :HID1] = np.asarray(inputs["L1_ffn_w1"], np.float32).T
    w3T = np.zeros((D, HID1P), np.float32); w3T[:, :HID1] = np.asarray(inputs["L1_ffn_w3"], np.float32).T
    w2T = np.zeros((HID1P, D), np.float32); w2T[:HID1, :] = np.asarray(inputs["L1_ffn_w2"], np.float32).T
    H["w1f"] = _slab3(w1T)
    H["w3f"] = _slab3(w3T)
    H["w2f"] = _slab3(w2T)

    H["w1a"] = [np.ascontiguousarray(w1T[256 * c:256 * c + 256].reshape(2, 128, HID1P))
                for c in range(NCORE)]
    H["w3a"] = [np.ascontiguousarray(w3T[256 * c:256 * c + 256].reshape(2, 128, HID1P))
                for c in range(NCORE)]
    H["w2a"] = [np.ascontiguousarray(w2T[:, 256 * c:256 * c + 256].reshape(IC1, 128, 256))
                for c in range(NCORE)]
    H["offs"] = [np.array([[(c // 4) * 2048, 256 * c, (1 - c // 4) * 2048, 2 * c]], np.int32)
                 for c in range(NCORE)]
    return H




_UN = [0]


def _u(nm):
    _UN[0] += 1
    return f"{nm}_{_UN[0]}"


class Prog:
    def __init__(self, nc, stack):
        self.nc, self.stack = nc, stack
        self.h, self.n = {}, {}

    def sem(self, name):
        if name not in self.h:
            self.h[name] = self.stack.enter_context(self.nc.semaphore(name))
            self.n[name] = 0
        return self.h[name]

    def inc(self, inst, name, amt=1):
        inst.then_inc(self.sem(name), amt)
        self.n[name] += amt
        return self.n[name]

    def wait(self, eng, name, val=None):
        eng.wait_ge(self.sem(name), self.n[name] if val is None else val)

    def base(self, *names):
        return {k: self.n.get(k, 0) for k in names}


class Ctx:
    def __init__(self, nc, stack):
        self.nc, self.stack = nc, stack
        self.P = Prog(nc, stack)
        e = stack.enter_context
        self.fscr = e(nc.sbuf_tensor("fscr", [1, 4], F32))
        self.dumr = e(nc.sbuf_tensor("dumr", [1, 256], BF16))
        self.scr_ps = e(nc.psum_tensor("scr_ps_g", [1, 256], F32))
        self.ones1_f32 = e(nc.sbuf_tensor("ones1_f32", [1, 128], F32))
        self.ones128_bf = e(nc.sbuf_tensor("ones128_bf", [128, 1], BF16))
        self.m128_bf = e(nc.sbuf_tensor("m128_bf", [1, 128], BF16))
        self.eps_t = e(nc.sbuf_tensor("eps_t", [1, 1], F32))

    def init_consts(self):
        nc = self.nc
        with nc.Block() as b:
            @b.vector
            def _(v):
                v.memset(self.ones1_f32[:], 1.0)
                v.memset(self.dumr[:], 0.0)
                v.memset(self.ones128_bf[:], 1.0)
                v.memset(self.m128_bf[:], -128.0)
                v.memset(self.eps_t[:], EPS)


def col(t2, i):
    return t2[i].rearrange("p -> p 1")


def slot2(buf, i, w):
    return buf[:, (i % 2) * w:(i % 2 + 1) * w]



def emit_stats(C, src_fn, a_dst, b_dst, mode="plain", apl=None):
    nc, P = C.nc, C.P
    with ExitStack() as sc:
        e = sc.enter_context
        sum_ps = e(nc.psum_tensor(_u("st_sum_ps"), [1, TPC], F32))
        sq_ps = e(nc.psum_tensor(_u("st_sq_ps"), [1, TPC], F32))
        xt = e(nc.sbuf_tensor(_u("st_xt"), [128, 2 * TPC], F32))
        xb = e(nc.sbuf_tensor(_u("st_xb"), [128, 2 * TPC], BF16))
        xq = e(nc.sbuf_tensor(_u("st_xq"), [128, 2 * TPC], BF16))
        mean = xt[0:1, 0:TPC]
        var = xt[0:1, TPC:2 * TPC]
        aa = e(nc.sbuf_tensor(_u("st_aa"), [1, TPC], F32))
        bb = e(nc.sbuf_tensor(_u("st_bb"), [1, TPC], F32))
        if mode == "l0inner":
            sA = e(nc.sbuf_tensor(_u("st_sA"), [1, TPC], F32))
            sB = e(nc.sbuf_tensor(_u("st_sB"), [1, TPC], F32))
        B = P.base("ld", "v", "s", "t")

        with nc.Block() as b:
            @b.sync
            def _(sy):
                for mf in range(MF):
                    if mf >= 2:
                        sy.wait_ge(P.sem("v"), B["v"] + (mf - 2) + 1)
                    P.inc(sy.dma_start(out=slot2(xt, mf, TPC), in_=src_fn(mf)), "ld", 16)

            @b.vector
            def _(v):
                for mf in range(MF):
                    v.wait_ge(P.sem("ld"), B["ld"] + 16 * (mf + 1))
                    if mf >= 2:
                        v.wait_ge(P.sem("s"), B["s"] + (mf - 2) + 1)
                        v.wait_ge(P.sem("t"), B["t"] + (mf - 2) + 1)
                    P.inc(v.tensor_copy(slot2(xb, mf, TPC), slot2(xt, mf, TPC)), "v")

            @b.scalar
            def _(s):
                for mf in range(MF):
                    s.wait_ge(P.sem("v"), B["v"] + mf + 1)
                    if mf >= 2:
                        s.wait_ge(P.sem("t"), B["t"] + (mf - 2) + 1)
                    P.inc(s.activation(slot2(xq, mf, TPC), slot2(xb, mf, TPC), AF.Square), "s")
                P.inc(s.activation(C.fscr[0:1, 2:3], C.fscr[0:1, 2:3], AF.Square), "s")

            @b.tensor
            def _(tn):
                for mf in range(MF):
                    tn.wait_ge(P.sem("s"), B["s"] + mf + 2)
                    st, sp = (mf == 0), (mf == MF - 1)
                    for (h0, h1) in HV:
                        tn.matmul(sum_ps[0:1, h0:h1], C.ones128_bf[:], slot2(xb, mf, TPC)[:, h0:h1],
                                  start=st, stop=sp)
                        last = tn.matmul(sq_ps[0:1, h0:h1], C.ones128_bf[:], slot2(xq, mf, TPC)[:, h0:h1],
                                         start=st, stop=sp)
                    P.inc(last, "t")
                drain(C, tn, "t")

        B = P.base("v", "s")
        if mode == "l0inner":
            with nc.Block() as b:
                @b.vector
                def _(v):
                    P.wait(v, "t")
                    vsync(P, v, v.tensor_copy(sA[:], sum_ps[:]))
                    vsync(P, v, v.tensor_copy(sB[:], sq_ps[:]))
                    vsync(P, v, v.tensor_scalar(mean[:], sA[:], 1.0 / D, None, ALU.mult))
                    vsync(P, v, v.tensor_scalar(sB[:], sB[:], 1.0 / D, None, ALU.mult))
                    vsync(P, v, v.tensor_tensor(var[:], mean[:], mean[:], ALU.mult))
                    P.inc(v.tensor_tensor(var[:], sB[:], var[:], ALU.subtract), "v")
                    v.wait_ge(P.sem("s"), B["s"] + 1)
                    vsync(P, v, v.reciprocal(aa[:], sA[:]))
                    vsync(P, v, v.tensor_copy(sA[:], aa[:]))
                    vsync(P, v, v.tensor_scalar(sB[:], var[:], EPS, None, ALU.add))
                    vsync(P, v, v.reciprocal(sB[:], sB[:]))
                    P.inc(v.tensor_tensor(var[:], var[:], sB[:], ALU.mult), "v")
                    v.wait_ge(P.sem("s"), B["s"] + 2)
                    vsync(P, v, v.reciprocal(var[:], var[:]))
                    vsync(P, v, v.tensor_tensor(aa[:], aa[:], var[:], ALU.mult))
                    P.inc(v.tensor_tensor(bb[:], mean[:], aa[:], ALU.mult), "v")

                @b.scalar
                def _(s):
                    s.wait_ge(P.sem("v"), B["v"] + 1)
                    P.inc(s.activation(sA[:], var[:], AF.Sqrt, bias=C.eps_t[:]), "s")
                    s.wait_ge(P.sem("v"), B["v"] + 2)
                    P.inc(s.activation(var[:], var[:], AF.Sqrt, bias=C.eps_t[:]), "s")
        else:
            with nc.Block() as b:
                @b.vector
                def _(v):
                    P.wait(v, "t")
                    vsync(P, v, v.tensor_copy(mean[:], sum_ps[:]))
                    vsync(P, v, v.tensor_scalar(mean[:], mean[:], 1.0 / D, None, ALU.mult))
                    vsync(P, v, v.tensor_copy(var[:], sq_ps[:]))
                    vsync(P, v, v.tensor_scalar(var[:], var[:], 1.0 / D, None, ALU.mult))
                    vsync(P, v, v.tensor_tensor(aa[:], mean[:], mean[:], ALU.mult))
                    P.inc(v.tensor_tensor(var[:], var[:], aa[:], ALU.subtract), "v")
                    v.wait_ge(P.sem("s"), B["s"] + 1)
                    vsync(P, v, v.reciprocal(var[:], aa[:]))
                    P.inc(v.tensor_tensor(bb[:], mean[:], var[:], ALU.mult), "v")

                @b.scalar
                def _(s):
                    s.wait_ge(P.sem("v"), B["v"] + 1)
                    P.inc(s.activation(aa[:], var[:], AF.Sqrt, bias=C.eps_t[:]), "s")

        if mode == "l0inner":
            if apl is not None:
                emit_bcast(C, [aa, bb, sA], [a_dst, b_dst, apl])
            else:
                emit_bcast(C, [aa, bb], [a_dst, b_dst])
        else:
            emit_bcast(C, [var, bb], [a_dst, b_dst])


def emit_bcast(C, rows, dsts):
    nc, P = C.nc, C.P
    with ExitStack() as sc:
        bc_ps = sc.enter_context(nc.psum_tensor(_u("bc_ps"), [128, TPC], F32))
        B = P.base("v", "t")
        with nc.Block() as b:
            @b.tensor
            def _(tn):
                for i, row in enumerate(rows):
                    if i >= 1:
                        tn.wait_ge(P.sem("v"), B["v"] + i)
                    for (h0, h1) in HV:
                        last = tn.matmul(bc_ps[:, h0:h1], C.ones1_f32[:], row[0:1, h0:h1],
                                         start=True, stop=True)
                    P.inc(last, "t")
                    drain(C, tn, "t")

            @b.vector
            def _(v):
                for i, dst in enumerate(dsts):
                    v.wait_ge(P.sem("t"), B["t"] + 2 * (i + 1))
                    P.inc(v.tensor_copy(dst[:], bc_ps[:]), "v")


def emit_bcast_col(C, scalar_dram_off, dsts_col):
    nc, P = C.nc, C.P
    with nc.Block() as b:
        @b.sync
        def _(sy):
            for (t, off), dst in zip(scalar_dram_off, dsts_col):
                P.inc(sy.dma_start(out=dst, in_=bass.AP(t, off, [[0, 128], [1, 1]])), "ld", 16)


def emit_apply_pre(C, x_fn, dst_big, a_bc, b_bc, gam=None, bet=None, extra_g=None):
    nc, P = C.nc, C.P
    with ExitStack() as sc:
        e = sc.enter_context
        xt = e(nc.sbuf_tensor(_u("ap_xt"), [128, 2 * TPC], F32))
        t1 = e(nc.sbuf_tensor(_u("ap_t1"), [128, 2 * TPC], F32))
        B = P.base("ld", "v", "s")
        VPM = 2

        with nc.Block() as b:
            @b.sync
            def _(sy):
                for mf in range(MF):
                    if mf >= 2:
                        sy.wait_ge(P.sem("v"), B["v"] + VPM * (mf - 2) + 1)
                    P.inc(sy.dma_start(out=slot2(xt, mf, TPC), in_=x_fn(mf)), "ld", 16)

            @b.vector
            def _(v):
                for mf in range(MF):
                    v.wait_ge(P.sem("ld"), B["ld"] + 16 * (mf + 1))
                    if gam is not None and mf >= 2:
                        v.wait_ge(P.sem("s"), B["s"] + (mf - 2) + 1)
                    P.inc(v.tensor_tensor(slot2(t1, mf, TPC), slot2(xt, mf, TPC), a_bc[:], ALU.mult), "v")
                    if gam is None:
                        dst = dst_big[:, mf * TPC:(mf + 1) * TPC]
                        P.inc(v.tensor_tensor(dst, slot2(t1, mf, TPC), b_bc[:], ALU.subtract), "v")
                    else:
                        P.inc(v.tensor_tensor(slot2(t1, mf, TPC), slot2(t1, mf, TPC), b_bc[:], ALU.subtract), "v")

            @b.scalar
            def _(s):
                if gam is None:
                    return
                for mf in range(MF):
                    s.wait_ge(P.sem("v"), B["v"] + VPM * (mf + 1))
                    dst = dst_big[:, mf * TPC:(mf + 1) * TPC]
                    P.inc(s.activation(dst, slot2(t1, mf, TPC), AF.Identity,
                                       scale=gam[:, mf:mf + 1], bias=bet[:, mf:mf + 1]), "s")
                P.inc(s.activation(C.fscr[0:1, 2:3], C.fscr[0:1, 2:3], AF.Square), "s")

            if extra_g is not None:
                @b.gpsimd
                def _(g):
                    for mf in range(MF):
                        if gam is None:
                            g.wait_ge(P.sem("v"), B["v"] + VPM * (mf + 1))
                        else:
                            g.wait_ge(P.sem("s"), B["s"] + mf + 1)
                        extra_g(g, mf)


def emit_apply_post(C, f_fn, x_fn, x_out, a_bc, b_bc):
    nc, P = C.nc, C.P
    with ExitStack() as sc:
        e = sc.enter_context
        ft = e(nc.sbuf_tensor(_u("po_ft"), [128, 2 * TPC], F32))
        xt = e(nc.sbuf_tensor(_u("po_xt"), [128, 2 * TPC], F32))
        xo = e(nc.sbuf_tensor(_u("po_xo"), [128, 2 * TPC], F32))
        B = P.base("ld", "v", "g")
        VPM = 3

        with nc.Block() as b:
            @b.sync
            def _(sy):
                for mf in range(MF):
                    if mf >= 2:
                        sy.wait_ge(P.sem("v"), B["v"] + VPM * (mf - 2) + 3)
                    P.inc(sy.dma_start(out=slot2(ft, mf, TPC), in_=f_fn(mf)), "ld", 16)
                    P.inc(sy.dma_start(out=slot2(xt, mf, TPC), in_=x_fn(mf)), "ld", 16)

            @b.vector
            def _(v):
                for mf in range(MF):
                    v.wait_ge(P.sem("ld"), B["ld"] + 32 * (mf + 1))
                    if mf >= 2:
                        v.wait_ge(P.sem("g"), B["g"] + 16 * (mf - 1))
                    v.tensor_tensor(slot2(ft, mf, TPC), slot2(ft, mf, TPC), a_bc[:], ALU.mult)
                    v.tensor_tensor(slot2(ft, mf, TPC), slot2(ft, mf, TPC), b_bc[:], ALU.subtract)
                    P.inc(v.tensor_tensor(slot2(xo, mf, TPC), slot2(xt, mf, TPC), slot2(ft, mf, TPC),
                                          ALU.add), "v", VPM)

            @b.gpsimd
            def _(g):
                for mf in range(MF):
                    g.wait_ge(P.sem("v"), B["v"] + VPM * (mf + 1))
                    P.inc(g.dma_start(out=x_out[mf * 128:(mf + 1) * 128, :],
                                      in_=slot2(xo, mf, TPC)), "g", 16)


def drain(C, tn, sem):
    return C.P.inc(tn.matmul(C.scr_ps[0:1, 0:256], C.dumr[0:1, 0:1], C.dumr[0:1, 0:256],
                             start=True, stop=True), sem)


def vsync(P, eng, inst):
    t = P.inc(inst, "vx")
    eng.wait_ge(P.sem("vx"), t)


def fence(C, sems=("ld", "g", "cc")):
    nc, P = C.nc, C.P
    with nc.Block() as b:
        @b.vector
        def _(v):
            P.inc(v.memset(C.fscr[0:1, 0:1], 0.0), "fzv")
            P.wait(v, "fzv")

        @b.scalar
        def _(s):
            for nm in sems:
                if nm in P.n and P.n[nm] > 0:
                    P.wait(s, nm)
            P.inc(s.activation(C.fscr[0:1, 1:2], C.fscr[0:1, 1:2], AF.Abs), "fzs")
            P.wait(s, "fzs")



def ph_l0a(C, io):
    nc, P = C.nc, C.P
    x_in, x_out, f_dram = io["x_in"], io["x_out"], io["f_dram"]

    def xsrc(mf):
        return x_in[mf * 128:(mf + 1) * 128, :]

    with ExitStack() as sc:
        e = sc.enter_context
        qb = e(nc.sbuf_tensor("qb", [128, MF * TPC], BF16))
        kb = e(nc.sbuf_tensor("kb", [128, MF * TPC], BF16))
        vb = e(nc.sbuf_tensor("vb", [128, MF * TPC], BF16))
        gam = e(nc.sbuf_tensor("gam", [128, MF], F32))
        bet = e(nc.sbuf_tensor("bet", [128, MF], F32))
        bo = e(nc.sbuf_tensor("bo", [128, MF], F32))
        bin_sb = e(nc.sbuf_tensor("bin_sb", [128, NOC_IN], F32))
        offs_sb = e(nc.sbuf_tensor("offs_sb", [1, 4], I32))
        a0 = e(nc.sbuf_tensor("a0", [128, TPC], F32))
        b0 = e(nc.sbuf_tensor("b0", [128, TPC], F32))
        apl = e(nc.sbuf_tensor("apl", [128, TPC], F32))
        af = e(nc.sbuf_tensor("af_l0a", [128, TPC], F32))
        bfc = e(nc.sbuf_tensor("bf_l0a", [128, TPC], F32))
        A_sb = e(nc.sbuf_tensor("A_sb", [128, MF], F32))
        z128 = e(nc.sbuf_tensor("z128", [128, MF], F32))
        rb = e(nc.gpsimd.register("rb"))
        rb2 = e(nc.gpsimd.register("rb2"))

        def tile(buf, mf):
            return buf[:, mf * TPC:(mf + 1) * TPC]

        with nc.Block() as b:
            @b.sync
            def _(sy):
                P.inc(sy.dma_start(out=gam[:], in_=io["gamma_t"][:]), "ld", 16)
                P.inc(sy.dma_start(out=bet[:], in_=io["beta_t"][:]), "ld", 16)
                P.inc(sy.dma_start(out=bo[:], in_=io["b_out_t"][:]), "ld", 16)
                P.inc(sy.dma_start(out=bin_sb[:], in_=io["b_in_t"][:]), "ld", 16)
                P.inc(sy.dma_start(out=offs_sb[:], in_=io["offs"][:]), "ld", 16)

            @b.vector
            def _(v):
                v.memset(z128[:], 0.0)
        fence(C)

        with ExitStack() as sxn:
            Xn = sxn.enter_context(nc.sbuf_tensor("Xn", [128, MF * TPC], BF16))
            emit_stats(C, xsrc, a0, b0, mode="l0inner", apl=apl)
            fence(C)
            if "abdbg" in io:
                with nc.Block() as b:
                    @b.gpsimd
                    def _(g):
                        P.inc(g.dma_start(out=io["abdbg"][0:128, :], in_=a0[:]), "g", 16)
                        P.inc(g.dma_start(out=io["abdbg"][128:256, :], in_=b0[:]), "g", 16)
                        P.inc(g.dma_start(out=io["abdbg"][256:384, :], in_=apl[:]), "g", 16)
                fence(C, sems=("g",))
            emit_apply_pre(C, xsrc, Xn, a0, b0, gam=gam, bet=bet)
            fence(C)
            if "xndbg" in io:
                with nc.Block() as b:
                    @b.gpsimd
                    def _(g):
                        for mf in range(MF):
                            P.inc(g.dma_start(out=io["xndbg"][mf * 128:(mf + 1) * 128, :],
                                              in_=Xn[:, mf * TPC:(mf + 1) * TPC]), "g", 16)
                fence(C, sems=("g",))

            with ExitStack() as s4:
                e4 = s4.enter_context
                wsl = e4(nc.sbuf_tensor("wsl", [128, 2 * D], F32))
                wbf = e4(nc.sbuf_tensor("wbf", [128, 2 * D], BF16))
                ps = [e4(nc.psum_tensor(f"qkv_ps{i}", [128, TPC], F32)) for i in range(2)]
                B = P.base("ld", "cst", "mm", "mmd", "ev")
                with nc.Block() as b:
                    @b.sync
                    def _(sy):
                        for oc in range(NOC_IN):
                            if oc >= 2:
                                sy.wait_ge(P.sem("cst"), B["cst"] + oc - 1)
                            P.inc(sy.dma_start(out=slot2(wsl, oc, D), in_=io["w_in_t"][oc]), "ld", 16)

                    @b.vector
                    def _(v):
                        for oc in range(NOC_IN):
                            v.wait_ge(P.sem("ld"), B["ld"] + 16 * (oc + 1))
                            if oc >= 2:
                                v.wait_ge(P.sem("mm"), B["mm"] + oc - 1)
                            P.inc(v.tensor_copy(slot2(wbf, oc, D), slot2(wsl, oc, D)), "cst")
                            if oc >= 1:
                                po = oc - 1
                                v.wait_ge(P.sem("mmd"), B["mmd"] + po + 1)
                                dst = (qb if po < 16 else (kb if po < 32 else vb))
                                P.inc(v.tensor_scalar(tile(dst, po % 16), ps[po % 2][:],
                                                      bin_sb[:, po:po + 1], None, ALU.add), "ev")
                        po = NOC_IN - 1
                        v.wait_ge(P.sem("mmd"), B["mmd"] + po + 1)
                        P.inc(v.tensor_scalar(tile(vb, po % 16), ps[po % 2][:],
                                              bin_sb[:, po:po + 1], None, ALU.add), "ev")

                    @b.tensor
                    def _(tn):
                        P.wait(tn, "s")
                        for oc in range(NOC_IN):
                            tn.wait_ge(P.sem("cst"), B["cst"] + oc + 1)
                            if oc >= 2:
                                tn.wait_ge(P.sem("ev"), B["ev"] + oc - 1)
                            for mf in range(MF):
                                for (h0, h1) in HV:
                                    last = tn.matmul(ps[oc % 2][:, h0:h1],
                                                     slot2(wbf, oc, D)[:, mf * 128:(mf + 1) * 128],
                                                     tile(Xn, mf)[:, h0:h1],
                                                     start=(mf == 0), stop=(mf == MF - 1))
                            P.inc(last, "mm")
                            drain(C, tn, "mmd")
                fence(C)

        with ExitStack() as s5:
            e5 = s5.enter_context
            rcs = e5(nc.sbuf_tensor("rcs", [128, 8 * TPC], F32))
            u1 = e5(nc.sbuf_tensor("u1", [128, TPC], F32))
            u2 = e5(nc.sbuf_tensor("u2", [128, TPC], F32))
            u3 = e5(nc.sbuf_tensor("u3", [128, TPC], F32))
            u4 = e5(nc.sbuf_tensor("u4", [128, TPC], F32))
            with nc.Block() as b:
                @b.sync
                def _(sy):
                    for t in range(4):
                        P.inc(sy.dma_start(out=rcs[:, t * TPC:(t + 1) * TPC],
                                           in_=io["rot_cos"][t * 128:(t + 1) * 128, :]), "ld", 16)
                        P.inc(sy.dma_start(out=rcs[:, (4 + t) * TPC:(5 + t) * TPC],
                                           in_=io["rot_sin"][t * 128:(t + 1) * 128, :]), "ld", 16)
            fence(C)
            with nc.Block() as b:
                @b.vector
                def _(v):
                    for buf in (kb, vb):
                        for t in range(4):
                            cosT = rcs[:, t * TPC:(t + 1) * TPC]
                            sinT = rcs[:, (4 + t) * TPC:(5 + t) * TPC]
                            v.tensor_tensor(u1[:], tile(buf, t), cosT, ALU.mult)
                            v.tensor_tensor(u2[:], tile(buf, t + 4), sinT, ALU.mult)
                            v.tensor_tensor(u3[:], tile(buf, t), sinT, ALU.mult)
                            v.tensor_tensor(u4[:], tile(buf, t + 4), cosT, ALU.mult)
                            v.tensor_tensor(tile(buf, t), u1[:], u2[:], ALU.subtract)
                            inst = v.tensor_tensor(tile(buf, t + 4), u3[:], u4[:], ALU.add)
                    P.inc(inst, "v")
            fence(C)

        if "kdbg" in io:
            with nc.Block() as b:
                @b.gpsimd
                def _(g):
                    for mf in range(MF):
                        P.inc(g.dma_start(out=io["kdbg"][mf * 128:(mf + 1) * 128, :],
                                          in_=tile(kb, mf)), "g", 16)
                        P.inc(g.dma_start(out=io["vdbg"][mf * 128:(mf + 1) * 128, :],
                                          in_=tile(vb, mf)), "g", 16)
                        P.inc(g.dma_start(out=io["qdbg"][mf * 128:(mf + 1) * 128, :],
                                          in_=tile(qb, mf)), "g", 16)
            fence(C, sems=("g",))

        with ExitStack() as s6:
            e6 = s6.enter_context
            sqk = e6(nc.sbuf_tensor("sqk", [128, 2 * TPC], BF16))
            sqv = e6(nc.sbuf_tensor("sqv", [128, 2 * TPC], BF16))
            nk_ps = e6(nc.psum_tensor("nk_ps", [1, TPC], F32))
            nv_ps = e6(nc.psum_tensor("nv_ps", [1, TPC], F32))
            nk_sb = e6(nc.sbuf_tensor("nk_sb", [1, TPC], F32))
            nv_sb = e6(nc.sbuf_tensor("nv_sb", [1, TPC], F32))
            W_bc = e6(nc.sbuf_tensor("W_bc", [128, TPC], F32))
            pk = e6(nc.sbuf_tensor("pk", [128, TPC], BF16))
            pw = e6(nc.sbuf_tensor("pw", [128, TPC], F32))
            apart = e6(nc.sbuf_tensor("apart", [128, MF], F32))
            B = P.base("sq", "nmm")
            with nc.Block() as b:
                @b.scalar
                def _(s):
                    P.wait(s, "v")
                    for mf in range(MF):
                        if mf >= 2:
                            s.wait_ge(P.sem("nmm"), B["nmm"] + mf - 1)
                        s.activation(slot2(sqk, mf, TPC), tile(kb, mf), AF.Square)
                        P.inc(s.activation(slot2(sqv, mf, TPC), tile(vb, mf), AF.Square), "sq")
                    P.inc(s.activation(C.fscr[0:1, 2:3], C.fscr[0:1, 2:3], AF.Square), "sq")

                @b.tensor
                def _(tn):
                    for mf in range(MF):
                        tn.wait_ge(P.sem("sq"), B["sq"] + mf + 2)
                        st, sp = (mf == 0), (mf == MF - 1)
                        for (h0, h1) in HV:
                            tn.matmul(nk_ps[0:1, h0:h1], C.ones128_bf[:],
                                      slot2(sqk, mf, TPC)[:, h0:h1], start=st, stop=sp)
                            last = tn.matmul(nv_ps[0:1, h0:h1], C.ones128_bf[:],
                                             slot2(sqv, mf, TPC)[:, h0:h1], start=st, stop=sp)
                        P.inc(last, "nmm")
                    drain(C, tn, "nmm")
            with nc.Block() as b:
                @b.vector
                def _(v):
                    P.wait(v, "nmm")
                    vsync(P, v, v.tensor_copy(nk_sb[:], nk_ps[:]))
                    vsync(P, v, v.tensor_copy(nv_sb[:], nv_ps[:]))
                    v.tensor_tensor(nk_sb[:], nk_sb[:], nv_sb[:], ALU.mult)
            with nc.Block() as b:
                @b.scalar
                def _(s):
                    s.activation(nv_sb[:], nk_sb[:], AF.Sqrt)
            with nc.Block() as b:
                @b.vector
                def _(v):
                    v.reciprocal(nk_sb[:], nv_sb[:])
            emit_bcast(C, [nk_sb], [W_bc])
            with nc.Block() as b:
                @b.vector
                def _(v):
                    for mf in range(MF):
                        v.tensor_tensor(pk[:], tile(kb, mf), tile(vb, mf), ALU.mult)
                        v.tensor_tensor(pw[:], pk[:], W_bc[:], ALU.mult)
                        inst = v.reduce_sum(apart[:, mf:mf + 1], pw[:], axis=AX)
                    P.inc(inst, "v")
            fence(C)
            with nc.Block() as b:
                @b.gpsimd
                def _(g):
                    P.wait(g, "v")
                    g.reg_load(rb, offs_sb[0:1, 0:1])
                    g.reg_load(rb2, offs_sb[0:1, 2:3])
                    P.inc(g.dma_start(out=bass.AP(io["ar_in"], rb, [[16, 128], [1, 16]]),
                                      in_=apart[:]), "g", 16)
                    P.inc(g.dma_start(out=bass.AP(io["ar_in"], rb2, [[16, 128], [1, 16]]),
                                      in_=z128[:]), "g", 16)
                    P.wait(g, "g")
                    P.inc(g.collective_compute(
                        "AllReduce", ALU.add, replica_groups=RG,
                        ins=[io["ar_in"][:]], outs=[io["ar_out"][:]]), "cc")
                    P.wait(g, "cc")
                    P.inc(g.dma_start(out=A_sb[:], in_=bass.AP(io["ar_out"], rb, [[16, 128], [1, 16]])),
                          "g", 16)
            fence(C)
            if "adbg" in io:
                with nc.Block() as b:
                    @b.gpsimd
                    def _(g):
                        P.inc(g.dma_start(out=io["adbg"][:], in_=A_sb[:]), "g", 16)
                fence(C, sems=("g",))
            with nc.Block() as b:
                @b.vector
                def _(v):
                    v.tensor_scalar(A_sb[:], A_sb[:], 0.5, 0.5, ALU.mult, ALU.add)
            with nc.Block() as b:
                @b.vector
                def _(v):
                    for mf in range(MF):
                        inst = v.tensor_scalar(tile(qb, mf), tile(qb, mf), A_sb[:, mf:mf + 1],
                                               None, ALU.mult)
                    P.inc(inst, "v")
            fence(C, sems=())
            if "qdbg2" in io:
                with nc.Block() as b:
                    @b.gpsimd
                    def _(g):
                        for mf in range(MF):
                            P.inc(g.dma_start(out=io["qdbg2"][mf * 128:(mf + 1) * 128, :],
                                              in_=tile(qb, mf)), "g", 16)
                fence(C, sems=("g",))

        with ExitStack() as s8:
            e8 = s8.enter_context
            wsl = e8(nc.sbuf_tensor("wsl8", [128, 2 * D], F32))
            wbf = e8(nc.sbuf_tensor("wbf8", [128, 2 * D], BF16))
            xt = e8(nc.sbuf_tensor("xt8", [128, 2 * TPC], F32))
            fo = e8(nc.sbuf_tensor("fo8", [128, 2 * TPC], F32))
            xr = e8(nc.sbuf_tensor("xr8", [128, TPC], F32))
            fs = e8(nc.sbuf_tensor("fs8", [128, 2 * TPC], F32))
            ghost = e8(nc.psum_tensor("ghost8", [128, TPC], F32))
            ps = [e8(nc.psum_tensor(f"wo_ps{i}", [128, TPC], F32)) for i in range(2)]
            for oc in range(MF):
                with nc.Block() as b:
                    @b.sync
                    def _(sy, oc=oc):
                        P.inc(sy.dma_start(out=slot2(wsl, oc, D), in_=io["w_out_t"][oc]), "ld", 16)
                        P.inc(sy.dma_start(out=slot2(xt, oc, TPC), in_=xsrc(oc)), "ld", 16)
                fence(C, sems=("ld",))
                with nc.Block() as b:
                    @b.vector
                    def _(v, oc=oc):
                        P.inc(v.tensor_copy(slot2(wbf, oc, D), slot2(wsl, oc, D)), "cst")
                with nc.Block() as b:
                    @b.tensor
                    def _(tn, oc=oc):
                        P.wait(tn, "cst")
                        P.wait(tn, "v")
                        for qf in range(MF):
                            for (h0, h1) in HV:
                                last = tn.matmul(ps[oc % 2][:, h0:h1],
                                                 slot2(wbf, oc, D)[:, qf * 128:(qf + 1) * 128],
                                                 tile(qb, qf)[:, h0:h1],
                                                 start=(qf == 0), stop=(qf == MF - 1))
                        P.inc(last, "mm")
                with nc.Block() as b:
                    @b.scalar
                    def _(s, oc=oc):
                        P.wait(s, "mm")
                        P.inc(s.activation(slot2(fo, oc, TPC), ps[oc % 2][:], AF.Identity,
                                           bias=bo[:, oc:oc + 1]), "evs")
                with nc.Block() as b:
                    @b.vector
                    def _(v, oc=oc):
                        P.wait(v, "evs")
                        v.tensor_tensor(xr[:], slot2(xt, oc, TPC), apl[:], ALU.mult)
                        P.inc(v.tensor_tensor(slot2(fs, oc, TPC), slot2(fo, oc, TPC), xr[:],
                                              ALU.add), "evv")
                with nc.Block() as b:
                    @b.gpsimd
                    def _(g, oc=oc):
                        P.wait(g, "evv")
                        P.inc(g.dma_start(out=f_dram[oc * 128:(oc + 1) * 128, :],
                                          in_=slot2(fs, oc, TPC)), "g", 16)
                fence(C, sems=("g",))
            fence(C)

        emit_stats(C, lambda mf: f_dram[mf * 128:(mf + 1) * 128, :], af, bfc)
        fence(C)
        emit_apply_post(C, lambda mf: f_dram[mf * 128:(mf + 1) * 128, :], xsrc, x_out, af, bfc)
        fence(C)
        if "q3dbg" in io:
            with nc.Block() as b:
                @b.gpsimd
                def _(g):
                    for mf in range(MF):
                        P.inc(g.dma_start(out=io["q3dbg"][mf * 128:(mf + 1) * 128, :],
                                          in_=tile(qb, mf)), "g", 16)
                    P.wait(g, "g")



def ph_l0f(C, io):
    nc, P = C.nc, C.P
    x_in, x_out, f_dram = io["x_in"], io["x_out"], io["f_dram"]

    def xsrc(mf):
        return x_in[mf * 128:(mf + 1) * 128, :]

    with ExitStack() as sc:
        e = sc.enter_context
        gsc = e(nc.sbuf_tensor("gsc", [128, IC0], F32))
        usc = e(nc.sbuf_tensor("usc", [128, IC0], F32))
        dsc = e(nc.sbuf_tensor("dsc", [128, MF], F32))
        sch = sc.enter_context(ExitStack())
        h_sb = sch.enter_context(nc.sbuf_tensor("h_sb", [128, IC0 * TPC], BF16))
        hi_bf = sch.enter_context(nc.sbuf_tensor("hi_bf", [1, TPC], BF16))
        lo_bf = sch.enter_context(nc.sbuf_tensor("lo_bf", [1, TPC], BF16))

        def htile(ic):
            return h_sb[:, ic * TPC:(ic + 1) * TPC]

        with nc.Block() as b:
            @b.sync
            def _(sy):
                P.inc(sy.dma_start(out=gsc[:], in_=io["gsc_t"][:]), "ld", 16)
                P.inc(sy.dma_start(out=usc[:], in_=io["usc_t"][:]), "ld", 16)
                P.inc(sy.dma_start(out=dsc[:], in_=io["dsc_t"][:]), "ld", 16)

        with ExitStack() as s12:
            ln1 = s12.enter_context(nc.sbuf_tensor("ln1", [128, MF * TPC], BF16))
            tsum_bf = s12.enter_context(nc.sbuf_tensor("tsum_bf", [1, TPC], BF16))

            def ltile(mf):
                return ln1[:, mf * TPC:(mf + 1) * TPC]

            with ExitStack() as sab:
                a1 = sab.enter_context(nc.sbuf_tensor("a1", [128, TPC], F32))
                b1 = sab.enter_context(nc.sbuf_tensor("b1", [128, TPC], F32))
                emit_stats(C, xsrc, a1, b1)
                fence(C)
                emit_apply_pre(C, xsrc, ln1, a1, b1)
                fence(C)

            with ExitStack() as s3:
                ts_ps = s3.enter_context(nc.psum_tensor("ts_ps", [1, TPC], F32))
                tsum_f = s3.enter_context(nc.sbuf_tensor("tsum_f", [1, TPC], F32))
                with nc.Block() as b:
                    @b.tensor
                    def _(tn):
                        for mf in range(MF):
                            for (h0, h1) in HV:
                                last = tn.matmul(ts_ps[0:1, h0:h1], C.ones128_bf[:],
                                                 ltile(mf)[:, h0:h1],
                                                 start=(mf == 0), stop=(mf == MF - 1))
                        P.inc(last, "t")
                        drain(C, tn, "t")
                with nc.Block() as b:
                    @b.vector
                    def _(v):
                        P.wait(v, "t")
                        vsync(P, v, v.tensor_copy(tsum_f[:], ts_ps[:]))
                        v.tensor_copy(tsum_bf[:], tsum_f[:])
                fence(C, sems=())

            with ExitStack() as s4:
                e4 = s4.enter_context
                g8 = e4(nc.sbuf_tensor("g8", [128, 2 * D], U8))
                u8b = e4(nc.sbuf_tensor("u8b", [128, 2 * D], U8))
                gbf = e4(nc.sbuf_tensor("gbf", [128, 2 * D], BF16))
                ubf = e4(nc.sbuf_tensor("ubf", [128, 2 * D], BF16))
                sl = e4(nc.sbuf_tensor("sl", [128, 2 * TPC], BF16))
                ul = e4(nc.sbuf_tensor("ul", [128, 2 * TPC], BF16))
                ps_g = e4(nc.psum_tensor("ps_g", [128, TPC], F32))
                ps_u = e4(nc.psum_tensor("ps_u", [128, TPC], F32))
                B = P.base("ld", "cst", "mm", "mmd", "evs", "evv")
                with nc.Block() as b:
                    @b.sync
                    def _(sy):
                        for ic in range(IC0):
                            if ic >= 2:
                                sy.wait_ge(P.sem("cst"), B["cst"] + 2 * (ic - 2) + 2)
                            P.inc(sy.dma_start(out=slot2(g8, ic, D), in_=io["gw_t"][ic]), "ld", 16)
                            P.inc(sy.dma_start(out=slot2(u8b, ic, D), in_=io["uw_t"][ic]), "ld", 16)

                    @b.vector
                    def _(v):
                        for ic in range(IC0):
                            v.wait_ge(P.sem("ld"), B["ld"] + 32 * (ic + 1))
                            if ic >= 2:
                                v.wait_ge(P.sem("mm"), B["mm"] + ic - 1)
                            v.tensor_copy(slot2(gbf, ic, D), slot2(g8, ic, D))
                            P.inc(v.tensor_copy(slot2(ubf, ic, D), slot2(u8b, ic, D)), "cst", 2)
                            if ic >= 1:
                                po = ic - 1
                                v.wait_ge(P.sem("mmd"), B["mmd"] + po + 1)
                                v.tensor_scalar(slot2(ul, po, TPC), ps_u[:], usc[:, po:po + 1],
                                                None, ALU.mult)
                                v.wait_ge(P.sem("evs"), B["evs"] + po + 1)
                                P.inc(v.tensor_tensor(htile(po), slot2(sl, po, TPC),
                                                      slot2(ul, po, TPC), ALU.mult), "evv")
                        po = IC0 - 1
                        v.wait_ge(P.sem("mmd"), B["mmd"] + po + 1)
                        v.tensor_scalar(slot2(ul, po, TPC), ps_u[:], usc[:, po:po + 1], None, ALU.mult)
                        v.wait_ge(P.sem("evs"), B["evs"] + po + 1)
                        P.inc(v.tensor_tensor(htile(po), slot2(sl, po, TPC), slot2(ul, po, TPC),
                                              ALU.mult), "evv")

                    @b.scalar
                    def _(s):
                        for ic in range(IC0):
                            s.wait_ge(P.sem("mmd"), B["mmd"] + ic + 1)
                            if ic >= 2:
                                s.wait_ge(P.sem("evv"), B["evv"] + ic - 1)
                            P.inc(s.activation(slot2(sl, ic, TPC), ps_g[:], AF.Silu,
                                               scale=gsc[:, ic:ic + 1]), "evs")

                    @b.tensor
                    def _(tn):
                        for ic in range(IC0):
                            tn.wait_ge(P.sem("cst"), B["cst"] + 2 * (ic + 1))
                            if ic >= 1:
                                tn.wait_ge(P.sem("evs"), B["evs"] + ic)
                                tn.wait_ge(P.sem("evv"), B["evv"] + ic)
                            for (h0, h1) in HV:
                                for hc in range(MF):
                                    tn.matmul(ps_g[:, h0:h1],
                                              slot2(gbf, ic, D)[:, hc * 128:(hc + 1) * 128],
                                              ltile(hc)[:, h0:h1], start=(hc == 0), stop=False)
                                tn.matmul(ps_g[:, h0:h1], C.m128_bf[:], tsum_bf[0:1, h0:h1],
                                          start=False, stop=True)
                                for hc in range(MF):
                                    tn.matmul(ps_u[:, h0:h1],
                                              slot2(ubf, ic, D)[:, hc * 128:(hc + 1) * 128],
                                              ltile(hc)[:, h0:h1], start=(hc == 0), stop=False)
                                last = tn.matmul(ps_u[:, h0:h1], C.m128_bf[:], tsum_bf[0:1, h0:h1],
                                                 start=False, stop=True)
                            P.inc(last, "mm")
                            drain(C, tn, "mmd")
                fence(C, sems=())

        with ExitStack() as s5:
            hs_ps = s5.enter_context(nc.psum_tensor("hs_ps", [1, TPC], F32))
            hs_f = s5.enter_context(nc.sbuf_tensor("hs_f", [1, TPC], F32))
            lo_f = s5.enter_context(nc.sbuf_tensor("lo_f", [1, TPC], F32))
            with nc.Block() as b:
                @b.tensor
                def _(tn):
                    for ic in range(IC0):
                        for (h0, h1) in HV:
                            last = tn.matmul(hs_ps[0:1, h0:h1], C.ones128_bf[:],
                                             h_sb[:, ic * TPC + h0:ic * TPC + h1],
                                             start=(ic == 0), stop=(ic == IC0 - 1))
                    P.inc(last, "t")
                    drain(C, tn, "t")
            with nc.Block() as b:
                @b.vector
                def _(v):
                    P.wait(v, "t")
                    vsync(P, v, v.tensor_copy(hs_f[:], hs_ps[:]))
                    vsync(P, v, v.tensor_copy(hi_bf[:], hs_f[:]))
                    vsync(P, v, v.tensor_tensor(lo_f[:], hs_f[:], hi_bf[:], ALU.subtract))
                    v.tensor_copy(lo_bf[:], lo_f[:])
            fence(C, sems=())

        with ExitStack() as s6:
            e6 = s6.enter_context
            d8 = e6(nc.sbuf_tensor("d8", [128, 2 * HID0], U8))
            dbf = e6(nc.sbuf_tensor("dbf", [128, 2 * HID0], BF16))
            fo = e6(nc.sbuf_tensor("fo_l0f", [128, 2 * TPC], F32))
            ps_d = [e6(nc.psum_tensor(f"ps_d{i}", [128, TPC], F32)) for i in range(2)]
            for dc in range(MF):
                with nc.Block() as b:
                    @b.sync
                    def _(sy, dc=dc):
                        P.inc(sy.dma_start(out=slot2(d8, dc, HID0), in_=io["dw_t"][dc]), "ld", 16)
                fence(C, sems=("ld",))
                with nc.Block() as b:
                    @b.vector
                    def _(v, dc=dc):
                        P.inc(v.tensor_copy(slot2(dbf, dc, HID0)[:, 0:HID0 // 2],
                                            slot2(d8, dc, HID0)[:, 0:HID0 // 2]), "cst")

                    @b.scalar
                    def _(s, dc=dc):
                        P.inc(s.activation(slot2(dbf, dc, HID0)[:, HID0 // 2:HID0],
                                           slot2(d8, dc, HID0)[:, HID0 // 2:HID0], AF.Identity),
                              "cs2")
                with nc.Block() as b:
                    @b.tensor
                    def _(tn, dc=dc):
                        P.wait(tn, "cst")
                        P.wait(tn, "cs2")
                        for (h0, h1) in HV:
                            for ic in range(IC0):
                                tn.matmul(ps_d[dc % 2][:, h0:h1],
                                          slot2(dbf, dc, HID0)[:, ic * 128:(ic + 1) * 128],
                                          h_sb[:, ic * TPC + h0:ic * TPC + h1],
                                          start=(ic == 0), stop=False)
                            tn.matmul(ps_d[dc % 2][:, h0:h1], C.m128_bf[:], hi_bf[0:1, h0:h1],
                                      start=False, stop=False)
                            last = tn.matmul(ps_d[dc % 2][:, h0:h1], C.m128_bf[:], lo_bf[0:1, h0:h1],
                                             start=False, stop=True)
                        P.inc(last, "mm")
                with nc.Block() as b:
                    @b.scalar
                    def _(s, dc=dc):
                        P.wait(s, "mm")
                        P.inc(s.activation(slot2(fo, dc, TPC), ps_d[dc % 2][:], AF.Identity,
                                           scale=dsc[:, dc:dc + 1]), "evs")
                with nc.Block() as b:
                    @b.gpsimd
                    def _(g, dc=dc):
                        P.wait(g, "evs")
                        P.inc(g.dma_start(out=f_dram[dc * 128:(dc + 1) * 128, :],
                                          in_=slot2(fo, dc, TPC)), "g", 16)
                fence(C, sems=("g",))
            fence(C)

        sch.close()
        af = e(nc.sbuf_tensor("af_l0f", [128, TPC], F32))
        bfc = e(nc.sbuf_tensor("bf_l0f", [128, TPC], F32))
        emit_stats(C, lambda mf: f_dram[mf * 128:(mf + 1) * 128, :], af, bfc)
        fence(C)
        emit_apply_post(C, lambda mf: f_dram[mf * 128:(mf + 1) * 128, :], xsrc, x_out, af, bfc)
        fence(C)


def ph_l1a(C, io):
    nc, P = C.nc, C.P
    x_in, x_out, f_dram = io["x_in"], io["x_out"], io["f_dram"]
    lng, ag_in = io["lng"], io["ag_in"]
    a2a_in, attg = io["a2a_in"], io["attg"]

    def xsrc(mf):
        return x_in[mf * 128:(mf + 1) * 128, :]

    with ExitStack() as sc:
        e = sc.enter_context
        qcat = e(nc.sbuf_tensor("qcat", [128, 2 * 8192], BF16))
        kcat = e(nc.sbuf_tensor("kcat", [128, 2 * 8192], BF16))
        vtm = e(nc.sbuf_tensor("vtm", [128, 64 * 256], BF16))
        gatesc = e(nc.sbuf_tensor("gatesc", [128, 2], F32))
        sig_sb = e(nc.sbuf_tensor("sig_sb", [128, 4], F32))
        msk_sb = e(nc.sbuf_tensor("msk_sb", [128, 5 * 512], BF16))
        rope_sb = e(nc.sbuf_tensor("rope_sb", [64, SEQ], F32))
        rope2_sb = e(nc.sbuf_tensor("rope2_sb", [64, SEQ], F32))
        gl_sb = e(nc.sbuf_tensor("gl_sb", [2, 1], F32))
        gv_sb = e(nc.sbuf_tensor("gv_sb", [2, 2], F32))

        def qc(hl):
            return qcat[:, hl * 8192:(hl + 1) * 8192]

        def kc(hl):
            return kcat[:, hl * 8192:(hl + 1) * 8192]

        with ExitStack() as s12:
            ln2 = s12.enter_context(nc.sbuf_tensor("ln2", [128, MF * TPC], BF16))
            a2 = s12.enter_context(nc.sbuf_tensor("a2", [128, TPC], F32))
            b2 = s12.enter_context(nc.sbuf_tensor("b2", [128, TPC], F32))
            emit_stats(C, xsrc, a2, b2)
            fence(C)

            def eg(g, mf):
                P.inc(g.dma_start(out=ag_in[mf * 128:(mf + 1) * 128, :],
                                  in_=ln2[:, mf * TPC:(mf + 1) * TPC]), "g", 16)
            emit_apply_pre(C, xsrc, ln2, a2, b2, extra_g=eg)
        with nc.Block() as b:
            @b.sync
            def _(sy):
                P.inc(sy.dma_start(out=gl_sb[:], in_=io["gl"][:]), "ld", 16)
                P.inc(sy.dma_start(out=rope_sb[:], in_=io["rope1"][0:64, :]), "ld", 16)
                P.inc(sy.dma_start(out=rope2_sb[:], in_=io["rope1"][64:128, :]), "ld", 16)
                for d_ in range(5):
                    P.inc(sy.dma_start(out=msk_sb[:, d_ * 512:(d_ + 1) * 512],
                                       in_=io["masks"][d_]), "ld", 16)

            @b.gpsimd
            def _(g):
                P.wait(g, "g")
                P.inc(g.collective_compute("AllGather", ALU.bypass, replica_groups=RG,
                                           ins=[ag_in[:]], outs=[lng[:]]), "cc")
        fence(C, sems=("ld",))
        with nc.Block() as b:
            @b.scalar
            def _(s):
                P.wait(s, "ld")
                P.inc(s.activation(gl_sb[:], gl_sb[:], AF.Sigmoid), "s")
        with nc.Block() as b:
            @b.vector
            def _(v):
                P.wait(v, "s")
                v.tensor_scalar(gv_sb[:, 0:1], gl_sb[:], -0.125, 0.125, ALU.mult, ALU.add)
                P.inc(v.tensor_scalar(gv_sb[:, 1:2], gl_sb[:], 0.125, None, ALU.mult), "v")
        with nc.Block() as b:
            @b.gpsimd
            def _(g):
                P.wait(g, "v")
                P.inc(g.dma_start(out=io["gtmp"][:], in_=gv_sb[:]), "g", 16)
        fence(C, sems=("g",))
        with nc.Block() as b:
            @b.sync
            def _(sy):
                for hl in range(2):
                    P.inc(sy.dma_start(out=gatesc[0:64, hl:hl + 1],
                                       in_=bass.AP(io["gtmp"], 2 * hl, [[0, 64], [1, 1]])), "ld", 16)
                    P.inc(sy.dma_start(out=gatesc[64:128, hl:hl + 1],
                                       in_=bass.AP(io["gtmp"], 2 * hl + 1, [[0, 64], [1, 1]])), "ld", 16)
        fence(C)

        with ExitStack() as s4:
            wqb = s4.enter_context(nc.sbuf_tensor("wqb", [128, MF * 256], BF16))
            wkb = s4.enter_context(nc.sbuf_tensor("wkb", [128, MF * 256], BF16))
            wvb = s4.enter_context(nc.sbuf_tensor("wvb", [128, MF * 256], BF16))
            with ExitStack() as s4f:
                wf = s4f.enter_context(nc.sbuf_tensor("wf_stage", [128, MF * 256], F32))
                for (wt, wb) in ((io["wq_t"], wqb), (io["wk_t"], wkb), (io["wv_t"], wvb)):
                    with nc.Block() as b:
                        @b.sync
                        def _(sy, wt=wt):
                            for m in range(MF):
                                P.inc(sy.dma_start(out=wf[:, m * 256:(m + 1) * 256],
                                                   in_=wt[m]), "ld", 16)
                    fence(C, sems=("ld",))
                    with nc.Block() as b:
                        @b.vector
                        def _(v, wb=wb):
                            P.inc(v.tensor_copy(wb[:], wf[:]), "v")
                    fence(C, sems=())

            with ExitStack() as s5:
                e5 = s5.enter_context
                lng_sb = e5(nc.sbuf_tensor("lng_sb", [128, MF * 512], BF16))
                swp = e5(nc.sbuf_tensor("swp", [64, 4 * 512], BF16))
                rp1 = e5(nc.sbuf_tensor("rp1", [64, 512], F32))
                rp2 = e5(nc.sbuf_tensor("rp2", [64, 512], F32))
                ps_q = e5(nc.psum_tensor("ps_q", [128, 2 * 512], F32))
                ps_k = e5(nc.psum_tensor("ps_k", [128, 2 * 512], F32))
                ps_v = e5(nc.psum_tensor("ps_v", [128, 2 * 512], F32))
                B = P.base("ld", "mm", "mmd", "vev", "swp", "rop")
                with nc.Block() as b:
                    @b.sync
                    def _(sy):
                        for ts in range(16):
                            if ts >= 1:
                                sy.wait_ge(P.sem("mm"), B["mm"] + ts)
                            r0 = (ts // 2) * 2048
                            c0 = (ts % 2) * 512
                            for mf in range(MF):
                                P.inc(sy.dma_start(
                                    out=lng_sb[:, mf * 512:(mf + 1) * 512],
                                    in_=lng[r0 + mf * 128:r0 + (mf + 1) * 128, c0:c0 + 512]),
                                    "ld", 16)

                    @b.tensor
                    def _(tn):
                        P.wait(tn, "v")
                        P.wait(tn, "s")
                        for ts in range(16):
                            tn.wait_ge(P.sem("ld"), B["ld"] + 256 * (ts + 1))
                            if ts >= 1:
                                tn.wait_ge(P.sem("vev"), B["vev"] + 5 * ts)
                            for hl in range(2):
                                for (wb, pp) in ((wqb, ps_q), (wkb, ps_k)):
                                    for mf in range(MF):
                                        last = tn.matmul(
                                            pp[:, hl * 512:(hl + 1) * 512],
                                            wb[:, mf * 256 + hl * 128:mf * 256 + (hl + 1) * 128],
                                            lng_sb[:, mf * 512:(mf + 1) * 512],
                                            start=(mf == 0), stop=(mf == MF - 1))
                            for tc in range(4):
                                for mf in range(MF):
                                    last = tn.matmul(
                                        ps_v[:, (tc % 2) * 256 + (tc // 2) * 512:
                                             (tc % 2) * 256 + (tc // 2) * 512 + 256],
                                        lng_sb[:, mf * 512 + tc * 128:mf * 512 + (tc + 1) * 128],
                                        wvb[:, mf * 256:(mf + 1) * 256],
                                        start=(mf == 0), stop=(mf == MF - 1))
                            P.inc(last, "mm")
                            drain(C, tn, "mmd")

                    @b.vector
                    def _(v):
                        for ts in range(16):
                            v.wait_ge(P.sem("mmd"), B["mmd"] + ts + 1)
                            span = slice(ts * 512, (ts + 1) * 512)
                            for hl in range(2):
                                P.inc(v.tensor_scalar(qc(hl)[:, span], ps_q[:, hl * 512:(hl + 1) * 512],
                                                      gatesc[:, hl:hl + 1], None, ALU.mult), "vev")
                                P.inc(v.tensor_copy(kc(hl)[:, span], ps_k[:, hl * 512:(hl + 1) * 512]),
                                      "vev")
                            P.inc(v.tensor_copy(vtm[:, ts * 1024:(ts + 1) * 1024], ps_v[:]), "vev")
                            v.wait_ge(P.sem("swp"), B["swp"] + 128 * (ts + 1))
                            for gi, (buf, hl) in enumerate(((qcat, 0), (qcat, 1), (kcat, 0), (kcat, 1))):
                                geo = (qc(hl) if buf is qcat else kc(hl))[0:64, span]
                                v.tensor_tensor(rp1[:], geo, rope_sb[:, (ts % 8) * 512:(ts % 8 + 1) * 512],
                                                ALU.mult)
                                v.tensor_tensor(rp2[:], swp[:, gi * 512:(gi + 1) * 512],
                                                rope2_sb[:, (ts % 8) * 512:(ts % 8 + 1) * 512],
                                                ALU.mult)
                                P.inc(v.tensor_tensor(geo, rp1[:], rp2[:], ALU.add), "rop")

                    @b.gpsimd
                    def _(g):
                        for ts in range(16):
                            g.wait_ge(P.sem("vev"), B["vev"] + 5 * ts + 4)
                            if ts >= 1:
                                g.wait_ge(P.sem("rop"), B["rop"] + 4 * ts)
                            span = slice(ts * 512, (ts + 1) * 512)
                            for gi, (buf_hl) in enumerate(((0, 0), (0, 1), (1, 0), (1, 1))):
                                isq, hl = buf_hl
                                src = (qc(hl) if isq == 0 else kc(hl))
                                P.inc(g.dma_start(out=swp[0:32, gi * 512:gi * 512 + 512],
                                                  in_=src[32:64, span]), "swp", 16)
                                P.inc(g.dma_start(out=swp[32:64, gi * 512:gi * 512 + 512],
                                                  in_=src[0:32, span]), "swp", 16)
                fence(C)

        with ExitStack() as s6:
            e6 = s6.enter_context
            sel = e6(nc.sbuf_tensor("sel", [128, 2], BF16))
            sqq = e6(nc.sbuf_tensor("sqq", [128, SEQ], BF16))
            ps_n = e6(nc.psum_tensor("ps_n", [2, 512], F32))
            nsb = e6(nc.sbuf_tensor("nsb", [2, 512], F32))
            rmx = e6(nc.sbuf_tensor("rmx", [2, 8], F32))
            rq = e6(nc.sbuf_tensor("rq", [2, 1], F32))
            rk = e6(nc.sbuf_tensor("rk", [2, 1], F32))
            ps_d1 = e6(nc.psum_tensor("sig_d1", [1, 1], F32))
            ps_bc = e6(nc.psum_tensor("ps_bc", [128, 1], F32))
            sgn = e6(nc.sbuf_tensor("sgn", [1, 1], F32))
            with nc.Block() as b:
                @b.vector
                def _(v):
                    v.memset(sel[:, 0:2], 0.0)
                    v.memset(sel[0:64, 0:1], 1.0)
                    P.inc(v.memset(sel[64:128, 1:2], 1.0), "v")
            for hl in range(2):
                for bb_ in range(2):
                    for (srcbuf, rdst) in ((qc(hl), rq), (kc(hl), rk)):
                        with nc.Block() as b:
                            @b.scalar
                            def _(s, srcbuf=srcbuf, bb_=bb_):
                                P.inc(s.activation(sqq[:], srcbuf[:, bb_ * 4096:(bb_ + 1) * 4096],
                                                   AF.Square), "s")
                                P.inc(s.activation(C.fscr[0:1, 2:3], C.fscr[0:1, 2:3], AF.Square), "s")
                        for ch in range(8):
                            with nc.Block() as b:
                                @b.tensor
                                def _(tn, ch=ch):
                                    P.wait(tn, "v")
                                    P.wait(tn, "s")
                                    P.inc(tn.matmul(ps_n[:], sel[:], sqq[:, ch * 512:(ch + 1) * 512],
                                                    start=True, stop=True), "t")
                                    drain(C, tn, "t")
                            with nc.Block() as b:
                                @b.vector
                                def _(v, ch=ch):
                                    P.wait(v, "t")
                                    vsync(P, v, v.tensor_copy(nsb[:], ps_n[:]))
                                    v.reduce_max(rmx[:, ch:ch + 1], nsb[:], axis=AX)
                        with nc.Block() as b:
                            @b.vector
                            def _(v, rdst=rdst):
                                P.inc(v.reduce_max(rdst[:], rmx[:], axis=AX), "v")
                        with nc.Block() as b:
                            @b.scalar
                            def _(s, rdst=rdst):
                                P.wait(s, "v")
                                P.inc(s.activation(rdst[:], rdst[:], AF.Sqrt), "s")
                    with nc.Block() as b:
                        @b.tensor
                        def _(tn):
                            P.wait(tn, "s")
                            P.inc(tn.matmul(ps_d1[:], rq[:], rk[:], start=True, stop=True), "t")
                            drain(C, tn, "t")
                    with nc.Block() as b:
                        @b.vector
                        def _(v):
                            P.wait(v, "t")
                            vsync(P, v, v.tensor_copy(sgn[:], ps_d1[:]))
                            P.inc(v.tensor_scalar(sgn[:], sgn[:], -1.0, None, ALU.mult), "v")
                    with nc.Block() as b:
                        @b.tensor
                        def _(tn):
                            P.wait(tn, "v")
                            P.inc(tn.matmul(ps_bc[:], C.ones1_f32[:], sgn[:], start=True, stop=True),
                                  "t")
                            drain(C, tn, "t")
                    with nc.Block() as b:
                        @b.vector
                        def _(v, hl=hl, bb_=bb_):
                            P.wait(v, "t")
                            P.inc(v.tensor_copy(sig_sb[:, bb_ * 2 + hl:bb_ * 2 + hl + 1], ps_bc[:]),
                                  "v")
            fence(C, sems=())

        with ExitStack() as s7:
            e7 = s7.enter_context
            ps_sc = [e7(nc.psum_tensor(f"ps_sc{i}", [128, 512], F32)) for i in range(3)]
            ps_av = e7(nc.psum_tensor("ps_av", [128, 512], F32))
            ps_sum = e7(nc.psum_tensor("ps_sum", [1, 512], F32))
            ps_rb = e7(nc.psum_tensor("ps_rb", [128, 512], F32))
            p_t = e7(nc.sbuf_tensor("p_t", [128, 3 * 512], BF16))
            att_t = e7(nc.sbuf_tensor("att_t", [128, 512], BF16))
            sumf = e7(nc.sbuf_tensor("sumf", [1, 512], F32))
            sumr = e7(nc.sbuf_tensor("sumr", [1, 512], F32))
            rbf = e7(nc.sbuf_tensor("rbf", [128, 512], F32))
            for bb_ in range(2):
                for hl in range(2):
                    for sm in range(8):
                        nj = 4 * sm + 4
                        gs = bb_ * 8 + sm
                        qspan = slice(bb_ * 4096 + sm * 512, bb_ * 4096 + (sm + 1) * 512)
                        B = P.base("tsc", "tav", "tsm", "exp", "msk", "vfi", "trb", "g")

                        def kcol(j):
                            return bb_ * 4096 + j * 128

                        def vcol(j):
                            return (32 * bb_ + j) * 256 + hl * 128

                        def pts(j):
                            return p_t[:, (j % 3) * 512:(j % 3 + 1) * 512]

                        with nc.Block() as blk:
                            @blk.tensor
                            def _(tn, hl=hl, bb_=bb_, nj=nj, qspan=qspan, B=B,
                                  kcol=kcol, vcol=vcol, pts=pts):
                                P.wait(tn, "rop")
                                P.wait(tn, "vev")
                                for j in range(nj):
                                    if j >= 3:
                                        tn.wait_ge(P.sem("exp"), B["exp"] + j - 2)
                                    P.inc(tn.matmul(ps_sc[j % 3][:],
                                                    kc(hl)[:, kcol(j):kcol(j) + 128],
                                                    qc(hl)[:, qspan],
                                                    start=True, stop=True), "tsc")
                                    if j == nj - 1:
                                        drain(C, tn, "tsc")
                                    if j >= 2:
                                        jj = j - 2
                                        tn.wait_ge(P.sem("msk"), B["msk"] + jj + 1)
                                        P.inc(tn.matmul(ps_av[:], vtm[:, vcol(jj):vcol(jj) + 128],
                                                        pts(jj), start=(jj == 0), stop=False), "tav")
                                        P.inc(tn.matmul(ps_sum[:], C.ones128_bf[:],
                                                        pts(jj), start=(jj == 0), stop=False), "tsm")
                                for jj in (nj - 2, nj - 1):
                                    tn.wait_ge(P.sem("msk"), B["msk"] + jj + 1)
                                    P.inc(tn.matmul(ps_av[:], vtm[:, vcol(jj):vcol(jj) + 128],
                                                    pts(jj), start=(jj == 0), stop=(jj == nj - 1)),
                                          "tav")
                                    P.inc(tn.matmul(ps_sum[:], C.ones128_bf[:],
                                                    pts(jj), start=(jj == 0), stop=(jj == nj - 1)),
                                          "tsm")
                                drain(C, tn, "tsm")
                                tn.wait_ge(P.sem("vfi"), B["vfi"] + 1)
                                P.inc(tn.matmul(ps_rb[:], C.ones1_f32[:], sumr[:],
                                                start=True, stop=True), "trb")
                                drain(C, tn, "trb")

                            @blk.scalar
                            def _(s, hl=hl, bb_=bb_, nj=nj, B=B, pts=pts):
                                P.wait(s, "v")
                                for j in range(nj):
                                    s.wait_ge(P.sem("tsc"), B["tsc"] + min(j + 2, nj + 1))
                                    if j >= 3:
                                        s.wait_ge(P.sem("tav"), B["tav"] + j - 2)
                                    P.inc(s.activation(pts(j), ps_sc[j % 3][:], AF.Exp,
                                                       bias=sig_sb[:, bb_ * 2 + hl:bb_ * 2 + hl + 1]),
                                          "exp")
                                P.inc(s.activation(C.fscr[0:1, 0:4], C.fscr[0:1, 0:4], AF.Square),
                                      "exp")

                            @blk.vector
                            def _(v, nj=nj, B=B, pts=pts):
                                for j in range(nj):
                                    d_ = j - (nj - 4) if j >= nj - 4 else 4
                                    v.wait_ge(P.sem("exp"), B["exp"] + min(j + 2, nj + 1))
                                    if j >= 3:
                                        v.wait_ge(P.sem("tav"), B["tav"] + j - 2)
                                    P.inc(v.tensor_tensor(pts(j), pts(j),
                                                          msk_sb[:, d_ * 512:(d_ + 1) * 512],
                                                          ALU.mult), "msk")
                                v.wait_ge(P.sem("tsm"), B["tsm"] + nj + 1)
                                vsync(P, v, v.tensor_copy(sumf[:], ps_sum[:]))
                                P.inc(v.reciprocal(sumr[:], sumf[:]), "vfi")
                                v.wait_ge(P.sem("trb"), B["trb"] + 2)
                                v.tensor_copy(rbf[:], ps_rb[:])
                                P.wait(v, "g")
                                P.inc(v.tensor_tensor(att_t[:], ps_av[:], rbf[:], ALU.mult), "vfi")

                            @blk.gpsimd
                            def _(g, hl=hl, gs=gs, B=B):
                                g.wait_ge(P.sem("vfi"), B["vfi"] + 2)
                                dd, half = gs // 2, gs % 2
                                P.inc(g.dma_start(
                                    out=a2a_in[dd * 256 + hl * 128:dd * 256 + (hl + 1) * 128,
                                               half * 512:(half + 1) * 512],
                                    in_=att_t[:]), "g", 16)
            fence(C)

        with nc.Block() as b:
            @b.gpsimd
            def _(g):
                with allow_ici_all_to_all():
                    P.inc(g.collective_compute("AllToAll", ALU.bypass, replica_groups=RG,
                                               ins=[a2a_in[:]], outs=[attg[:]]), "cc")
        fence(C)

        with ExitStack() as s9:
            e9 = s9.enter_context
            att_sb = e9(nc.sbuf_tensor("att_sb", [128, MF * TPC], BF16))
            wsl = e9(nc.sbuf_tensor("wsl9", [128, 2 * D], F32))
            wbf = e9(nc.sbuf_tensor("wbf9", [128, 2 * D], BF16))
            fo = e9(nc.sbuf_tensor("fo9", [128, 2 * TPC], F32))
            ps = [e9(nc.psum_tensor(f"wo9_ps{i}", [128, TPC], F32)) for i in range(2)]
            with nc.Block() as b:
                @b.sync
                def _(sy):
                    for mf in range(MF):
                        P.inc(sy.dma_start(out=att_sb[:, mf * TPC:(mf + 1) * TPC],
                                           in_=attg[mf * 128:(mf + 1) * 128, :]), "ld", 16)
            fence(C, sems=("ld",))
            for oc in range(MF):
                with nc.Block() as b:
                    @b.sync
                    def _(sy, oc=oc):
                        P.inc(sy.dma_start(out=slot2(wsl, oc, D), in_=io["wo_t"][oc]), "ld", 16)
                fence(C, sems=("ld",))
                with nc.Block() as b:
                    @b.vector
                    def _(v, oc=oc):
                        P.inc(v.tensor_copy(slot2(wbf, oc, D), slot2(wsl, oc, D)), "cst")
                with nc.Block() as b:
                    @b.tensor
                    def _(tn, oc=oc):
                        P.wait(tn, "cst")
                        P.wait(tn, "v")
                        for hv in range(MF):
                            for (h0, h1) in HV:
                                last = tn.matmul(ps[oc % 2][:, h0:h1],
                                                 slot2(wbf, oc, D)[:, hv * 128:(hv + 1) * 128],
                                                 att_sb[:, hv * TPC + h0:hv * TPC + h1],
                                                 start=(hv == 0), stop=(hv == MF - 1))
                        P.inc(last, "mm")
                with nc.Block() as b:
                    @b.scalar
                    def _(s, oc=oc):
                        P.wait(s, "mm")
                        P.inc(s.activation(slot2(fo, oc, TPC), ps[oc % 2][:], AF.Identity), "evs")
                with nc.Block() as b:
                    @b.gpsimd
                    def _(g, oc=oc):
                        P.wait(g, "evs")
                        P.inc(g.dma_start(out=f_dram[oc * 128:(oc + 1) * 128, :],
                                          in_=slot2(fo, oc, TPC)), "g", 16)
                fence(C, sems=("g",))
            fence(C)

        af = e(nc.sbuf_tensor("af_l1a", [128, TPC], F32))
        bfc = e(nc.sbuf_tensor("bf_l1a", [128, TPC], F32))
        emit_stats(C, lambda mf: f_dram[mf * 128:(mf + 1) * 128, :], af, bfc)
        fence(C)
        emit_apply_post(C, lambda mf: f_dram[mf * 128:(mf + 1) * 128, :], xsrc, x_out, af, bfc)
        fence(C)


def ph_l1f(C, io):
    nc, P = C.nc, C.P
    x_in, x_out, f_dram = io["x_in"], io["x_out"], io["f_dram"]

    def xsrc(mf):
        return x_in[mf * 128:(mf + 1) * 128, :]

    with ExitStack() as sc:
        e = sc.enter_context
        h_sb = e(nc.sbuf_tensor("h1_sb", [128, IC1 * TPC], BF16))
        thp = e(nc.sbuf_tensor("thp", [128, 3], F32))
        thm = e(nc.sbuf_tensor("thm", [128, 3], F32))
        s23 = e(nc.sbuf_tensor("s23", [128, 1], F32))

        def htile(ic):
            return h_sb[:, ic * TPC:(ic + 1) * TPC]

        with ExitStack() as s0:
            e0 = s0.enter_context
            wst = e0(nc.sbuf_tensor("q_wst", [128, 2 * 5504], F32))
            acc = e0(nc.sbuf_tensor("q_acc", [128, 48], F32))
            accb = e0(nc.sbuf_tensor("q_accb", [128, 48], BF16))
            accb2 = e0(nc.sbuf_tensor("q_accb2", [128, 48], BF16))
            abs_ps = e0(nc.psum_tensor("abs_ps", [1, 48], F32))
            part = e0(nc.sbuf_tensor("q_part", [1, 4], F32))
            ssb = e0(nc.sbuf_tensor("q_ssb", [1, 4], F32))
            jobs = ([("w1a", dcl, HID1P) for dcl in range(2)]
                    + [("w3a", dcl, HID1P) for dcl in range(2)]
                    + [("w2a", icp, 256) for icp in range(IC1)])
            NJOB = len(jobs)
            B = P.base("g", "qab", "qmm")
            with nc.Block() as b:
                @b.gpsimd
                def _(g):
                    for i, (nm, idx, wid) in enumerate(jobs):
                        if i >= 2:
                            g.wait_ge(P.sem("qab"), B["qab"] + i - 1)
                        P.inc(g.dma_start(out=slot2(wst, i, 5504)[:, 0:wid],
                                          in_=io[nm][idx]), "g", 16)

                @b.vector
                def _(v):
                    for i, (nm, idx, wid) in enumerate(jobs):
                        v.wait_ge(P.sem("g"), B["g"] + 16 * (i + 1))
                        P.inc(v.reduce_sum(acc[:, i:i + 1], slot2(wst, i, 5504)[:, 0:wid],
                                           axis=AX, apply_absolute_value=True), "qab")
                    vsync(P, v, v.tensor_copy(accb[:], acc[:]))
                    vsync(P, v, v.tensor_tensor(acc[:], acc[:], accb[:], ALU.subtract))
                    P.inc(v.tensor_copy(accb2[:], acc[:]), "qab")
            with nc.Block() as b:
                @b.tensor
                def _(tn):
                    P.wait(tn, "qab")
                    tn.matmul(abs_ps[0:1, 0:NJOB], C.ones128_bf[:], accb[:, 0:NJOB],
                              start=True, stop=False)
                    P.inc(tn.matmul(abs_ps[0:1, 0:NJOB], C.ones128_bf[:], accb2[:, 0:NJOB],
                                    start=False, stop=True), "qmm")
                    drain(C, tn, "qmm")
            with nc.Block() as b:
                @b.vector
                def _(v):
                    P.wait(v, "qmm")
                    vsync(P, v, v.memset(part[:], 0.0))
                    v.reduce_sum(part[0:1, 0:1], abs_ps[0:1, 0:2], axis=AX)
                    v.reduce_sum(part[0:1, 1:2], abs_ps[0:1, 2:4], axis=AX)
                    P.inc(v.reduce_sum(part[0:1, 2:3], abs_ps[0:1, 4:4 + IC1], axis=AX), "v")
            fence(C, sems=())
            with nc.Block() as b:
                @b.gpsimd
                def _(g):
                    P.wait(g, "v")
                    P.inc(g.dma_start(out=io["s_in"][:], in_=part[:]), "g", 16)
                    P.wait(g, "g")
                    P.inc(g.collective_compute("AllReduce", ALU.add, replica_groups=RG,
                                               ins=[io["s_in"][:]], outs=[io["s_out"][:]]), "cc")
                    P.wait(g, "cc")
                    P.inc(g.dma_start(out=ssb[:], in_=io["s_out"][:]), "g", 16)
            fence(C)
            with nc.Block() as b:
                @b.vector
                def _(v):
                    vsync(P, v, v.tensor_scalar(ssb[:], ssb[:], 0.5 / float(HID1 * D), None, ALU.mult))
                    P.inc(v.tensor_tensor(ssb[0:1, 3:4], ssb[0:1, 1:2], ssb[0:1, 2:3], ALU.mult),
                          "v")
            with nc.Block() as b:
                @b.gpsimd
                def _(g):
                    P.wait(g, "v")
                    P.inc(g.dma_start(out=io["th_dram"][:], in_=ssb[:]), "g", 16)
                    if "thdbg" in io:
                        P.inc(g.dma_start(out=io["thdbg"][:], in_=ssb[:]), "g", 16)
            fence(C, sems=("g",))
            with nc.Block() as b:
                @b.sync
                def _(sy):
                    for w in range(3):
                        P.inc(sy.dma_start(out=thp[:, w:w + 1],
                                           in_=bass.AP(io["th_dram"], w, [[0, 128], [1, 1]])), "ld", 16)
                    P.inc(sy.dma_start(out=s23[:],
                                       in_=bass.AP(io["th_dram"], 3, [[0, 128], [1, 1]])), "ld", 16)
            fence(C, sems=("ld",))
            with nc.Block() as b:
                @b.vector
                def _(v):
                    P.inc(v.tensor_scalar(thm[:], thp[:], -1.0, None, ALU.mult), "v")
            fence(C, sems=())

        with ExitStack() as s12:
            ln3 = s12.enter_context(nc.sbuf_tensor("ln3", [128, MF * TPC], BF16))
            a3 = s12.enter_context(nc.sbuf_tensor("a3", [128, TPC], F32))
            b3 = s12.enter_context(nc.sbuf_tensor("b3", [128, TPC], F32))

            def ltile(mf):
                return ln3[:, mf * TPC:(mf + 1) * TPC]

            emit_stats(C, xsrc, a3, b3)
            fence(C)
            emit_apply_pre(C, xsrc, ln3, a3, b3)
            fence(C)

            with ExitStack() as s3:
                e3 = s3.enter_context
                w1s = e3(nc.sbuf_tensor("w1s_sl", [128, 2 * D], F32))
                w3s = e3(nc.sbuf_tensor("w3s_sl", [128, 2 * D], F32))
                t1b = e3(nc.sbuf_tensor("t1b", [128, 2 * D], BF16))
                t3b = e3(nc.sbuf_tensor("t3b", [128, 2 * D], BF16))
                sgA = e3(nc.sbuf_tensor("sgA", [128, D], BF16))
                sgB = e3(nc.sbuf_tensor("sgB", [128, D], BF16))
                sl = e3(nc.sbuf_tensor("sl_l1f", [128, 2 * TPC], BF16))
                ul = e3(nc.sbuf_tensor("ul_l1f", [128, 2 * TPC], BF16))
                ps_1 = e3(nc.psum_tensor("ps_1", [128, TPC], F32))
                ps_3 = e3(nc.psum_tensor("ps_3", [128, TPC], F32))
                scr_ps = e3(nc.psum_tensor("scr_ps_l1f3", [1, 128], F32))
                B = P.base("ld", "sg", "qtt", "mm", "mmd", "evs", "evv")
                with nc.Block() as b:
                    @b.sync
                    def _(sy):
                        for ic in range(IC1):
                            if ic >= 2:
                                sy.wait_ge(P.sem("sg"), B["sg"] + 2 * (ic - 2) + 2)
                            P.inc(sy.dma_start(out=slot2(w1s, ic, D), in_=io["w1f"][ic]), "ld", 16)
                            P.inc(sy.dma_start(out=slot2(w3s, ic, D), in_=io["w3f"][ic]), "ld", 16)

                    @b.scalar
                    def _(s):
                        P.wait(s, "v")
                        for ic in range(IC1):
                            s.wait_ge(P.sem("ld"), B["ld"] + 32 * (ic + 1))
                            if ic >= 1:
                                s.wait_ge(P.sem("qtt"), B["qtt"] + 2 * ic)
                            s.activation(sgA[:], slot2(w1s, ic, D), AF.Sign, bias=thm[:, 0:1])
                            P.inc(s.activation(sgB[:], slot2(w1s, ic, D), AF.Sign, bias=thp[:, 0:1]),
                                  "sg")
                            s.wait_ge(P.sem("qtt"), B["qtt"] + 2 * ic + 1)
                            s.activation(sgA[:], slot2(w3s, ic, D), AF.Sign, bias=thm[:, 1:2])
                            P.inc(s.activation(sgB[:], slot2(w3s, ic, D), AF.Sign, bias=thp[:, 1:2]),
                                  "sg")
                            if ic >= 1:
                                po = ic - 1
                                s.wait_ge(P.sem("mmd"), B["mmd"] + po + 1)
                                if po >= 2:
                                    s.wait_ge(P.sem("evv"), B["evv"] + po - 1)
                                P.inc(s.activation(slot2(sl, po, TPC), ps_1[:], AF.Silu,
                                                   scale=thp[:, 0:1]), "evs")
                        po = IC1 - 1
                        s.wait_ge(P.sem("mmd"), B["mmd"] + po + 1)
                        s.wait_ge(P.sem("evv"), B["evv"] + po - 1)
                        P.inc(s.activation(slot2(sl, po, TPC), ps_1[:], AF.Silu,
                                           scale=thp[:, 0:1]), "evs")

                    @b.vector
                    def _(v):
                        for ic in range(IC1):
                            v.wait_ge(P.sem("sg"), B["sg"] + 2 * ic + 1)
                            if ic >= 2:
                                v.wait_ge(P.sem("mm"), B["mm"] + ic - 1)
                            P.inc(v.tensor_tensor(slot2(t1b, ic, D), sgA[:], sgB[:], ALU.add), "qtt")
                            v.wait_ge(P.sem("sg"), B["sg"] + 2 * ic + 2)
                            P.inc(v.tensor_tensor(slot2(t3b, ic, D), sgA[:], sgB[:], ALU.add), "qtt")
                            if ic >= 1:
                                po = ic - 1
                                v.wait_ge(P.sem("mmd"), B["mmd"] + po + 1)
                                v.tensor_copy(slot2(ul, po, TPC), ps_3[:])
                                v.wait_ge(P.sem("evs"), B["evs"] + po + 1)
                                P.inc(v.tensor_tensor(htile(po), slot2(sl, po, TPC),
                                                      slot2(ul, po, TPC), ALU.mult), "evv")
                        po = IC1 - 1
                        v.wait_ge(P.sem("mmd"), B["mmd"] + po + 1)
                        v.tensor_copy(slot2(ul, po, TPC), ps_3[:])
                        v.wait_ge(P.sem("evs"), B["evs"] + po + 1)
                        P.inc(v.tensor_tensor(htile(po), slot2(sl, po, TPC), slot2(ul, po, TPC),
                                              ALU.mult), "evv")

                    @b.tensor
                    def _(tn):
                        for ic in range(IC1):
                            tn.wait_ge(P.sem("qtt"), B["qtt"] + 2 * ic + 2)
                            if ic >= 1:
                                tn.wait_ge(P.sem("evs"), B["evs"] + ic)
                                tn.wait_ge(P.sem("evv"), B["evv"] + ic)
                            for (h0, h1) in HV:
                                for dc in range(MF):
                                    tn.matmul(ps_1[:, h0:h1],
                                              slot2(t1b, ic, D)[:, dc * 128:(dc + 1) * 128],
                                              ltile(dc)[:, h0:h1], start=(dc == 0), stop=(dc == MF - 1))
                                for dc in range(MF):
                                    last = tn.matmul(ps_3[:, h0:h1],
                                                     slot2(t3b, ic, D)[:, dc * 128:(dc + 1) * 128],
                                                     ltile(dc)[:, h0:h1], start=(dc == 0),
                                                     stop=(dc == MF - 1))
                            P.inc(last, "mm")
                            P.inc(tn.matmul(scr_ps[:], C.ones128_bf[:],
                                            slot2(t1b, ic, D)[:, 0:128],
                                            start=True, stop=True), "mmd")
                fence(C, sems=())

        if "hdbg" in io:
            with nc.Block() as b:
                @b.gpsimd
                def _(g):
                    for ic in range(IC1):
                        P.inc(g.dma_start(out=io["hdbg"][ic * 128:(ic + 1) * 128, :],
                                          in_=htile(ic)), "g", 16)
            fence(C, sems=("g",))

        with ExitStack() as s4:
            e4 = s4.enter_context
            w2s = e4(nc.sbuf_tensor("w2s_sl", [128, 2 * HID1P], F32))
            t2b = e4(nc.sbuf_tensor("t2b", [128, 2 * HID1P], BF16))
            sgA2 = e4(nc.sbuf_tensor("sgA2", [128, HID1P], BF16))
            sgB2 = e4(nc.sbuf_tensor("sgB2", [128, HID1P], BF16))
            fo = e4(nc.sbuf_tensor("fo_l1f", [128, 2 * TPC], F32))
            ps_d = [e4(nc.psum_tensor(f"l1f_psd{i}", [128, TPC], F32)) for i in range(2)]
            for dc in range(MF):
                with nc.Block() as b:
                    @b.sync
                    def _(sy, dc=dc):
                        P.inc(sy.dma_start(out=slot2(w2s, dc, HID1P), in_=io["w2f"][dc]), "ld", 16)
                fence(C, sems=("ld",))
                with nc.Block() as b:
                    @b.scalar
                    def _(s, dc=dc):
                        s.activation(sgA2[:], slot2(w2s, dc, HID1P), AF.Sign, bias=thm[:, 2:3])
                        P.inc(s.activation(sgB2[:], slot2(w2s, dc, HID1P), AF.Sign, bias=thp[:, 2:3]),
                              "sg")
                with nc.Block() as b:
                    @b.vector
                    def _(v, dc=dc):
                        P.wait(v, "sg")
                        P.inc(v.tensor_tensor(slot2(t2b, dc, HID1P), sgA2[:], sgB2[:], ALU.add), "qtt")
                with nc.Block() as b:
                    @b.tensor
                    def _(tn, dc=dc):
                        P.wait(tn, "qtt")
                        for (h0, h1) in HV:
                            for ic in range(IC1):
                                last = tn.matmul(ps_d[dc % 2][:, h0:h1],
                                                 slot2(t2b, dc, HID1P)[:, ic * 128:(ic + 1) * 128],
                                                 h_sb[:, ic * TPC + h0:ic * TPC + h1],
                                                 start=(ic == 0), stop=(ic == IC1 - 1))
                        P.inc(last, "mm")
                with nc.Block() as b:
                    @b.scalar
                    def _(s, dc=dc):
                        P.wait(s, "mm")
                        P.inc(s.activation(slot2(fo, dc, TPC), ps_d[dc % 2][:], AF.Identity,
                                           scale=s23[:, 0:1]), "evs")
                with nc.Block() as b:
                    @b.gpsimd
                    def _(g, dc=dc):
                        P.wait(g, "evs")
                        P.inc(g.dma_start(out=f_dram[dc * 128:(dc + 1) * 128, :],
                                          in_=slot2(fo, dc, TPC)), "g", 16)
                fence(C, sems=("g",))
            fence(C)

        af = e(nc.sbuf_tensor("af_l1f", [128, TPC], F32))
        bfc = e(nc.sbuf_tensor("bf_l1f", [128, TPC], F32))
        emit_stats(C, lambda mf: f_dram[mf * 128:(mf + 1) * 128, :], af, bfc)
        fence(C)
        emit_apply_post(C, lambda mf: f_dram[mf * 128:(mf + 1) * 128, :], xsrc, x_out, af, bfc)
        fence(C)



def _mk_io(nc, specs):
    io = {}
    for nm, shp, dt, kind in specs:
        io[nm] = nc.dram_tensor(nm, list(shp), dt, kind=kind)
    return io


DBG_F = False
DBG_LIGHT = False

_COMMON = [("x_in", [D, TPC], F32, "ExternalInput"),
           ("x_out", [D, TPC], F32, "ExternalOutput")]

_SPECS = {
    "l0a": _COMMON + [
        ("w_in_t", [NOC_IN, 128, D], F32, "ExternalInput"),
        ("b_in_t", [128, NOC_IN], F32, "ExternalInput"),
        ("w_out_t", [MF, 128, D], F32, "ExternalInput"),
        ("b_out_t", [128, MF], F32, "ExternalInput"),
        ("gamma_t", [128, MF], F32, "ExternalInput"),
        ("beta_t", [128, MF], F32, "ExternalInput"),
        ("rot_cos", [512, TPC], F32, "ExternalInput"),
        ("rot_sin", [512, TPC], F32, "ExternalInput"),
        ("offs", [1, 4], I32, "ExternalInput"),
        ("ar_in", [2, 2048], F32, "Internal"),
    ],
    "l0f": _COMMON + [
        ("gw_t", [IC0, 128, D], U8, "ExternalInput"),
        ("uw_t", [IC0, 128, D], U8, "ExternalInput"),
        ("dw_t", [MF, 128, HID0], U8, "ExternalInput"),
        ("gsc_t", [128, IC0], F32, "ExternalInput"),
        ("usc_t", [128, IC0], F32, "ExternalInput"),
        ("dsc_t", [128, MF], F32, "ExternalInput"),
    ],
    "l1a": _COMMON + [
        ("wq_t", [MF, 128, 256], F32, "ExternalInput"),
        ("wk_t", [MF, 128, 256], F32, "ExternalInput"),
        ("wv_t", [MF, 128, 256], F32, "ExternalInput"),
        ("gl", [2, 1], F32, "ExternalInput"),
        ("wo_t", [MF, 128, D], F32, "ExternalInput"),
        ("rope1", [128, SEQ], F32, "ExternalInput"),
        ("masks", [5, 128, 512], BF16, "ExternalInput"),
        ("ag_in", [D, TPC], BF16, "Internal"),
        ("lng", [NCORE * D, TPC], BF16, "Internal"),
        ("a2a_in", [D, TPC], BF16, "Internal"),
        ("attg", [D, TPC], BF16, "Internal"),
        ("gtmp", [2, 2], F32, "Internal"),
    ],
    "l1f": _COMMON + [
        ("w1f", [IC1, 128, D], F32, "ExternalInput"),
        ("w3f", [IC1, 128, D], F32, "ExternalInput"),
        ("w2f", [MF, 128, HID1P], F32, "ExternalInput"),
        ("w1a", [2, 128, HID1P], F32, "ExternalInput"),
        ("w3a", [2, 128, HID1P], F32, "ExternalInput"),
        ("w2a", [IC1, 128, 256], F32, "ExternalInput"),
        ("s_in", [1, 4], F32, "Internal"),
        ("s_out", [1, 4], F32, "Internal"),
        ("th_dram", [1, 4], F32, "Internal"),
    ],
}

_PHASES = {"l0a": ph_l0a, "l0f": ph_l0f, "l1a": ph_l1a, "l1f": ph_l1f}

_IN_NAMES = {ph: [nm for nm, _, _, k in sp if k == "ExternalInput"]
             for ph, sp in _SPECS.items()}

_PERCORE = {"x_in", "rot_cos", "rot_sin", "offs", "wq_t", "wk_t", "wv_t", "gl", "w1a", "w3a", "w2a"}


def build_phase(ph):
    nc = bass.Bass("TRN2", target_bir_lowering=False)
    specs = list(_SPECS[ph]) + [("f_dram", [D, TPC], F32,
                                 "ExternalOutput" if (DBG_F or DBG_LIGHT) else "Internal")]
    if ph == "l0a" and DBG_LIGHT:
        specs += [("q3dbg", [D, TPC], BF16, "ExternalOutput")]
    if ph == "l1f" and DBG_F:
        specs += [("thdbg", [1, 4], F32, "ExternalOutput"),
                  ("hdbg", [HID1P, TPC], BF16, "ExternalOutput")]
    if ph == "l0a":
        specs += [("ar_out", [2, 2048], F32, "Internal")]
        if DBG_F:
            specs += [("adbg", [128, MF], F32, "ExternalOutput"),
                      ("kdbg", [D, TPC], BF16, "ExternalOutput"),
                      ("vdbg", [D, TPC], BF16, "ExternalOutput"),
                      ("qdbg", [D, TPC], BF16, "ExternalOutput"),
                      ("xndbg", [D, TPC], BF16, "ExternalOutput"),
                      ("abdbg", [3 * 128, TPC], F32, "ExternalOutput"),
                      ("qdbg2", [D, TPC], BF16, "ExternalOutput")]
    io = _mk_io(nc, specs)
    with ExitStack() as stack:
        C = Ctx(nc, stack)
        C.init_consts()
        _PHASES[ph](C, io)
    return nc


def run_phase(ph, H, xs):
    nc = build_phase(ph)
    maps = []
    for c in range(NCORE):
        m = {"x_in": xs[c]}
        for nm in _IN_NAMES[ph]:
            if nm == "x_in":
                continue
            v = H[nm]
            m[nm] = v[c] if nm in _PERCORE else v
        maps.append(m)
    res = run_bass_kernel_spmd(nc, maps, core_ids=list(range(NCORE)))
    if DBG_F or DBG_LIGHT:
        run_phase.f = [res.results[c]["f_dram"] for c in range(NCORE)]
        if ph == "l0a" and not DBG_F:
            run_phase.q3 = [res.results[c]["q3dbg"] for c in range(NCORE)]
    if DBG_F:
        if ph == "l1f":
            run_phase.th = [res.results[c]["thdbg"] for c in range(NCORE)]
            run_phase.h = [res.results[c]["hdbg"] for c in range(NCORE)]
        if ph == "l0a":
            run_phase.ar = [res.results[c]["adbg"] for c in range(NCORE)]
            run_phase.kv = [(res.results[c]["kdbg"], res.results[c]["vdbg"],
                             res.results[c]["qdbg"]) for c in range(NCORE)]
            run_phase.xn = [res.results[c]["xndbg"] for c in range(NCORE)]
            run_phase.ab = [res.results[c]["abdbg"] for c in range(NCORE)]
            run_phase.q2 = [res.results[c]["qdbg2"] for c in range(NCORE)]
    return [res.results[c]["x_out"] for c in range(NCORE)]


def kernel(**inputs):
    H = _prep(inputs)
    xs = H["x0"]
    for ph in ("l0a", "l0f", "l1a", "l1f"):
        xs = run_phase(ph, H, xs)
    out = np.concatenate([np.asarray(x, np.float32).T for x in xs], axis=0)
    return np.ascontiguousarray(out.reshape(2, SEQ, D))



FUSE = True

_ALL_IN = []
for _ph in ("l0a", "l0f", "l1a", "l1f"):
    for _nm in _IN_NAMES[_ph]:
        if _nm not in ("x_in",) and _nm not in _ALL_IN:
            _ALL_IN.append(_nm)


def build_fused():
    nc = bass.Bass("TRN2", target_bir_lowering=False)
    io = {}
    io["x_in"] = nc.dram_tensor("x_in", [D, TPC], F32, kind="ExternalInput")
    io["x_out"] = nc.dram_tensor("x_out", [D, TPC], F32, kind="ExternalOutput")
    seen = {"x_in", "x_out"}
    for ph in ("l0a", "l0f", "l1a", "l1f"):
        for nm, shp, dt, kind in _SPECS[ph]:
            if nm in seen or nm in ("x_in", "x_out", "f_dram"):
                continue
            seen.add(nm)
            io[nm] = nc.dram_tensor(nm, list(shp), dt, kind=kind)
    io["f_dram"] = nc.dram_tensor("f_dram", [D, TPC], F32)
    io["ar_out"] = nc.dram_tensor("ar_out", [2, 2048], F32)
    x1 = nc.dram_tensor("x1d", [D, TPC], F32)
    x2 = nc.dram_tensor("x2d", [D, TPC], F32)
    x3 = nc.dram_tensor("x3d", [D, TPC], F32)
    with ExitStack() as stack:
        C = Ctx(nc, stack)
        C.init_consts()
        ph_l0a(C, {**io, "x_in": io["x_in"], "x_out": x1})
        ph_l0f(C, {**io, "x_in": x1, "x_out": x2})
        ph_l1a(C, {**io, "x_in": x2, "x_out": x3})
        ph_l1f(C, {**io, "x_in": x3, "x_out": io["x_out"]})
    return nc


def kernel(**inputs):
    H = _prep(inputs)
    if FUSE:
        nc = build_fused()
        maps = []
        for c in range(NCORE):
            m = {"x_in": H["x0"][c]}
            for nm in _ALL_IN:
                v = H[nm]
                m[nm] = v[c] if nm in _PERCORE else v
            maps.append(m)
        res = run_bass_kernel_spmd(nc, maps, core_ids=list(range(NCORE)))
        xs = [res.results[c]["x_out"] for c in range(NCORE)]
    else:
        xs = H["x0"]
        for ph in ("l0a", "l0f", "l1a", "l1f"):
            xs = run_phase(ph, H, xs)
    out = np.concatenate([np.asarray(x, np.float32).T for x in xs], axis=0)
    return np.ascontiguousarray(out.reshape(2, SEQ, D))


# version 05
# speedup vs baseline: 1.0589x; 1.0589x over previous
import numpy as np
import ml_dtypes
from contextlib import ExitStack

import concourse.bass as bass
import concourse.mybir as mybir
from concourse.bass_utils import run_bass_kernel_spmd
from concourse.aot_env import allow_ici_all_to_all

F32 = mybir.dt.float32
BF16 = mybir.dt.bfloat16
U8 = mybir.dt.uint8
I32 = mybir.dt.int32
AF = mybir.ActivationFunctionType
ALU = mybir.AluOpType
AX = mybir.AxisListType.X
BF = ml_dtypes.bfloat16

NCORE = 8
TPC = 1024
D = 2048
MF = 16
SEQ = 4096
NOC_IN = 48
HID0 = 8192
IC0 = 64
HID1 = 5461
HID1P = 5504
IC1 = 43
EPS = 1e-5
RG = [list(range(NCORE))]
HV = [(0, 512), (512, 1024)]



def _slab3(w, pc=128, cc=128):
    K, N = w.shape
    kc, ncc = K // pc, N // cc
    return np.ascontiguousarray(
        w.reshape(kc, pc, ncc, cc).transpose(2, 1, 0, 3).reshape(ncc, pc, kc * cc))


def _prep(inputs):
    H = {}
    x = np.asarray(inputs["L0_attn_X"], np.float32).reshape(NCORE * TPC, D)
    H["x0"] = [np.ascontiguousarray(x[c * TPC:(c + 1) * TPC].T) for c in range(NCORE)]

    perm = np.concatenate([np.arange(0, 1024, 2), np.arange(1, 1024, 2), np.arange(1024, 2048)])
    W_in = np.asarray(inputs["L0_attn_W_in"], np.float32)
    b_in = np.asarray(inputs["L0_attn_b_in"], np.float32)
    W_in_p = np.concatenate([W_in[:, 0:D][:, perm], W_in[:, D:2 * D][:, perm],
                             W_in[:, 2 * D:3 * D][:, perm]], axis=1)
    b_in_p = np.concatenate([b_in[0:D][perm], b_in[D:2 * D][perm], b_in[2 * D:3 * D][perm]])
    W_out_p = np.asarray(inputs["L0_attn_W_out"], np.float32)[perm, :]
    H["w_in_t"] = _slab3(W_in_p)
    H["b_in_t"] = np.ascontiguousarray(b_in_p.reshape(NOC_IN, 128).T)
    H["w_out_t"] = _slab3(W_out_p)
    H["b_out_t"] = np.ascontiguousarray(np.asarray(inputs["L0_attn_b_out"], np.float32).reshape(MF, 128).T)
    H["gamma_t"] = np.ascontiguousarray(np.asarray(inputs["L0_attn_gamma"], np.float32).reshape(MF, 128).T)
    H["beta_t"] = np.ascontiguousarray(np.asarray(inputs["L0_attn_beta"], np.float32).reshape(MF, 128).T)
    invf = 10000.0 ** (-np.arange(0, 1024, 2, dtype=np.float64) / 1024.0)
    H["rot_cos"], H["rot_sin"] = [], []
    for c in range(NCORE):
        pos = np.arange(TPC, dtype=np.float64) + (c % 4) * TPC
        ang = invf[:, None] * pos[None, :]
        H["rot_cos"].append(np.ascontiguousarray(np.cos(ang).astype(np.float32)))
        H["rot_sin"].append(np.ascontiguousarray(np.sin(ang).astype(np.float32)))

    def u8(a):
        a = np.ascontiguousarray(np.asarray(a, np.int32))
        return np.ascontiguousarray(a.view(np.uint8).reshape(a.shape[0], a.shape[1], 4)[:, :, 0])
    H["gw_t"] = _slab3(u8(inputs["L0_ffn_gate_w"]).T)
    H["uw_t"] = _slab3(u8(inputs["L0_ffn_up_w"]).T)
    H["dw_t"] = _slab3(u8(inputs["L0_ffn_down_w"]).T)
    H["gsc_t"] = np.ascontiguousarray(np.asarray(inputs["L0_ffn_gate_scale"], np.float32).reshape(IC0, 128).T)
    H["usc_t"] = np.ascontiguousarray(np.asarray(inputs["L0_ffn_up_scale"], np.float32).reshape(IC0, 128).T)
    H["dsc_t"] = np.ascontiguousarray(np.asarray(inputs["L0_ffn_down_scale"], np.float32).reshape(MF, 128).T)

    wqs = np.asarray(inputs["L1_attn_wq_sem"], np.float32)
    wks = np.asarray(inputs["L1_attn_wk_sem"], np.float32)
    wqg = np.asarray(inputs["L1_attn_wq_geo"], np.float32)
    wkg = np.asarray(inputs["L1_attn_wk_geo"], np.float32)
    wv = np.asarray(inputs["L1_attn_wv"], np.float32)
    gl = np.asarray(inputs["L1_attn_gate_logit"], np.float32)
    H["wq_t"], H["wk_t"], H["wv_t"], H["gl"] = [], [], [], []
    for c in range(NCORE):
        hs = [2 * c, 2 * c + 1]
        q = np.concatenate([np.concatenate([wqg[:, h * 64:h * 64 + 64], wqs[:, h * 64:h * 64 + 64]], 1) for h in hs], 1)
        k = np.concatenate([np.concatenate([wkg[:, h * 64:h * 64 + 64], wks[:, h * 64:h * 64 + 64]], 1) for h in hs], 1)
        v = np.concatenate([wv[:, h * 128:h * 128 + 128] for h in hs], 1)
        H["wq_t"].append(np.ascontiguousarray(q.reshape(MF, 128, 256)))
        H["wk_t"].append(np.ascontiguousarray(k.reshape(MF, 128, 256)))
        H["wv_t"].append(np.ascontiguousarray(v.reshape(MF, 128, 256)))
        H["gl"].append(np.ascontiguousarray(gl[hs].reshape(2, 1)))
    H["wo_t"] = _slab3(np.asarray(inputs["L1_attn_wo"], np.float32))
    invg = 10000.0 ** (-np.arange(32, dtype=np.float64) / 32.0)
    ang = np.arange(SEQ, dtype=np.float64)[None, :] * invg[:, None]
    cg, sg = np.cos(ang), np.sin(ang)
    H["rope1"] = np.ascontiguousarray(np.concatenate([cg, cg, -sg, sg], 0).astype(np.float32))
    kk = np.arange(128)[:, None]
    qq = np.arange(512)[None, :]
    mk = [(kk <= qq - 128 * d_).astype(np.float32) for d_ in range(4)]
    mk.append(np.ones((128, 512), np.float32))
    H["masks"] = np.ascontiguousarray(np.stack(mk).astype(BF))

    w1T = np.zeros((D, HID1P), np.float32); w1T[:, :HID1] = np.asarray(inputs["L1_ffn_w1"], np.float32).T
    w3T = np.zeros((D, HID1P), np.float32); w3T[:, :HID1] = np.asarray(inputs["L1_ffn_w3"], np.float32).T
    w2T = np.zeros((HID1P, D), np.float32); w2T[:HID1, :] = np.asarray(inputs["L1_ffn_w2"], np.float32).T
    H["w1f"] = _slab3(w1T)
    H["w3f"] = _slab3(w3T)
    H["w2f"] = _slab3(w2T)

    H["w1a"] = [np.ascontiguousarray(w1T[256 * c:256 * c + 256].reshape(2, 128, HID1P))
                for c in range(NCORE)]
    H["w3a"] = [np.ascontiguousarray(w3T[256 * c:256 * c + 256].reshape(2, 128, HID1P))
                for c in range(NCORE)]
    H["w2a"] = [np.ascontiguousarray(w2T[:, 256 * c:256 * c + 256].reshape(IC1, 128, 256))
                for c in range(NCORE)]
    H["offs"] = [np.array([[(c // 4) * 2048, 256 * c, (1 - c // 4) * 2048, 2 * c]], np.int32)
                 for c in range(NCORE)]
    return H




_UN = [0]


def _u(nm):
    _UN[0] += 1
    return f"{nm}_{_UN[0]}"


class Prog:
    def __init__(self, nc, stack):
        self.nc, self.stack = nc, stack
        self.h, self.n = {}, {}

    def sem(self, name):
        if name not in self.h:
            self.h[name] = self.stack.enter_context(self.nc.semaphore(name))
            self.n[name] = 0
        return self.h[name]

    def inc(self, inst, name, amt=1):
        inst.then_inc(self.sem(name), amt)
        self.n[name] += amt
        return self.n[name]

    def wait(self, eng, name, val=None):
        eng.wait_ge(self.sem(name), self.n[name] if val is None else val)

    def base(self, *names):
        return {k: self.n.get(k, 0) for k in names}


class Ctx:
    def __init__(self, nc, stack):
        self.nc, self.stack = nc, stack
        self.P = Prog(nc, stack)
        e = stack.enter_context
        self.fscr = e(nc.sbuf_tensor("fscr", [1, 4], F32))
        self.dumr = e(nc.sbuf_tensor("dumr", [1, 256], BF16))
        self.scr_ps = e(nc.psum_tensor("scr_ps_g", [1, 256], F32))
        self.ones1_f32 = e(nc.sbuf_tensor("ones1_f32", [1, 128], F32))
        self.ones128_bf = e(nc.sbuf_tensor("ones128_bf", [128, 1], BF16))
        self.m128_bf = e(nc.sbuf_tensor("m128_bf", [1, 128], BF16))
        self.eps_t = e(nc.sbuf_tensor("eps_t", [1, 1], F32))

    def init_consts(self):
        nc = self.nc
        with nc.Block() as b:
            @b.vector
            def _(v):
                v.memset(self.ones1_f32[:], 1.0)
                v.memset(self.dumr[:], 0.0)
                v.memset(self.ones128_bf[:], 1.0)
                v.memset(self.m128_bf[:], -128.0)
                v.memset(self.eps_t[:], EPS)


def col(t2, i):
    return t2[i].rearrange("p -> p 1")


def slot2(buf, i, w):
    return buf[:, (i % 2) * w:(i % 2 + 1) * w]



def emit_stats(C, src_fn, a_dst, b_dst, mode="plain", apl=None):
    nc, P = C.nc, C.P
    with ExitStack() as sc:
        e = sc.enter_context
        sum_ps = e(nc.psum_tensor(_u("st_sum_ps"), [1, TPC], F32))
        sq_ps = e(nc.psum_tensor(_u("st_sq_ps"), [1, TPC], F32))
        xt = e(nc.sbuf_tensor(_u("st_xt"), [128, 2 * TPC], F32))
        xb = e(nc.sbuf_tensor(_u("st_xb"), [128, 2 * TPC], BF16))
        xq = e(nc.sbuf_tensor(_u("st_xq"), [128, 2 * TPC], BF16))
        mean = xt[0:1, 0:TPC]
        var = xt[0:1, TPC:2 * TPC]
        aa = e(nc.sbuf_tensor(_u("st_aa"), [1, TPC], F32))
        bb = e(nc.sbuf_tensor(_u("st_bb"), [1, TPC], F32))
        if mode == "l0inner":
            sA = e(nc.sbuf_tensor(_u("st_sA"), [1, TPC], F32))
            sB = e(nc.sbuf_tensor(_u("st_sB"), [1, TPC], F32))
        B = P.base("ld", "v", "s", "t")

        with nc.Block() as b:
            @b.sync
            def _(sy):
                for mf in range(MF):
                    if mf >= 2:
                        sy.wait_ge(P.sem("v"), B["v"] + (mf - 2) + 1)
                    P.inc(sy.dma_start(out=slot2(xt, mf, TPC), in_=src_fn(mf)), "ld", 16)

            @b.vector
            def _(v):
                for mf in range(MF):
                    v.wait_ge(P.sem("ld"), B["ld"] + 16 * (mf + 1))
                    if mf >= 2:
                        v.wait_ge(P.sem("s"), B["s"] + (mf - 2) + 1)
                        v.wait_ge(P.sem("t"), B["t"] + (mf - 2) + 1)
                    P.inc(v.tensor_copy(slot2(xb, mf, TPC), slot2(xt, mf, TPC)), "v")

            @b.scalar
            def _(s):
                for mf in range(MF):
                    s.wait_ge(P.sem("v"), B["v"] + mf + 1)
                    if mf >= 2:
                        s.wait_ge(P.sem("t"), B["t"] + (mf - 2) + 1)
                    P.inc(s.activation(slot2(xq, mf, TPC), slot2(xb, mf, TPC), AF.Square), "s")
                P.inc(s.activation(C.fscr[0:1, 2:3], C.fscr[0:1, 2:3], AF.Square), "s")

            @b.tensor
            def _(tn):
                for mf in range(MF):
                    tn.wait_ge(P.sem("s"), B["s"] + mf + 2)
                    st, sp = (mf == 0), (mf == MF - 1)
                    for (h0, h1) in HV:
                        tn.matmul(sum_ps[0:1, h0:h1], C.ones128_bf[:], slot2(xb, mf, TPC)[:, h0:h1],
                                  start=st, stop=sp)
                        last = tn.matmul(sq_ps[0:1, h0:h1], C.ones128_bf[:], slot2(xq, mf, TPC)[:, h0:h1],
                                         start=st, stop=sp)
                    P.inc(last, "t")
                drain(C, tn, "t")

        B = P.base("v", "s")
        if mode == "l0inner":
            with nc.Block() as b:
                @b.vector
                def _(v):
                    P.wait(v, "t")
                    vsync(P, v, v.tensor_copy(sA[:], sum_ps[:]))
                    vsync(P, v, v.tensor_copy(sB[:], sq_ps[:]))
                    vsync(P, v, v.tensor_scalar(mean[:], sA[:], 1.0 / D, None, ALU.mult))
                    vsync(P, v, v.tensor_scalar(sB[:], sB[:], 1.0 / D, None, ALU.mult))
                    vsync(P, v, v.tensor_tensor(var[:], mean[:], mean[:], ALU.mult))
                    P.inc(v.tensor_tensor(var[:], sB[:], var[:], ALU.subtract), "v")
                    v.wait_ge(P.sem("s"), B["s"] + 1)
                    vsync(P, v, v.reciprocal(aa[:], sA[:]))
                    vsync(P, v, v.tensor_copy(sA[:], aa[:]))
                    vsync(P, v, v.tensor_scalar(sB[:], var[:], EPS, None, ALU.add))
                    vsync(P, v, v.reciprocal(sB[:], sB[:]))
                    P.inc(v.tensor_tensor(var[:], var[:], sB[:], ALU.mult), "v")
                    v.wait_ge(P.sem("s"), B["s"] + 2)
                    vsync(P, v, v.reciprocal(var[:], var[:]))
                    vsync(P, v, v.tensor_tensor(aa[:], aa[:], var[:], ALU.mult))
                    P.inc(v.tensor_tensor(bb[:], mean[:], aa[:], ALU.mult), "v")

                @b.scalar
                def _(s):
                    s.wait_ge(P.sem("v"), B["v"] + 1)
                    P.inc(s.activation(sA[:], var[:], AF.Sqrt, bias=C.eps_t[:]), "s")
                    s.wait_ge(P.sem("v"), B["v"] + 2)
                    P.inc(s.activation(var[:], var[:], AF.Sqrt, bias=C.eps_t[:]), "s")
        else:
            with nc.Block() as b:
                @b.vector
                def _(v):
                    P.wait(v, "t")
                    vsync(P, v, v.tensor_copy(mean[:], sum_ps[:]))
                    vsync(P, v, v.tensor_scalar(mean[:], mean[:], 1.0 / D, None, ALU.mult))
                    vsync(P, v, v.tensor_copy(var[:], sq_ps[:]))
                    vsync(P, v, v.tensor_scalar(var[:], var[:], 1.0 / D, None, ALU.mult))
                    vsync(P, v, v.tensor_tensor(aa[:], mean[:], mean[:], ALU.mult))
                    P.inc(v.tensor_tensor(var[:], var[:], aa[:], ALU.subtract), "v")
                    v.wait_ge(P.sem("s"), B["s"] + 1)
                    vsync(P, v, v.reciprocal(var[:], aa[:]))
                    P.inc(v.tensor_tensor(bb[:], mean[:], var[:], ALU.mult), "v")

                @b.scalar
                def _(s):
                    s.wait_ge(P.sem("v"), B["v"] + 1)
                    P.inc(s.activation(aa[:], var[:], AF.Sqrt, bias=C.eps_t[:]), "s")

        if mode == "l0inner":
            if apl is not None:
                emit_bcast(C, [aa, bb, sA], [a_dst, b_dst, apl])
            else:
                emit_bcast(C, [aa, bb], [a_dst, b_dst])
        else:
            emit_bcast(C, [var, bb], [a_dst, b_dst])


def emit_bcast(C, rows, dsts):
    nc, P = C.nc, C.P
    with ExitStack() as sc:
        bc_ps = sc.enter_context(nc.psum_tensor(_u("bc_ps"), [128, TPC], F32))
        B = P.base("v", "t")
        with nc.Block() as b:
            @b.tensor
            def _(tn):
                for i, row in enumerate(rows):
                    if i >= 1:
                        tn.wait_ge(P.sem("v"), B["v"] + i)
                    for (h0, h1) in HV:
                        last = tn.matmul(bc_ps[:, h0:h1], C.ones1_f32[:], row[0:1, h0:h1],
                                         start=True, stop=True)
                    P.inc(last, "t")
                    drain(C, tn, "t")

            @b.vector
            def _(v):
                for i, dst in enumerate(dsts):
                    v.wait_ge(P.sem("t"), B["t"] + 2 * (i + 1))
                    P.inc(v.tensor_copy(dst[:], bc_ps[:]), "v")


def emit_bcast_col(C, scalar_dram_off, dsts_col):
    nc, P = C.nc, C.P
    with nc.Block() as b:
        @b.sync
        def _(sy):
            for (t, off), dst in zip(scalar_dram_off, dsts_col):
                P.inc(sy.dma_start(out=dst, in_=bass.AP(t, off, [[0, 128], [1, 1]])), "ld", 16)


def emit_apply_pre(C, x_fn, dst_big, a_bc, b_bc, gam=None, bet=None, extra_g=None):
    nc, P = C.nc, C.P
    with ExitStack() as sc:
        e = sc.enter_context
        xt = e(nc.sbuf_tensor(_u("ap_xt"), [128, 2 * TPC], F32))
        t1 = e(nc.sbuf_tensor(_u("ap_t1"), [128, 2 * TPC], F32))
        B = P.base("ld", "v", "s")
        VPM = 2

        with nc.Block() as b:
            @b.sync
            def _(sy):
                for mf in range(MF):
                    if mf >= 2:
                        sy.wait_ge(P.sem("v"), B["v"] + VPM * (mf - 2) + 1)
                    P.inc(sy.dma_start(out=slot2(xt, mf, TPC), in_=x_fn(mf)), "ld", 16)

            @b.vector
            def _(v):
                for mf in range(MF):
                    v.wait_ge(P.sem("ld"), B["ld"] + 16 * (mf + 1))
                    if gam is not None and mf >= 2:
                        v.wait_ge(P.sem("s"), B["s"] + (mf - 2) + 1)
                    P.inc(v.tensor_tensor(slot2(t1, mf, TPC), slot2(xt, mf, TPC), a_bc[:], ALU.mult), "v")
                    if gam is None:
                        dst = dst_big[:, mf * TPC:(mf + 1) * TPC]
                        P.inc(v.tensor_tensor(dst, slot2(t1, mf, TPC), b_bc[:], ALU.subtract), "v")
                    else:
                        P.inc(v.tensor_tensor(slot2(t1, mf, TPC), slot2(t1, mf, TPC), b_bc[:], ALU.subtract), "v")

            @b.scalar
            def _(s):
                if gam is None:
                    return
                for mf in range(MF):
                    s.wait_ge(P.sem("v"), B["v"] + VPM * (mf + 1))
                    dst = dst_big[:, mf * TPC:(mf + 1) * TPC]
                    P.inc(s.activation(dst, slot2(t1, mf, TPC), AF.Identity,
                                       scale=gam[:, mf:mf + 1], bias=bet[:, mf:mf + 1]), "s")
                P.inc(s.activation(C.fscr[0:1, 2:3], C.fscr[0:1, 2:3], AF.Square), "s")

            if extra_g is not None:
                @b.gpsimd
                def _(g):
                    for mf in range(MF):
                        if gam is None:
                            g.wait_ge(P.sem("v"), B["v"] + VPM * (mf + 1))
                        else:
                            g.wait_ge(P.sem("s"), B["s"] + mf + 1)
                        extra_g(g, mf)


def emit_apply_post(C, f_fn, x_fn, x_out, a_bc, b_bc):
    nc, P = C.nc, C.P
    with ExitStack() as sc:
        e = sc.enter_context
        ft = e(nc.sbuf_tensor(_u("po_ft"), [128, 2 * TPC], F32))
        xt = e(nc.sbuf_tensor(_u("po_xt"), [128, 2 * TPC], F32))
        xo = e(nc.sbuf_tensor(_u("po_xo"), [128, 2 * TPC], F32))
        B = P.base("ld", "v", "g")
        VPM = 3

        with nc.Block() as b:
            @b.sync
            def _(sy):
                for mf in range(MF):
                    if mf >= 2:
                        sy.wait_ge(P.sem("v"), B["v"] + VPM * (mf - 2) + 3)
                    P.inc(sy.dma_start(out=slot2(ft, mf, TPC), in_=f_fn(mf)), "ld", 16)
                    P.inc(sy.dma_start(out=slot2(xt, mf, TPC), in_=x_fn(mf)), "ld", 16)

            @b.vector
            def _(v):
                for mf in range(MF):
                    v.wait_ge(P.sem("ld"), B["ld"] + 32 * (mf + 1))
                    if mf >= 2:
                        v.wait_ge(P.sem("g"), B["g"] + 16 * (mf - 1))
                    v.tensor_tensor(slot2(ft, mf, TPC), slot2(ft, mf, TPC), a_bc[:], ALU.mult)
                    v.tensor_tensor(slot2(ft, mf, TPC), slot2(ft, mf, TPC), b_bc[:], ALU.subtract)
                    P.inc(v.tensor_tensor(slot2(xo, mf, TPC), slot2(xt, mf, TPC), slot2(ft, mf, TPC),
                                          ALU.add), "v", VPM)

            @b.gpsimd
            def _(g):
                for mf in range(MF):
                    g.wait_ge(P.sem("v"), B["v"] + VPM * (mf + 1))
                    P.inc(g.dma_start(out=x_out[mf * 128:(mf + 1) * 128, :],
                                      in_=slot2(xo, mf, TPC)), "g", 16)


def drain(C, tn, sem):
    return C.P.inc(tn.matmul(C.scr_ps[0:1, 0:256], C.dumr[0:1, 0:1], C.dumr[0:1, 0:256],
                             start=True, stop=True), sem)


def vsync(P, eng, inst):
    t = P.inc(inst, "vx")
    eng.wait_ge(P.sem("vx"), t)


def fence(C, sems=("ld", "g", "cc")):
    nc, P = C.nc, C.P
    with nc.Block() as b:
        @b.vector
        def _(v):
            P.inc(v.memset(C.fscr[0:1, 0:1], 0.0), "fzv")
            P.wait(v, "fzv")

        @b.scalar
        def _(s):
            for nm in sems:
                if nm in P.n and P.n[nm] > 0:
                    P.wait(s, nm)
            P.inc(s.activation(C.fscr[0:1, 1:2], C.fscr[0:1, 1:2], AF.Abs), "fzs")
            P.wait(s, "fzs")



def ph_l0a(C, io):
    nc, P = C.nc, C.P
    x_in, x_out, f_dram = io["x_in"], io["x_out"], io["f_dram"]

    def xsrc(mf):
        return x_in[mf * 128:(mf + 1) * 128, :]

    with ExitStack() as sc:
        e = sc.enter_context
        qb = e(nc.sbuf_tensor("qb", [128, MF * TPC], BF16))
        kb = e(nc.sbuf_tensor("kb", [128, MF * TPC], BF16))
        vb = e(nc.sbuf_tensor("vb", [128, MF * TPC], BF16))
        gam = e(nc.sbuf_tensor("gam", [128, MF], F32))
        bet = e(nc.sbuf_tensor("bet", [128, MF], F32))
        bo = e(nc.sbuf_tensor("bo", [128, MF], F32))
        bin_sb = e(nc.sbuf_tensor("bin_sb", [128, NOC_IN], F32))
        offs_sb = e(nc.sbuf_tensor("offs_sb", [1, 4], I32))
        a0 = e(nc.sbuf_tensor("a0", [128, TPC], F32))
        b0 = e(nc.sbuf_tensor("b0", [128, TPC], F32))
        apl = e(nc.sbuf_tensor("apl", [128, TPC], F32))
        af = e(nc.sbuf_tensor("af_l0a", [128, TPC], F32))
        bfc = e(nc.sbuf_tensor("bf_l0a", [128, TPC], F32))
        A_sb = e(nc.sbuf_tensor("A_sb", [128, MF], F32))
        z128 = e(nc.sbuf_tensor("z128", [128, MF], F32))
        rb = e(nc.gpsimd.register("rb"))
        rb2 = e(nc.gpsimd.register("rb2"))

        def tile(buf, mf):
            return buf[:, mf * TPC:(mf + 1) * TPC]

        with nc.Block() as b:
            @b.sync
            def _(sy):
                P.inc(sy.dma_start(out=gam[:], in_=io["gamma_t"][:]), "ld", 16)
                P.inc(sy.dma_start(out=bet[:], in_=io["beta_t"][:]), "ld", 16)
                P.inc(sy.dma_start(out=bo[:], in_=io["b_out_t"][:]), "ld", 16)
                P.inc(sy.dma_start(out=bin_sb[:], in_=io["b_in_t"][:]), "ld", 16)
                P.inc(sy.dma_start(out=offs_sb[:], in_=io["offs"][:]), "ld", 16)

            @b.vector
            def _(v):
                v.memset(z128[:], 0.0)
        fence(C)

        with ExitStack() as sxn:
            Xn = sxn.enter_context(nc.sbuf_tensor("Xn", [128, MF * TPC], BF16))
            emit_stats(C, xsrc, a0, b0, mode="l0inner", apl=apl)
            fence(C)
            if "abdbg" in io:
                with nc.Block() as b:
                    @b.gpsimd
                    def _(g):
                        P.inc(g.dma_start(out=io["abdbg"][0:128, :], in_=a0[:]), "g", 16)
                        P.inc(g.dma_start(out=io["abdbg"][128:256, :], in_=b0[:]), "g", 16)
                        P.inc(g.dma_start(out=io["abdbg"][256:384, :], in_=apl[:]), "g", 16)
                fence(C, sems=("g",))
            emit_apply_pre(C, xsrc, Xn, a0, b0, gam=gam, bet=bet)
            fence(C)
            if "xndbg" in io:
                with nc.Block() as b:
                    @b.gpsimd
                    def _(g):
                        for mf in range(MF):
                            P.inc(g.dma_start(out=io["xndbg"][mf * 128:(mf + 1) * 128, :],
                                              in_=Xn[:, mf * TPC:(mf + 1) * TPC]), "g", 16)
                fence(C, sems=("g",))

            with ExitStack() as s4:
                e4 = s4.enter_context
                wsl = e4(nc.sbuf_tensor("wsl", [128, 2 * D], F32))
                wbf = e4(nc.sbuf_tensor("wbf", [128, 2 * D], BF16))
                ps = [e4(nc.psum_tensor(f"qkv_ps{i}", [128, TPC], F32)) for i in range(2)]
                B = P.base("ld", "cst", "mm", "mmd", "ev")
                with nc.Block() as b:
                    @b.sync
                    def _(sy):
                        for oc in range(NOC_IN):
                            if oc >= 2:
                                sy.wait_ge(P.sem("cst"), B["cst"] + oc - 1)
                            P.inc(sy.dma_start(out=slot2(wsl, oc, D), in_=io["w_in_t"][oc]), "ld", 16)

                    @b.vector
                    def _(v):
                        for oc in range(NOC_IN):
                            v.wait_ge(P.sem("ld"), B["ld"] + 16 * (oc + 1))
                            if oc >= 2:
                                v.wait_ge(P.sem("mm"), B["mm"] + oc - 1)
                            P.inc(v.tensor_copy(slot2(wbf, oc, D), slot2(wsl, oc, D)), "cst")
                            if oc >= 1:
                                po = oc - 1
                                v.wait_ge(P.sem("mmd"), B["mmd"] + po + 1)
                                dst = (qb if po < 16 else (kb if po < 32 else vb))
                                P.inc(v.tensor_scalar(tile(dst, po % 16), ps[po % 2][:],
                                                      bin_sb[:, po:po + 1], None, ALU.add), "ev")
                        po = NOC_IN - 1
                        v.wait_ge(P.sem("mmd"), B["mmd"] + po + 1)
                        P.inc(v.tensor_scalar(tile(vb, po % 16), ps[po % 2][:],
                                              bin_sb[:, po:po + 1], None, ALU.add), "ev")

                    @b.tensor
                    def _(tn):
                        P.wait(tn, "s")
                        for oc in range(NOC_IN):
                            tn.wait_ge(P.sem("cst"), B["cst"] + oc + 1)
                            if oc >= 2:
                                tn.wait_ge(P.sem("ev"), B["ev"] + oc - 1)
                            for mf in range(MF):
                                for (h0, h1) in HV:
                                    last = tn.matmul(ps[oc % 2][:, h0:h1],
                                                     slot2(wbf, oc, D)[:, mf * 128:(mf + 1) * 128],
                                                     tile(Xn, mf)[:, h0:h1],
                                                     start=(mf == 0), stop=(mf == MF - 1))
                            P.inc(last, "mm")
                            drain(C, tn, "mmd")
                fence(C)

        with ExitStack() as s5:
            e5 = s5.enter_context
            rcs = e5(nc.sbuf_tensor("rcs", [128, 8 * TPC], F32))
            u1 = e5(nc.sbuf_tensor("u1", [128, TPC], F32))
            u2 = e5(nc.sbuf_tensor("u2", [128, TPC], F32))
            u3 = e5(nc.sbuf_tensor("u3", [128, TPC], F32))
            u4 = e5(nc.sbuf_tensor("u4", [128, TPC], F32))
            with nc.Block() as b:
                @b.sync
                def _(sy):
                    for t in range(4):
                        P.inc(sy.dma_start(out=rcs[:, t * TPC:(t + 1) * TPC],
                                           in_=io["rot_cos"][t * 128:(t + 1) * 128, :]), "ld", 16)
                        P.inc(sy.dma_start(out=rcs[:, (4 + t) * TPC:(5 + t) * TPC],
                                           in_=io["rot_sin"][t * 128:(t + 1) * 128, :]), "ld", 16)
            fence(C)
            with nc.Block() as b:
                @b.vector
                def _(v):
                    for buf in (kb, vb):
                        for t in range(4):
                            cosT = rcs[:, t * TPC:(t + 1) * TPC]
                            sinT = rcs[:, (4 + t) * TPC:(5 + t) * TPC]
                            v.tensor_tensor(u1[:], tile(buf, t), cosT, ALU.mult)
                            v.tensor_tensor(u2[:], tile(buf, t + 4), sinT, ALU.mult)
                            v.tensor_tensor(u3[:], tile(buf, t), sinT, ALU.mult)
                            v.tensor_tensor(u4[:], tile(buf, t + 4), cosT, ALU.mult)
                            v.tensor_tensor(tile(buf, t), u1[:], u2[:], ALU.subtract)
                            inst = v.tensor_tensor(tile(buf, t + 4), u3[:], u4[:], ALU.add)
                    P.inc(inst, "v")
            fence(C)

        if "kdbg" in io:
            with nc.Block() as b:
                @b.gpsimd
                def _(g):
                    for mf in range(MF):
                        P.inc(g.dma_start(out=io["kdbg"][mf * 128:(mf + 1) * 128, :],
                                          in_=tile(kb, mf)), "g", 16)
                        P.inc(g.dma_start(out=io["vdbg"][mf * 128:(mf + 1) * 128, :],
                                          in_=tile(vb, mf)), "g", 16)
                        P.inc(g.dma_start(out=io["qdbg"][mf * 128:(mf + 1) * 128, :],
                                          in_=tile(qb, mf)), "g", 16)
            fence(C, sems=("g",))

        with ExitStack() as s6:
            e6 = s6.enter_context
            sqk = e6(nc.sbuf_tensor("sqk", [128, 2 * TPC], BF16))
            sqv = e6(nc.sbuf_tensor("sqv", [128, 2 * TPC], BF16))
            nk_ps = e6(nc.psum_tensor("nk_ps", [1, TPC], F32))
            nv_ps = e6(nc.psum_tensor("nv_ps", [1, TPC], F32))
            nk_sb = e6(nc.sbuf_tensor("nk_sb", [1, TPC], F32))
            nv_sb = e6(nc.sbuf_tensor("nv_sb", [1, TPC], F32))
            W_bc = e6(nc.sbuf_tensor("W_bc", [128, TPC], F32))
            pk = e6(nc.sbuf_tensor("pk", [128, TPC], BF16))
            pw = e6(nc.sbuf_tensor("pw", [128, TPC], F32))
            apart = e6(nc.sbuf_tensor("apart", [128, MF], F32))
            B = P.base("sq", "nmm")
            with nc.Block() as b:
                @b.scalar
                def _(s):
                    P.wait(s, "v")
                    for mf in range(MF):
                        if mf >= 2:
                            s.wait_ge(P.sem("nmm"), B["nmm"] + mf - 1)
                        s.activation(slot2(sqk, mf, TPC), tile(kb, mf), AF.Square)
                        P.inc(s.activation(slot2(sqv, mf, TPC), tile(vb, mf), AF.Square), "sq")
                    P.inc(s.activation(C.fscr[0:1, 2:3], C.fscr[0:1, 2:3], AF.Square), "sq")

                @b.tensor
                def _(tn):
                    for mf in range(MF):
                        tn.wait_ge(P.sem("sq"), B["sq"] + mf + 2)
                        st, sp = (mf == 0), (mf == MF - 1)
                        for (h0, h1) in HV:
                            tn.matmul(nk_ps[0:1, h0:h1], C.ones128_bf[:],
                                      slot2(sqk, mf, TPC)[:, h0:h1], start=st, stop=sp)
                            last = tn.matmul(nv_ps[0:1, h0:h1], C.ones128_bf[:],
                                             slot2(sqv, mf, TPC)[:, h0:h1], start=st, stop=sp)
                        P.inc(last, "nmm")
                    drain(C, tn, "nmm")
            with nc.Block() as b:
                @b.vector
                def _(v):
                    P.wait(v, "nmm")
                    vsync(P, v, v.tensor_copy(nk_sb[:], nk_ps[:]))
                    vsync(P, v, v.tensor_copy(nv_sb[:], nv_ps[:]))
                    v.tensor_tensor(nk_sb[:], nk_sb[:], nv_sb[:], ALU.mult)
            with nc.Block() as b:
                @b.scalar
                def _(s):
                    s.activation(nv_sb[:], nk_sb[:], AF.Sqrt)
            with nc.Block() as b:
                @b.vector
                def _(v):
                    v.reciprocal(nk_sb[:], nv_sb[:])
            emit_bcast(C, [nk_sb], [W_bc])
            with nc.Block() as b:
                @b.vector
                def _(v):
                    for mf in range(MF):
                        v.tensor_tensor(pk[:], tile(kb, mf), tile(vb, mf), ALU.mult)
                        v.tensor_tensor(pw[:], pk[:], W_bc[:], ALU.mult)
                        inst = v.reduce_sum(apart[:, mf:mf + 1], pw[:], axis=AX)
                    P.inc(inst, "v")
            fence(C)
            with nc.Block() as b:
                @b.gpsimd
                def _(g):
                    P.wait(g, "v")
                    g.reg_load(rb, offs_sb[0:1, 0:1])
                    g.reg_load(rb2, offs_sb[0:1, 2:3])
                    P.inc(g.dma_start(out=bass.AP(io["ar_in"], rb, [[16, 128], [1, 16]]),
                                      in_=apart[:]), "g", 16)
                    P.inc(g.dma_start(out=bass.AP(io["ar_in"], rb2, [[16, 128], [1, 16]]),
                                      in_=z128[:]), "g", 16)
                    P.wait(g, "g")
                    P.inc(g.collective_compute(
                        "AllReduce", ALU.add, replica_groups=RG,
                        ins=[io["ar_in"][:]], outs=[io["ar_out"][:]]), "cc")
                    P.wait(g, "cc")
                    P.inc(g.dma_start(out=A_sb[:], in_=bass.AP(io["ar_out"], rb, [[16, 128], [1, 16]])),
                          "g", 16)
            fence(C)
            if "adbg" in io:
                with nc.Block() as b:
                    @b.gpsimd
                    def _(g):
                        P.inc(g.dma_start(out=io["adbg"][:], in_=A_sb[:]), "g", 16)
                fence(C, sems=("g",))
            with nc.Block() as b:
                @b.vector
                def _(v):
                    v.tensor_scalar(A_sb[:], A_sb[:], 0.5, 0.5, ALU.mult, ALU.add)
            with nc.Block() as b:
                @b.vector
                def _(v):
                    for mf in range(MF):
                        inst = v.tensor_scalar(tile(qb, mf), tile(qb, mf), A_sb[:, mf:mf + 1],
                                               None, ALU.mult)
                    P.inc(inst, "v")
            fence(C, sems=())
            if "qdbg2" in io:
                with nc.Block() as b:
                    @b.gpsimd
                    def _(g):
                        for mf in range(MF):
                            P.inc(g.dma_start(out=io["qdbg2"][mf * 128:(mf + 1) * 128, :],
                                              in_=tile(qb, mf)), "g", 16)
                fence(C, sems=("g",))

        with ExitStack() as s8:
            e8 = s8.enter_context
            wsl = e8(nc.sbuf_tensor("wsl8", [128, 2 * D], F32))
            wbf = e8(nc.sbuf_tensor("wbf8", [128, 2 * D], BF16))
            xt = e8(nc.sbuf_tensor("xt8", [128, 2 * TPC], F32))
            fo = e8(nc.sbuf_tensor("fo8", [128, 2 * TPC], F32))
            xr = e8(nc.sbuf_tensor("xr8", [128, TPC], F32))
            fs = e8(nc.sbuf_tensor("fs8", [128, 2 * TPC], F32))
            ghost = e8(nc.psum_tensor("ghost8", [128, TPC], F32))
            ps = [e8(nc.psum_tensor(f"wo_ps{i}", [128, TPC], F32)) for i in range(2)]
            B8l = P.base("ld")["ld"]
            B8g = P.base("g")["g"]
            with nc.Block() as b:
                @b.sync
                def _(sy):
                    P.inc(sy.dma_start(out=slot2(wsl, 0, D), in_=io["w_out_t"][0]), "ld", 16)
                    P.inc(sy.dma_start(out=slot2(xt, 0, TPC), in_=xsrc(0)), "ld", 16)
            for oc in range(MF):
                with nc.Block() as b:
                    @b.sync
                    def _(sy, oc=oc):
                        if oc + 1 < MF:
                            P.inc(sy.dma_start(out=slot2(wsl, oc + 1, D),
                                               in_=io["w_out_t"][oc + 1]), "ld", 16)
                            P.inc(sy.dma_start(out=slot2(xt, oc + 1, TPC),
                                               in_=xsrc(oc + 1)), "ld", 16)
                with nc.Block() as b:
                    @b.vector
                    def _(v, oc=oc):
                        v.wait_ge(P.sem("ld"), B8l + 32 * (oc + 1))
                        P.inc(v.tensor_copy(slot2(wbf, oc, D), slot2(wsl, oc, D)), "cst")
                with nc.Block() as b:
                    @b.tensor
                    def _(tn, oc=oc):
                        P.wait(tn, "cst")
                        P.wait(tn, "v")
                        for qf in range(MF):
                            for (h0, h1) in HV:
                                last = tn.matmul(ps[oc % 2][:, h0:h1],
                                                 slot2(wbf, oc, D)[:, qf * 128:(qf + 1) * 128],
                                                 tile(qb, qf)[:, h0:h1],
                                                 start=(qf == 0), stop=(qf == MF - 1))
                        P.inc(last, "mm")
                with nc.Block() as b:
                    @b.scalar
                    def _(s, oc=oc):
                        P.wait(s, "mm")
                        P.inc(s.activation(slot2(fo, oc, TPC), ps[oc % 2][:], AF.Identity,
                                           bias=bo[:, oc:oc + 1]), "evs")
                with nc.Block() as b:
                    @b.vector
                    def _(v, oc=oc):
                        P.wait(v, "evs")
                        if oc >= 2:
                            v.wait_ge(P.sem("g"), B8g + 16 * (oc - 1))
                        v.tensor_tensor(xr[:], slot2(xt, oc, TPC), apl[:], ALU.mult)
                        P.inc(v.tensor_tensor(slot2(fs, oc, TPC), slot2(fo, oc, TPC), xr[:],
                                              ALU.add), "evv")
                with nc.Block() as b:
                    @b.gpsimd
                    def _(g, oc=oc):
                        P.wait(g, "evv")
                        P.inc(g.dma_start(out=f_dram[oc * 128:(oc + 1) * 128, :],
                                          in_=slot2(fs, oc, TPC)), "g", 16)
            fence(C)

        emit_stats(C, lambda mf: f_dram[mf * 128:(mf + 1) * 128, :], af, bfc)
        fence(C)
        emit_apply_post(C, lambda mf: f_dram[mf * 128:(mf + 1) * 128, :], xsrc, x_out, af, bfc)
        fence(C)
        if "q3dbg" in io:
            with nc.Block() as b:
                @b.gpsimd
                def _(g):
                    for mf in range(MF):
                        P.inc(g.dma_start(out=io["q3dbg"][mf * 128:(mf + 1) * 128, :],
                                          in_=tile(qb, mf)), "g", 16)
                    P.wait(g, "g")



def ph_l0f(C, io):
    nc, P = C.nc, C.P
    x_in, x_out, f_dram = io["x_in"], io["x_out"], io["f_dram"]

    def xsrc(mf):
        return x_in[mf * 128:(mf + 1) * 128, :]

    with ExitStack() as sc:
        e = sc.enter_context
        gsc = e(nc.sbuf_tensor("gsc", [128, IC0], F32))
        usc = e(nc.sbuf_tensor("usc", [128, IC0], F32))
        dsc = e(nc.sbuf_tensor("dsc", [128, MF], F32))
        sch = sc.enter_context(ExitStack())
        h_sb = sch.enter_context(nc.sbuf_tensor("h_sb", [128, IC0 * TPC], BF16))
        hi_bf = sch.enter_context(nc.sbuf_tensor("hi_bf", [1, TPC], BF16))
        lo_bf = sch.enter_context(nc.sbuf_tensor("lo_bf", [1, TPC], BF16))

        def htile(ic):
            return h_sb[:, ic * TPC:(ic + 1) * TPC]

        with nc.Block() as b:
            @b.sync
            def _(sy):
                P.inc(sy.dma_start(out=gsc[:], in_=io["gsc_t"][:]), "ld", 16)
                P.inc(sy.dma_start(out=usc[:], in_=io["usc_t"][:]), "ld", 16)
                P.inc(sy.dma_start(out=dsc[:], in_=io["dsc_t"][:]), "ld", 16)

        with ExitStack() as s12:
            ln1 = s12.enter_context(nc.sbuf_tensor("ln1", [128, MF * TPC], BF16))
            tsum_bf = s12.enter_context(nc.sbuf_tensor("tsum_bf", [1, TPC], BF16))

            def ltile(mf):
                return ln1[:, mf * TPC:(mf + 1) * TPC]

            with ExitStack() as sab:
                a1 = sab.enter_context(nc.sbuf_tensor("a1", [128, TPC], F32))
                b1 = sab.enter_context(nc.sbuf_tensor("b1", [128, TPC], F32))
                emit_stats(C, xsrc, a1, b1)
                fence(C)
                emit_apply_pre(C, xsrc, ln1, a1, b1)
                fence(C)

            with ExitStack() as s3:
                ts_ps = s3.enter_context(nc.psum_tensor("ts_ps", [1, TPC], F32))
                tsum_f = s3.enter_context(nc.sbuf_tensor("tsum_f", [1, TPC], F32))
                with nc.Block() as b:
                    @b.tensor
                    def _(tn):
                        for mf in range(MF):
                            for (h0, h1) in HV:
                                last = tn.matmul(ts_ps[0:1, h0:h1], C.ones128_bf[:],
                                                 ltile(mf)[:, h0:h1],
                                                 start=(mf == 0), stop=(mf == MF - 1))
                        P.inc(last, "t")
                        drain(C, tn, "t")
                with nc.Block() as b:
                    @b.vector
                    def _(v):
                        P.wait(v, "t")
                        vsync(P, v, v.tensor_copy(tsum_f[:], ts_ps[:]))
                        v.tensor_copy(tsum_bf[:], tsum_f[:])
                fence(C, sems=())

            with ExitStack() as s4:
                e4 = s4.enter_context
                g8 = e4(nc.sbuf_tensor("g8", [128, 2 * D], U8))
                u8b = e4(nc.sbuf_tensor("u8b", [128, 2 * D], U8))
                gbf = e4(nc.sbuf_tensor("gbf", [128, 2 * D], BF16))
                ubf = e4(nc.sbuf_tensor("ubf", [128, 2 * D], BF16))
                sl = e4(nc.sbuf_tensor("sl", [128, 2 * TPC], BF16))
                ul = e4(nc.sbuf_tensor("ul", [128, 2 * TPC], BF16))
                ps_g = e4(nc.psum_tensor("ps_g", [128, TPC], F32))
                ps_u = e4(nc.psum_tensor("ps_u", [128, TPC], F32))
                B = P.base("ld", "cst", "mm", "mmd", "evs", "evv")
                with nc.Block() as b:
                    @b.sync
                    def _(sy):
                        for ic in range(IC0):
                            if ic >= 2:
                                sy.wait_ge(P.sem("cst"), B["cst"] + 2 * (ic - 2) + 2)
                            P.inc(sy.dma_start(out=slot2(g8, ic, D), in_=io["gw_t"][ic]), "ld", 16)
                            P.inc(sy.dma_start(out=slot2(u8b, ic, D), in_=io["uw_t"][ic]), "ld", 16)

                    @b.vector
                    def _(v):
                        for ic in range(IC0):
                            v.wait_ge(P.sem("ld"), B["ld"] + 32 * (ic + 1))
                            if ic >= 2:
                                v.wait_ge(P.sem("mm"), B["mm"] + ic - 1)
                            v.tensor_copy(slot2(gbf, ic, D), slot2(g8, ic, D))
                            P.inc(v.tensor_copy(slot2(ubf, ic, D), slot2(u8b, ic, D)), "cst", 2)
                            if ic >= 1:
                                po = ic - 1
                                v.wait_ge(P.sem("mmd"), B["mmd"] + po + 1)
                                v.tensor_scalar(slot2(ul, po, TPC), ps_u[:], usc[:, po:po + 1],
                                                None, ALU.mult)
                                v.wait_ge(P.sem("evs"), B["evs"] + po + 1)
                                P.inc(v.tensor_tensor(htile(po), slot2(sl, po, TPC),
                                                      slot2(ul, po, TPC), ALU.mult), "evv")
                        po = IC0 - 1
                        v.wait_ge(P.sem("mmd"), B["mmd"] + po + 1)
                        v.tensor_scalar(slot2(ul, po, TPC), ps_u[:], usc[:, po:po + 1], None, ALU.mult)
                        v.wait_ge(P.sem("evs"), B["evs"] + po + 1)
                        P.inc(v.tensor_tensor(htile(po), slot2(sl, po, TPC), slot2(ul, po, TPC),
                                              ALU.mult), "evv")

                    @b.scalar
                    def _(s):
                        for ic in range(IC0):
                            s.wait_ge(P.sem("mmd"), B["mmd"] + ic + 1)
                            if ic >= 2:
                                s.wait_ge(P.sem("evv"), B["evv"] + ic - 1)
                            P.inc(s.activation(slot2(sl, ic, TPC), ps_g[:], AF.Silu,
                                               scale=gsc[:, ic:ic + 1]), "evs")

                    @b.tensor
                    def _(tn):
                        for ic in range(IC0):
                            tn.wait_ge(P.sem("cst"), B["cst"] + 2 * (ic + 1))
                            if ic >= 1:
                                tn.wait_ge(P.sem("evs"), B["evs"] + ic)
                                tn.wait_ge(P.sem("evv"), B["evv"] + ic)
                            for (h0, h1) in HV:
                                for hc in range(MF):
                                    tn.matmul(ps_g[:, h0:h1],
                                              slot2(gbf, ic, D)[:, hc * 128:(hc + 1) * 128],
                                              ltile(hc)[:, h0:h1], start=(hc == 0), stop=False)
                                tn.matmul(ps_g[:, h0:h1], C.m128_bf[:], tsum_bf[0:1, h0:h1],
                                          start=False, stop=True)
                                for hc in range(MF):
                                    tn.matmul(ps_u[:, h0:h1],
                                              slot2(ubf, ic, D)[:, hc * 128:(hc + 1) * 128],
                                              ltile(hc)[:, h0:h1], start=(hc == 0), stop=False)
                                last = tn.matmul(ps_u[:, h0:h1], C.m128_bf[:], tsum_bf[0:1, h0:h1],
                                                 start=False, stop=True)
                            P.inc(last, "mm")
                            drain(C, tn, "mmd")
                fence(C, sems=())

        with ExitStack() as s5:
            hs_ps = s5.enter_context(nc.psum_tensor("hs_ps", [1, TPC], F32))
            hs_f = s5.enter_context(nc.sbuf_tensor("hs_f", [1, TPC], F32))
            lo_f = s5.enter_context(nc.sbuf_tensor("lo_f", [1, TPC], F32))
            with nc.Block() as b:
                @b.tensor
                def _(tn):
                    for ic in range(IC0):
                        for (h0, h1) in HV:
                            last = tn.matmul(hs_ps[0:1, h0:h1], C.ones128_bf[:],
                                             h_sb[:, ic * TPC + h0:ic * TPC + h1],
                                             start=(ic == 0), stop=(ic == IC0 - 1))
                    P.inc(last, "t")
                    drain(C, tn, "t")
            with nc.Block() as b:
                @b.vector
                def _(v):
                    P.wait(v, "t")
                    vsync(P, v, v.tensor_copy(hs_f[:], hs_ps[:]))
                    vsync(P, v, v.tensor_copy(hi_bf[:], hs_f[:]))
                    vsync(P, v, v.tensor_tensor(lo_f[:], hs_f[:], hi_bf[:], ALU.subtract))
                    v.tensor_copy(lo_bf[:], lo_f[:])
            fence(C, sems=())

        with ExitStack() as s6:
            e6 = s6.enter_context
            d8 = e6(nc.sbuf_tensor("d8", [128, 2 * HID0], U8))
            dbf = e6(nc.sbuf_tensor("dbf", [128, 2 * HID0], BF16))
            fo = e6(nc.sbuf_tensor("fo_l0f", [128, 2 * TPC], F32))
            ps_d = [e6(nc.psum_tensor(f"ps_d{i}", [128, TPC], F32)) for i in range(2)]
            BDl = P.base("ld")["ld"]
            BDg = P.base("g")["g"]
            with nc.Block() as b:
                @b.sync
                def _(sy):
                    P.inc(sy.dma_start(out=slot2(d8, 0, HID0), in_=io["dw_t"][0]), "ld", 16)
            for dc in range(MF):
                with nc.Block() as b:
                    @b.sync
                    def _(sy, dc=dc):
                        if dc + 1 < MF:
                            P.inc(sy.dma_start(out=slot2(d8, dc + 1, HID0),
                                               in_=io["dw_t"][dc + 1]), "ld", 16)
                with nc.Block() as b:
                    @b.vector
                    def _(v, dc=dc):
                        v.wait_ge(P.sem("ld"), BDl + 16 * (dc + 1))
                        P.inc(v.tensor_copy(slot2(dbf, dc, HID0)[:, 0:HID0 // 2],
                                            slot2(d8, dc, HID0)[:, 0:HID0 // 2]), "cst")

                    @b.scalar
                    def _(s, dc=dc):
                        s.wait_ge(P.sem("ld"), BDl + 16 * (dc + 1))
                        P.inc(s.activation(slot2(dbf, dc, HID0)[:, HID0 // 2:HID0],
                                           slot2(d8, dc, HID0)[:, HID0 // 2:HID0], AF.Identity),
                              "cs2")
                with nc.Block() as b:
                    @b.tensor
                    def _(tn, dc=dc):
                        P.wait(tn, "cst")
                        P.wait(tn, "cs2")
                        for (h0, h1) in HV:
                            for ic in range(IC0):
                                tn.matmul(ps_d[dc % 2][:, h0:h1],
                                          slot2(dbf, dc, HID0)[:, ic * 128:(ic + 1) * 128],
                                          h_sb[:, ic * TPC + h0:ic * TPC + h1],
                                          start=(ic == 0), stop=False)
                            tn.matmul(ps_d[dc % 2][:, h0:h1], C.m128_bf[:], hi_bf[0:1, h0:h1],
                                      start=False, stop=False)
                            last = tn.matmul(ps_d[dc % 2][:, h0:h1], C.m128_bf[:], lo_bf[0:1, h0:h1],
                                             start=False, stop=True)
                        P.inc(last, "mm")
                with nc.Block() as b:
                    @b.scalar
                    def _(s, dc=dc):
                        P.wait(s, "mm")
                        if dc >= 2:
                            s.wait_ge(P.sem("g"), BDg + 16 * (dc - 1))
                        P.inc(s.activation(slot2(fo, dc, TPC), ps_d[dc % 2][:], AF.Identity,
                                           scale=dsc[:, dc:dc + 1]), "evs")
                with nc.Block() as b:
                    @b.gpsimd
                    def _(g, dc=dc):
                        P.wait(g, "evs")
                        P.inc(g.dma_start(out=f_dram[dc * 128:(dc + 1) * 128, :],
                                          in_=slot2(fo, dc, TPC)), "g", 16)
            fence(C)

        sch.close()
        af = e(nc.sbuf_tensor("af_l0f", [128, TPC], F32))
        bfc = e(nc.sbuf_tensor("bf_l0f", [128, TPC], F32))
        emit_stats(C, lambda mf: f_dram[mf * 128:(mf + 1) * 128, :], af, bfc)
        fence(C)
        emit_apply_post(C, lambda mf: f_dram[mf * 128:(mf + 1) * 128, :], xsrc, x_out, af, bfc)
        fence(C)


def ph_l1a(C, io):
    nc, P = C.nc, C.P
    x_in, x_out, f_dram = io["x_in"], io["x_out"], io["f_dram"]
    lng, ag_in = io["lng"], io["ag_in"]
    a2a_in, attg = io["a2a_in"], io["attg"]

    def xsrc(mf):
        return x_in[mf * 128:(mf + 1) * 128, :]

    with ExitStack() as sc:
        e = sc.enter_context
        qcat = e(nc.sbuf_tensor("qcat", [128, 2 * 8192], BF16))
        kcat = e(nc.sbuf_tensor("kcat", [128, 2 * 8192], BF16))
        vtm = e(nc.sbuf_tensor("vtm", [128, 64 * 256], BF16))
        gatesc = e(nc.sbuf_tensor("gatesc", [128, 2], F32))
        sig_sb = e(nc.sbuf_tensor("sig_sb", [128, 4], F32))
        msk_sb = e(nc.sbuf_tensor("msk_sb", [128, 5 * 512], BF16))
        rope_sb = e(nc.sbuf_tensor("rope_sb", [64, SEQ], F32))
        rope2_sb = e(nc.sbuf_tensor("rope2_sb", [64, SEQ], F32))
        gl_sb = e(nc.sbuf_tensor("gl_sb", [2, 1], F32))
        gv_sb = e(nc.sbuf_tensor("gv_sb", [2, 2], F32))

        def qc(hl):
            return qcat[:, hl * 8192:(hl + 1) * 8192]

        def kc(hl):
            return kcat[:, hl * 8192:(hl + 1) * 8192]

        with ExitStack() as s12:
            ln2 = s12.enter_context(nc.sbuf_tensor("ln2", [128, MF * TPC], BF16))
            a2 = s12.enter_context(nc.sbuf_tensor("a2", [128, TPC], F32))
            b2 = s12.enter_context(nc.sbuf_tensor("b2", [128, TPC], F32))
            emit_stats(C, xsrc, a2, b2)
            fence(C)

            def eg(g, mf):
                P.inc(g.dma_start(out=ag_in[mf * 128:(mf + 1) * 128, :],
                                  in_=ln2[:, mf * TPC:(mf + 1) * TPC]), "g", 16)
            emit_apply_pre(C, xsrc, ln2, a2, b2, extra_g=eg)
        with nc.Block() as b:
            @b.sync
            def _(sy):
                P.inc(sy.dma_start(out=gl_sb[:], in_=io["gl"][:]), "ld", 16)
                P.inc(sy.dma_start(out=rope_sb[:], in_=io["rope1"][0:64, :]), "ld", 16)
                P.inc(sy.dma_start(out=rope2_sb[:], in_=io["rope1"][64:128, :]), "ld", 16)
                for d_ in range(5):
                    P.inc(sy.dma_start(out=msk_sb[:, d_ * 512:(d_ + 1) * 512],
                                       in_=io["masks"][d_]), "ld", 16)

            @b.gpsimd
            def _(g):
                P.wait(g, "g")
                P.inc(g.collective_compute("AllGather", ALU.bypass, replica_groups=RG,
                                           ins=[ag_in[:]], outs=[lng[:]]), "cc")
        fence(C, sems=("ld",))
        with nc.Block() as b:
            @b.scalar
            def _(s):
                P.wait(s, "ld")
                P.inc(s.activation(gl_sb[:], gl_sb[:], AF.Sigmoid), "s")
        with nc.Block() as b:
            @b.vector
            def _(v):
                P.wait(v, "s")
                v.tensor_scalar(gv_sb[:, 0:1], gl_sb[:], -0.125, 0.125, ALU.mult, ALU.add)
                P.inc(v.tensor_scalar(gv_sb[:, 1:2], gl_sb[:], 0.125, None, ALU.mult), "v")
        with nc.Block() as b:
            @b.gpsimd
            def _(g):
                P.wait(g, "v")
                P.inc(g.dma_start(out=io["gtmp"][:], in_=gv_sb[:]), "g", 16)
        fence(C, sems=("g",))
        with nc.Block() as b:
            @b.sync
            def _(sy):
                for hl in range(2):
                    P.inc(sy.dma_start(out=gatesc[0:64, hl:hl + 1],
                                       in_=bass.AP(io["gtmp"], 2 * hl, [[0, 64], [1, 1]])), "ld", 16)
                    P.inc(sy.dma_start(out=gatesc[64:128, hl:hl + 1],
                                       in_=bass.AP(io["gtmp"], 2 * hl + 1, [[0, 64], [1, 1]])), "ld", 16)
        fence(C)

        with ExitStack() as s4:
            wqb = s4.enter_context(nc.sbuf_tensor("wqb", [128, MF * 256], BF16))
            wkb = s4.enter_context(nc.sbuf_tensor("wkb", [128, MF * 256], BF16))
            wvb = s4.enter_context(nc.sbuf_tensor("wvb", [128, MF * 256], BF16))
            with ExitStack() as s4f:
                wf = s4f.enter_context(nc.sbuf_tensor("wf_stage", [128, MF * 256], F32))
                for (wt, wb) in ((io["wq_t"], wqb), (io["wk_t"], wkb), (io["wv_t"], wvb)):
                    with nc.Block() as b:
                        @b.sync
                        def _(sy, wt=wt):
                            for m in range(MF):
                                P.inc(sy.dma_start(out=wf[:, m * 256:(m + 1) * 256],
                                                   in_=wt[m]), "ld", 16)
                    fence(C, sems=("ld",))
                    with nc.Block() as b:
                        @b.vector
                        def _(v, wb=wb):
                            P.inc(v.tensor_copy(wb[:], wf[:]), "v")
                    fence(C, sems=())

            with ExitStack() as s5:
                e5 = s5.enter_context
                lng_sb = e5(nc.sbuf_tensor("lng_sb", [128, MF * 512], BF16))
                swp = e5(nc.sbuf_tensor("swp", [64, 4 * 512], BF16))
                rp1 = e5(nc.sbuf_tensor("rp1", [64, 512], F32))
                rp2 = e5(nc.sbuf_tensor("rp2", [64, 512], F32))
                ps_q = e5(nc.psum_tensor("ps_q", [128, 2 * 512], F32))
                ps_k = e5(nc.psum_tensor("ps_k", [128, 2 * 512], F32))
                ps_v = e5(nc.psum_tensor("ps_v", [128, 2 * 512], F32))
                B = P.base("ld", "mm", "mmd", "vev", "swp", "rop")
                with nc.Block() as b:
                    @b.sync
                    def _(sy):
                        for ts in range(16):
                            if ts >= 1:
                                sy.wait_ge(P.sem("mm"), B["mm"] + ts)
                            r0 = (ts // 2) * 2048
                            c0 = (ts % 2) * 512
                            for mf in range(MF):
                                P.inc(sy.dma_start(
                                    out=lng_sb[:, mf * 512:(mf + 1) * 512],
                                    in_=lng[r0 + mf * 128:r0 + (mf + 1) * 128, c0:c0 + 512]),
                                    "ld", 16)

                    @b.tensor
                    def _(tn):
                        P.wait(tn, "v")
                        P.wait(tn, "s")
                        for ts in range(16):
                            tn.wait_ge(P.sem("ld"), B["ld"] + 256 * (ts + 1))
                            if ts >= 1:
                                tn.wait_ge(P.sem("vev"), B["vev"] + 5 * ts)
                            for hl in range(2):
                                for (wb, pp) in ((wqb, ps_q), (wkb, ps_k)):
                                    for mf in range(MF):
                                        last = tn.matmul(
                                            pp[:, hl * 512:(hl + 1) * 512],
                                            wb[:, mf * 256 + hl * 128:mf * 256 + (hl + 1) * 128],
                                            lng_sb[:, mf * 512:(mf + 1) * 512],
                                            start=(mf == 0), stop=(mf == MF - 1))
                            for tc in range(4):
                                for mf in range(MF):
                                    last = tn.matmul(
                                        ps_v[:, (tc % 2) * 256 + (tc // 2) * 512:
                                             (tc % 2) * 256 + (tc // 2) * 512 + 256],
                                        lng_sb[:, mf * 512 + tc * 128:mf * 512 + (tc + 1) * 128],
                                        wvb[:, mf * 256:(mf + 1) * 256],
                                        start=(mf == 0), stop=(mf == MF - 1))
                            P.inc(last, "mm")
                            drain(C, tn, "mmd")

                    @b.vector
                    def _(v):
                        for ts in range(16):
                            v.wait_ge(P.sem("mmd"), B["mmd"] + ts + 1)
                            span = slice(ts * 512, (ts + 1) * 512)
                            for hl in range(2):
                                P.inc(v.tensor_scalar(qc(hl)[:, span], ps_q[:, hl * 512:(hl + 1) * 512],
                                                      gatesc[:, hl:hl + 1], None, ALU.mult), "vev")
                                P.inc(v.tensor_copy(kc(hl)[:, span], ps_k[:, hl * 512:(hl + 1) * 512]),
                                      "vev")
                            P.inc(v.tensor_copy(vtm[:, ts * 1024:(ts + 1) * 1024], ps_v[:]), "vev")
                            v.wait_ge(P.sem("swp"), B["swp"] + 128 * (ts + 1))
                            for gi, (buf, hl) in enumerate(((qcat, 0), (qcat, 1), (kcat, 0), (kcat, 1))):
                                geo = (qc(hl) if buf is qcat else kc(hl))[0:64, span]
                                v.tensor_tensor(rp1[:], geo, rope_sb[:, (ts % 8) * 512:(ts % 8 + 1) * 512],
                                                ALU.mult)
                                v.tensor_tensor(rp2[:], swp[:, gi * 512:(gi + 1) * 512],
                                                rope2_sb[:, (ts % 8) * 512:(ts % 8 + 1) * 512],
                                                ALU.mult)
                                P.inc(v.tensor_tensor(geo, rp1[:], rp2[:], ALU.add), "rop")

                    @b.gpsimd
                    def _(g):
                        for ts in range(16):
                            g.wait_ge(P.sem("vev"), B["vev"] + 5 * ts + 4)
                            if ts >= 1:
                                g.wait_ge(P.sem("rop"), B["rop"] + 4 * ts)
                            span = slice(ts * 512, (ts + 1) * 512)
                            for gi, (buf_hl) in enumerate(((0, 0), (0, 1), (1, 0), (1, 1))):
                                isq, hl = buf_hl
                                src = (qc(hl) if isq == 0 else kc(hl))
                                P.inc(g.dma_start(out=swp[0:32, gi * 512:gi * 512 + 512],
                                                  in_=src[32:64, span]), "swp", 16)
                                P.inc(g.dma_start(out=swp[32:64, gi * 512:gi * 512 + 512],
                                                  in_=src[0:32, span]), "swp", 16)
                fence(C)

        with ExitStack() as s6:
            e6 = s6.enter_context
            sel = e6(nc.sbuf_tensor("sel", [128, 2], BF16))
            sqq = e6(nc.sbuf_tensor("sqq", [128, SEQ], BF16))
            ps_n = e6(nc.psum_tensor("ps_n", [2, 512], F32))
            nsb = e6(nc.sbuf_tensor("nsb", [2, 512], F32))
            rmx = e6(nc.sbuf_tensor("rmx", [2, 8], F32))
            rq = e6(nc.sbuf_tensor("rq", [2, 1], F32))
            rk = e6(nc.sbuf_tensor("rk", [2, 1], F32))
            ps_d1 = e6(nc.psum_tensor("sig_d1", [1, 1], F32))
            ps_bc = e6(nc.psum_tensor("ps_bc", [128, 1], F32))
            sgn = e6(nc.sbuf_tensor("sgn", [1, 1], F32))
            with nc.Block() as b:
                @b.vector
                def _(v):
                    v.memset(sel[:, 0:2], 0.0)
                    v.memset(sel[0:64, 0:1], 1.0)
                    P.inc(v.memset(sel[64:128, 1:2], 1.0), "v")
            for hl in range(2):
                for bb_ in range(2):
                    for (srcbuf, rdst) in ((qc(hl), rq), (kc(hl), rk)):
                        with nc.Block() as b:
                            @b.scalar
                            def _(s, srcbuf=srcbuf, bb_=bb_):
                                P.inc(s.activation(sqq[:], srcbuf[:, bb_ * 4096:(bb_ + 1) * 4096],
                                                   AF.Square), "s")
                                P.inc(s.activation(C.fscr[0:1, 2:3], C.fscr[0:1, 2:3], AF.Square), "s")
                        for ch in range(8):
                            with nc.Block() as b:
                                @b.tensor
                                def _(tn, ch=ch):
                                    P.wait(tn, "v")
                                    P.wait(tn, "s")
                                    P.inc(tn.matmul(ps_n[:], sel[:], sqq[:, ch * 512:(ch + 1) * 512],
                                                    start=True, stop=True), "t")
                                    drain(C, tn, "t")
                            with nc.Block() as b:
                                @b.vector
                                def _(v, ch=ch):
                                    P.wait(v, "t")
                                    vsync(P, v, v.tensor_copy(nsb[:], ps_n[:]))
                                    v.reduce_max(rmx[:, ch:ch + 1], nsb[:], axis=AX)
                        with nc.Block() as b:
                            @b.vector
                            def _(v, rdst=rdst):
                                P.inc(v.reduce_max(rdst[:], rmx[:], axis=AX), "v")
                        with nc.Block() as b:
                            @b.scalar
                            def _(s, rdst=rdst):
                                P.wait(s, "v")
                                P.inc(s.activation(rdst[:], rdst[:], AF.Sqrt), "s")
                    with nc.Block() as b:
                        @b.tensor
                        def _(tn):
                            P.wait(tn, "s")
                            P.inc(tn.matmul(ps_d1[:], rq[:], rk[:], start=True, stop=True), "t")
                            drain(C, tn, "t")
                    with nc.Block() as b:
                        @b.vector
                        def _(v):
                            P.wait(v, "t")
                            vsync(P, v, v.tensor_copy(sgn[:], ps_d1[:]))
                            P.inc(v.tensor_scalar(sgn[:], sgn[:], -1.0, None, ALU.mult), "v")
                    with nc.Block() as b:
                        @b.tensor
                        def _(tn):
                            P.wait(tn, "v")
                            P.inc(tn.matmul(ps_bc[:], C.ones1_f32[:], sgn[:], start=True, stop=True),
                                  "t")
                            drain(C, tn, "t")
                    with nc.Block() as b:
                        @b.vector
                        def _(v, hl=hl, bb_=bb_):
                            P.wait(v, "t")
                            P.inc(v.tensor_copy(sig_sb[:, bb_ * 2 + hl:bb_ * 2 + hl + 1], ps_bc[:]),
                                  "v")
            fence(C, sems=())

        with ExitStack() as s7:
            e7 = s7.enter_context
            ps_sc = [e7(nc.psum_tensor(f"ps_sc{i}", [128, 512], F32)) for i in range(3)]
            ps_av = e7(nc.psum_tensor("ps_av", [128, 512], F32))
            ps_sum = e7(nc.psum_tensor("ps_sum", [1, 512], F32))
            ps_rb = e7(nc.psum_tensor("ps_rb", [128, 512], F32))
            p_t = e7(nc.sbuf_tensor("p_t", [128, 3 * 512], BF16))
            att_t = e7(nc.sbuf_tensor("att_t", [128, 512], BF16))
            sumf = e7(nc.sbuf_tensor("sumf", [1, 512], F32))
            sumr = e7(nc.sbuf_tensor("sumr", [1, 512], F32))
            rbf = e7(nc.sbuf_tensor("rbf", [128, 512], F32))
            for bb_ in range(2):
                for hl in range(2):
                    for sm in range(8):
                        nj = 4 * sm + 4
                        gs = bb_ * 8 + sm
                        qspan = slice(bb_ * 4096 + sm * 512, bb_ * 4096 + (sm + 1) * 512)
                        B = P.base("tsc", "tav", "tsm", "exp", "msk", "vfi", "trb", "g")

                        def kcol(j):
                            return bb_ * 4096 + j * 128

                        def vcol(j):
                            return (32 * bb_ + j) * 256 + hl * 128

                        def pts(j):
                            return p_t[:, (j % 3) * 512:(j % 3 + 1) * 512]

                        with nc.Block() as blk:
                            @blk.tensor
                            def _(tn, hl=hl, bb_=bb_, nj=nj, qspan=qspan, B=B,
                                  kcol=kcol, vcol=vcol, pts=pts):
                                P.wait(tn, "rop")
                                P.wait(tn, "vev")
                                for j in range(nj):
                                    if j >= 3:
                                        tn.wait_ge(P.sem("exp"), B["exp"] + j - 2)
                                    P.inc(tn.matmul(ps_sc[j % 3][:],
                                                    kc(hl)[:, kcol(j):kcol(j) + 128],
                                                    qc(hl)[:, qspan],
                                                    start=True, stop=True), "tsc")
                                    if j == nj - 1:
                                        drain(C, tn, "tsc")
                                    if j >= 2:
                                        jj = j - 2
                                        tn.wait_ge(P.sem("msk"), B["msk"] + jj + 1)
                                        P.inc(tn.matmul(ps_av[:], vtm[:, vcol(jj):vcol(jj) + 128],
                                                        pts(jj), start=(jj == 0), stop=False), "tav")
                                        P.inc(tn.matmul(ps_sum[:], C.ones128_bf[:],
                                                        pts(jj), start=(jj == 0), stop=False), "tsm")
                                for jj in (nj - 2, nj - 1):
                                    tn.wait_ge(P.sem("msk"), B["msk"] + jj + 1)
                                    P.inc(tn.matmul(ps_av[:], vtm[:, vcol(jj):vcol(jj) + 128],
                                                    pts(jj), start=(jj == 0), stop=(jj == nj - 1)),
                                          "tav")
                                    P.inc(tn.matmul(ps_sum[:], C.ones128_bf[:],
                                                    pts(jj), start=(jj == 0), stop=(jj == nj - 1)),
                                          "tsm")
                                drain(C, tn, "tsm")
                                tn.wait_ge(P.sem("vfi"), B["vfi"] + 1)
                                P.inc(tn.matmul(ps_rb[:], C.ones1_f32[:], sumr[:],
                                                start=True, stop=True), "trb")
                                drain(C, tn, "trb")

                            @blk.scalar
                            def _(s, hl=hl, bb_=bb_, nj=nj, B=B, pts=pts):
                                P.wait(s, "v")
                                for j in range(nj):
                                    s.wait_ge(P.sem("tsc"), B["tsc"] + min(j + 2, nj + 1))
                                    if j >= 3:
                                        s.wait_ge(P.sem("tav"), B["tav"] + j - 2)
                                    P.inc(s.activation(pts(j), ps_sc[j % 3][:], AF.Exp,
                                                       bias=sig_sb[:, bb_ * 2 + hl:bb_ * 2 + hl + 1]),
                                          "exp")
                                P.inc(s.activation(C.fscr[0:1, 0:4], C.fscr[0:1, 0:4], AF.Square),
                                      "exp")

                            @blk.vector
                            def _(v, nj=nj, B=B, pts=pts):
                                for j in range(nj):
                                    d_ = j - (nj - 4) if j >= nj - 4 else 4
                                    v.wait_ge(P.sem("exp"), B["exp"] + min(j + 2, nj + 1))
                                    if j >= 3:
                                        v.wait_ge(P.sem("tav"), B["tav"] + j - 2)
                                    P.inc(v.tensor_tensor(pts(j), pts(j),
                                                          msk_sb[:, d_ * 512:(d_ + 1) * 512],
                                                          ALU.mult), "msk")
                                v.wait_ge(P.sem("tsm"), B["tsm"] + nj + 1)
                                vsync(P, v, v.tensor_copy(sumf[:], ps_sum[:]))
                                P.inc(v.reciprocal(sumr[:], sumf[:]), "vfi")
                                v.wait_ge(P.sem("trb"), B["trb"] + 2)
                                v.tensor_copy(rbf[:], ps_rb[:])
                                P.wait(v, "g")
                                P.inc(v.tensor_tensor(att_t[:], ps_av[:], rbf[:], ALU.mult), "vfi")

                            @blk.gpsimd
                            def _(g, hl=hl, gs=gs, B=B):
                                g.wait_ge(P.sem("vfi"), B["vfi"] + 2)
                                dd, half = gs // 2, gs % 2
                                P.inc(g.dma_start(
                                    out=a2a_in[dd * 256 + hl * 128:dd * 256 + (hl + 1) * 128,
                                               half * 512:(half + 1) * 512],
                                    in_=att_t[:]), "g", 16)
            fence(C)

        with nc.Block() as b:
            @b.gpsimd
            def _(g):
                with allow_ici_all_to_all():
                    P.inc(g.collective_compute("AllToAll", ALU.bypass, replica_groups=RG,
                                               ins=[a2a_in[:]], outs=[attg[:]]), "cc")
        fence(C)

        with ExitStack() as s9:
            e9 = s9.enter_context
            att_sb = e9(nc.sbuf_tensor("att_sb", [128, MF * TPC], BF16))
            wsl = e9(nc.sbuf_tensor("wsl9", [128, 2 * D], F32))
            wbf = e9(nc.sbuf_tensor("wbf9", [128, 2 * D], BF16))
            fo = e9(nc.sbuf_tensor("fo9", [128, 2 * TPC], F32))
            ps = [e9(nc.psum_tensor(f"wo9_ps{i}", [128, TPC], F32)) for i in range(2)]
            with nc.Block() as b:
                @b.sync
                def _(sy):
                    for mf in range(MF):
                        P.inc(sy.dma_start(out=att_sb[:, mf * TPC:(mf + 1) * TPC],
                                           in_=attg[mf * 128:(mf + 1) * 128, :]), "ld", 16)
            fence(C, sems=("ld",))
            BWl = P.base("ld")["ld"]
            BWg = P.base("g")["g"]
            with nc.Block() as b:
                @b.sync
                def _(sy):
                    P.inc(sy.dma_start(out=slot2(wsl, 0, D), in_=io["wo_t"][0]), "ld", 16)
            for oc in range(MF):
                with nc.Block() as b:
                    @b.sync
                    def _(sy, oc=oc):
                        if oc + 1 < MF:
                            P.inc(sy.dma_start(out=slot2(wsl, oc + 1, D),
                                               in_=io["wo_t"][oc + 1]), "ld", 16)
                with nc.Block() as b:
                    @b.vector
                    def _(v, oc=oc):
                        v.wait_ge(P.sem("ld"), BWl + 16 * (oc + 1))
                        P.inc(v.tensor_copy(slot2(wbf, oc, D), slot2(wsl, oc, D)), "cst")
                with nc.Block() as b:
                    @b.tensor
                    def _(tn, oc=oc):
                        P.wait(tn, "cst")
                        P.wait(tn, "v")
                        for hv in range(MF):
                            for (h0, h1) in HV:
                                last = tn.matmul(ps[oc % 2][:, h0:h1],
                                                 slot2(wbf, oc, D)[:, hv * 128:(hv + 1) * 128],
                                                 att_sb[:, hv * TPC + h0:hv * TPC + h1],
                                                 start=(hv == 0), stop=(hv == MF - 1))
                        P.inc(last, "mm")
                with nc.Block() as b:
                    @b.scalar
                    def _(s, oc=oc):
                        P.wait(s, "mm")
                        if oc >= 2:
                            s.wait_ge(P.sem("g"), BWg + 16 * (oc - 1))
                        P.inc(s.activation(slot2(fo, oc, TPC), ps[oc % 2][:], AF.Identity), "evs")
                with nc.Block() as b:
                    @b.gpsimd
                    def _(g, oc=oc):
                        P.wait(g, "evs")
                        P.inc(g.dma_start(out=f_dram[oc * 128:(oc + 1) * 128, :],
                                          in_=slot2(fo, oc, TPC)), "g", 16)
            fence(C)

        af = e(nc.sbuf_tensor("af_l1a", [128, TPC], F32))
        bfc = e(nc.sbuf_tensor("bf_l1a", [128, TPC], F32))
        emit_stats(C, lambda mf: f_dram[mf * 128:(mf + 1) * 128, :], af, bfc)
        fence(C)
        emit_apply_post(C, lambda mf: f_dram[mf * 128:(mf + 1) * 128, :], xsrc, x_out, af, bfc)
        fence(C)


def ph_l1f(C, io):
    nc, P = C.nc, C.P
    x_in, x_out, f_dram = io["x_in"], io["x_out"], io["f_dram"]

    def xsrc(mf):
        return x_in[mf * 128:(mf + 1) * 128, :]

    with ExitStack() as sc:
        e = sc.enter_context
        h_sb = e(nc.sbuf_tensor("h1_sb", [128, IC1 * TPC], BF16))
        thp = e(nc.sbuf_tensor("thp", [128, 3], F32))
        thm = e(nc.sbuf_tensor("thm", [128, 3], F32))
        s23 = e(nc.sbuf_tensor("s23", [128, 1], F32))

        def htile(ic):
            return h_sb[:, ic * TPC:(ic + 1) * TPC]

        with ExitStack() as s0:
            e0 = s0.enter_context
            wst = e0(nc.sbuf_tensor("q_wst", [128, 2 * 5504], F32))
            acc = e0(nc.sbuf_tensor("q_acc", [128, 48], F32))
            accb = e0(nc.sbuf_tensor("q_accb", [128, 48], BF16))
            accb2 = e0(nc.sbuf_tensor("q_accb2", [128, 48], BF16))
            abs_ps = e0(nc.psum_tensor("abs_ps", [1, 48], F32))
            part = e0(nc.sbuf_tensor("q_part", [1, 4], F32))
            ssb = e0(nc.sbuf_tensor("q_ssb", [1, 4], F32))
            jobs = ([("w1a", dcl, HID1P) for dcl in range(2)]
                    + [("w3a", dcl, HID1P) for dcl in range(2)]
                    + [("w2a", icp, 256) for icp in range(IC1)])
            NJOB = len(jobs)
            B = P.base("g", "qab", "qmm")
            with nc.Block() as b:
                @b.gpsimd
                def _(g):
                    for i, (nm, idx, wid) in enumerate(jobs):
                        if i >= 2:
                            g.wait_ge(P.sem("qab"), B["qab"] + i - 1)
                        P.inc(g.dma_start(out=slot2(wst, i, 5504)[:, 0:wid],
                                          in_=io[nm][idx]), "g", 16)

                @b.vector
                def _(v):
                    for i, (nm, idx, wid) in enumerate(jobs):
                        v.wait_ge(P.sem("g"), B["g"] + 16 * (i + 1))
                        P.inc(v.reduce_sum(acc[:, i:i + 1], slot2(wst, i, 5504)[:, 0:wid],
                                           axis=AX, apply_absolute_value=True), "qab")
                    vsync(P, v, v.tensor_copy(accb[:], acc[:]))
                    vsync(P, v, v.tensor_tensor(acc[:], acc[:], accb[:], ALU.subtract))
                    P.inc(v.tensor_copy(accb2[:], acc[:]), "qab")
            with nc.Block() as b:
                @b.tensor
                def _(tn):
                    P.wait(tn, "qab")
                    tn.matmul(abs_ps[0:1, 0:NJOB], C.ones128_bf[:], accb[:, 0:NJOB],
                              start=True, stop=False)
                    P.inc(tn.matmul(abs_ps[0:1, 0:NJOB], C.ones128_bf[:], accb2[:, 0:NJOB],
                                    start=False, stop=True), "qmm")
                    drain(C, tn, "qmm")
            with nc.Block() as b:
                @b.vector
                def _(v):
                    P.wait(v, "qmm")
                    vsync(P, v, v.memset(part[:], 0.0))
                    v.reduce_sum(part[0:1, 0:1], abs_ps[0:1, 0:2], axis=AX)
                    v.reduce_sum(part[0:1, 1:2], abs_ps[0:1, 2:4], axis=AX)
                    P.inc(v.reduce_sum(part[0:1, 2:3], abs_ps[0:1, 4:4 + IC1], axis=AX), "v")
            fence(C, sems=())
            with nc.Block() as b:
                @b.gpsimd
                def _(g):
                    P.wait(g, "v")
                    P.inc(g.dma_start(out=io["s_in"][:], in_=part[:]), "g", 16)
                    P.wait(g, "g")
                    P.inc(g.collective_compute("AllReduce", ALU.add, replica_groups=RG,
                                               ins=[io["s_in"][:]], outs=[io["s_out"][:]]), "cc")
                    P.wait(g, "cc")
                    P.inc(g.dma_start(out=ssb[:], in_=io["s_out"][:]), "g", 16)
            fence(C)
            with nc.Block() as b:
                @b.vector
                def _(v):
                    vsync(P, v, v.tensor_scalar(ssb[:], ssb[:], 0.5 / float(HID1 * D), None, ALU.mult))
                    P.inc(v.tensor_tensor(ssb[0:1, 3:4], ssb[0:1, 1:2], ssb[0:1, 2:3], ALU.mult),
                          "v")
            with nc.Block() as b:
                @b.gpsimd
                def _(g):
                    P.wait(g, "v")
                    P.inc(g.dma_start(out=io["th_dram"][:], in_=ssb[:]), "g", 16)
                    if "thdbg" in io:
                        P.inc(g.dma_start(out=io["thdbg"][:], in_=ssb[:]), "g", 16)
            fence(C, sems=("g",))
            with nc.Block() as b:
                @b.sync
                def _(sy):
                    for w in range(3):
                        P.inc(sy.dma_start(out=thp[:, w:w + 1],
                                           in_=bass.AP(io["th_dram"], w, [[0, 128], [1, 1]])), "ld", 16)
                    P.inc(sy.dma_start(out=s23[:],
                                       in_=bass.AP(io["th_dram"], 3, [[0, 128], [1, 1]])), "ld", 16)
            fence(C, sems=("ld",))
            with nc.Block() as b:
                @b.vector
                def _(v):
                    P.inc(v.tensor_scalar(thm[:], thp[:], -1.0, None, ALU.mult), "v")
            fence(C, sems=())

        with ExitStack() as s12:
            ln3 = s12.enter_context(nc.sbuf_tensor("ln3", [128, MF * TPC], BF16))
            a3 = s12.enter_context(nc.sbuf_tensor("a3", [128, TPC], F32))
            b3 = s12.enter_context(nc.sbuf_tensor("b3", [128, TPC], F32))

            def ltile(mf):
                return ln3[:, mf * TPC:(mf + 1) * TPC]

            emit_stats(C, xsrc, a3, b3)
            fence(C)
            emit_apply_pre(C, xsrc, ln3, a3, b3)
            fence(C)

            with ExitStack() as s3:
                e3 = s3.enter_context
                w1s = e3(nc.sbuf_tensor("w1s_sl", [128, 2 * D], F32))
                w3s = e3(nc.sbuf_tensor("w3s_sl", [128, 2 * D], F32))
                t1b = e3(nc.sbuf_tensor("t1b", [128, 2 * D], BF16))
                t3b = e3(nc.sbuf_tensor("t3b", [128, 2 * D], BF16))
                sgA = e3(nc.sbuf_tensor("sgA", [128, D], BF16))
                sgB = e3(nc.sbuf_tensor("sgB", [128, D], BF16))
                sl = e3(nc.sbuf_tensor("sl_l1f", [128, 2 * TPC], BF16))
                ul = e3(nc.sbuf_tensor("ul_l1f", [128, 2 * TPC], BF16))
                ps_1 = e3(nc.psum_tensor("ps_1", [128, TPC], F32))
                ps_3 = e3(nc.psum_tensor("ps_3", [128, TPC], F32))
                scr_ps = e3(nc.psum_tensor("scr_ps_l1f3", [1, 128], F32))
                B = P.base("ld", "sg", "qtt", "mm", "mmd", "evs", "evv")
                with nc.Block() as b:
                    @b.sync
                    def _(sy):
                        for ic in range(IC1):
                            if ic >= 2:
                                sy.wait_ge(P.sem("sg"), B["sg"] + 2 * (ic - 2) + 2)
                            P.inc(sy.dma_start(out=slot2(w1s, ic, D), in_=io["w1f"][ic]), "ld", 16)
                            P.inc(sy.dma_start(out=slot2(w3s, ic, D), in_=io["w3f"][ic]), "ld", 16)

                    @b.scalar
                    def _(s):
                        P.wait(s, "v")
                        for ic in range(IC1):
                            s.wait_ge(P.sem("ld"), B["ld"] + 32 * (ic + 1))
                            if ic >= 1:
                                s.wait_ge(P.sem("qtt"), B["qtt"] + 2 * ic)
                            s.activation(sgA[:], slot2(w1s, ic, D), AF.Sign, bias=thm[:, 0:1])
                            P.inc(s.activation(sgB[:], slot2(w1s, ic, D), AF.Sign, bias=thp[:, 0:1]),
                                  "sg")
                            s.wait_ge(P.sem("qtt"), B["qtt"] + 2 * ic + 1)
                            s.activation(sgA[:], slot2(w3s, ic, D), AF.Sign, bias=thm[:, 1:2])
                            P.inc(s.activation(sgB[:], slot2(w3s, ic, D), AF.Sign, bias=thp[:, 1:2]),
                                  "sg")
                            if ic >= 1:
                                po = ic - 1
                                s.wait_ge(P.sem("mmd"), B["mmd"] + po + 1)
                                if po >= 2:
                                    s.wait_ge(P.sem("evv"), B["evv"] + po - 1)
                                P.inc(s.activation(slot2(sl, po, TPC), ps_1[:], AF.Silu,
                                                   scale=thp[:, 0:1]), "evs")
                        po = IC1 - 1
                        s.wait_ge(P.sem("mmd"), B["mmd"] + po + 1)
                        s.wait_ge(P.sem("evv"), B["evv"] + po - 1)
                        P.inc(s.activation(slot2(sl, po, TPC), ps_1[:], AF.Silu,
                                           scale=thp[:, 0:1]), "evs")

                    @b.vector
                    def _(v):
                        for ic in range(IC1):
                            v.wait_ge(P.sem("sg"), B["sg"] + 2 * ic + 1)
                            if ic >= 2:
                                v.wait_ge(P.sem("mm"), B["mm"] + ic - 1)
                            P.inc(v.tensor_tensor(slot2(t1b, ic, D), sgA[:], sgB[:], ALU.add), "qtt")
                            v.wait_ge(P.sem("sg"), B["sg"] + 2 * ic + 2)
                            P.inc(v.tensor_tensor(slot2(t3b, ic, D), sgA[:], sgB[:], ALU.add), "qtt")
                            if ic >= 1:
                                po = ic - 1
                                v.wait_ge(P.sem("mmd"), B["mmd"] + po + 1)
                                v.tensor_copy(slot2(ul, po, TPC), ps_3[:])
                                v.wait_ge(P.sem("evs"), B["evs"] + po + 1)
                                P.inc(v.tensor_tensor(htile(po), slot2(sl, po, TPC),
                                                      slot2(ul, po, TPC), ALU.mult), "evv")
                        po = IC1 - 1
                        v.wait_ge(P.sem("mmd"), B["mmd"] + po + 1)
                        v.tensor_copy(slot2(ul, po, TPC), ps_3[:])
                        v.wait_ge(P.sem("evs"), B["evs"] + po + 1)
                        P.inc(v.tensor_tensor(htile(po), slot2(sl, po, TPC), slot2(ul, po, TPC),
                                              ALU.mult), "evv")

                    @b.tensor
                    def _(tn):
                        for ic in range(IC1):
                            tn.wait_ge(P.sem("qtt"), B["qtt"] + 2 * ic + 2)
                            if ic >= 1:
                                tn.wait_ge(P.sem("evs"), B["evs"] + ic)
                                tn.wait_ge(P.sem("evv"), B["evv"] + ic)
                            for (h0, h1) in HV:
                                for dc in range(MF):
                                    tn.matmul(ps_1[:, h0:h1],
                                              slot2(t1b, ic, D)[:, dc * 128:(dc + 1) * 128],
                                              ltile(dc)[:, h0:h1], start=(dc == 0), stop=(dc == MF - 1))
                                for dc in range(MF):
                                    last = tn.matmul(ps_3[:, h0:h1],
                                                     slot2(t3b, ic, D)[:, dc * 128:(dc + 1) * 128],
                                                     ltile(dc)[:, h0:h1], start=(dc == 0),
                                                     stop=(dc == MF - 1))
                            P.inc(last, "mm")
                            P.inc(tn.matmul(scr_ps[:], C.ones128_bf[:],
                                            slot2(t1b, ic, D)[:, 0:128],
                                            start=True, stop=True), "mmd")
                fence(C, sems=())

        if "hdbg" in io:
            with nc.Block() as b:
                @b.gpsimd
                def _(g):
                    for ic in range(IC1):
                        P.inc(g.dma_start(out=io["hdbg"][ic * 128:(ic + 1) * 128, :],
                                          in_=htile(ic)), "g", 16)
            fence(C, sems=("g",))

        with ExitStack() as s4:
            e4 = s4.enter_context
            w2s = e4(nc.sbuf_tensor("w2s_sl", [128, 2 * HID1P], F32))
            t2b = e4(nc.sbuf_tensor("t2b", [128, 2 * HID1P], BF16))
            sgA2 = e4(nc.sbuf_tensor("sgA2", [128, HID1P], BF16))
            sgB2 = e4(nc.sbuf_tensor("sgB2", [128, HID1P], BF16))
            fo = e4(nc.sbuf_tensor("fo_l1f", [128, 2 * TPC], F32))
            ps_d = [e4(nc.psum_tensor(f"l1f_psd{i}", [128, TPC], F32)) for i in range(2)]
            BFl = P.base("ld")["ld"]
            BFg = P.base("g")["g"]
            with nc.Block() as b:
                @b.sync
                def _(sy):
                    P.inc(sy.dma_start(out=slot2(w2s, 0, HID1P), in_=io["w2f"][0]), "ld", 16)
            for dc in range(MF):
                with nc.Block() as b:
                    @b.sync
                    def _(sy, dc=dc):
                        if dc + 1 < MF:
                            P.inc(sy.dma_start(out=slot2(w2s, dc + 1, HID1P),
                                               in_=io["w2f"][dc + 1]), "ld", 16)
                with nc.Block() as b:
                    @b.scalar
                    def _(s, dc=dc):
                        s.wait_ge(P.sem("ld"), BFl + 16 * (dc + 1))
                        s.activation(sgA2[:], slot2(w2s, dc, HID1P), AF.Sign, bias=thm[:, 2:3])
                        P.inc(s.activation(sgB2[:], slot2(w2s, dc, HID1P), AF.Sign, bias=thp[:, 2:3]),
                              "sg")
                with nc.Block() as b:
                    @b.vector
                    def _(v, dc=dc):
                        P.wait(v, "sg")
                        P.inc(v.tensor_tensor(slot2(t2b, dc, HID1P), sgA2[:], sgB2[:], ALU.add), "qtt")
                with nc.Block() as b:
                    @b.tensor
                    def _(tn, dc=dc):
                        P.wait(tn, "qtt")
                        for (h0, h1) in HV:
                            for ic in range(IC1):
                                last = tn.matmul(ps_d[dc % 2][:, h0:h1],
                                                 slot2(t2b, dc, HID1P)[:, ic * 128:(ic + 1) * 128],
                                                 h_sb[:, ic * TPC + h0:ic * TPC + h1],
                                                 start=(ic == 0), stop=(ic == IC1 - 1))
                        P.inc(last, "mm")
                with nc.Block() as b:
                    @b.scalar
                    def _(s, dc=dc):
                        P.wait(s, "mm")
                        if dc >= 2:
                            s.wait_ge(P.sem("g"), BFg + 16 * (dc - 1))
                        P.inc(s.activation(slot2(fo, dc, TPC), ps_d[dc % 2][:], AF.Identity,
                                           scale=s23[:, 0:1]), "evs")
                with nc.Block() as b:
                    @b.gpsimd
                    def _(g, dc=dc):
                        P.wait(g, "evs")
                        P.inc(g.dma_start(out=f_dram[dc * 128:(dc + 1) * 128, :],
                                          in_=slot2(fo, dc, TPC)), "g", 16)
            fence(C)

        af = e(nc.sbuf_tensor("af_l1f", [128, TPC], F32))
        bfc = e(nc.sbuf_tensor("bf_l1f", [128, TPC], F32))
        emit_stats(C, lambda mf: f_dram[mf * 128:(mf + 1) * 128, :], af, bfc)
        fence(C)
        emit_apply_post(C, lambda mf: f_dram[mf * 128:(mf + 1) * 128, :], xsrc, x_out, af, bfc)
        fence(C)



def _mk_io(nc, specs):
    io = {}
    for nm, shp, dt, kind in specs:
        io[nm] = nc.dram_tensor(nm, list(shp), dt, kind=kind)
    return io


DBG_F = False
DBG_LIGHT = False

_COMMON = [("x_in", [D, TPC], F32, "ExternalInput"),
           ("x_out", [D, TPC], F32, "ExternalOutput")]

_SPECS = {
    "l0a": _COMMON + [
        ("w_in_t", [NOC_IN, 128, D], F32, "ExternalInput"),
        ("b_in_t", [128, NOC_IN], F32, "ExternalInput"),
        ("w_out_t", [MF, 128, D], F32, "ExternalInput"),
        ("b_out_t", [128, MF], F32, "ExternalInput"),
        ("gamma_t", [128, MF], F32, "ExternalInput"),
        ("beta_t", [128, MF], F32, "ExternalInput"),
        ("rot_cos", [512, TPC], F32, "ExternalInput"),
        ("rot_sin", [512, TPC], F32, "ExternalInput"),
        ("offs", [1, 4], I32, "ExternalInput"),
        ("ar_in", [2, 2048], F32, "Internal"),
    ],
    "l0f": _COMMON + [
        ("gw_t", [IC0, 128, D], U8, "ExternalInput"),
        ("uw_t", [IC0, 128, D], U8, "ExternalInput"),
        ("dw_t", [MF, 128, HID0], U8, "ExternalInput"),
        ("gsc_t", [128, IC0], F32, "ExternalInput"),
        ("usc_t", [128, IC0], F32, "ExternalInput"),
        ("dsc_t", [128, MF], F32, "ExternalInput"),
    ],
    "l1a": _COMMON + [
        ("wq_t", [MF, 128, 256], F32, "ExternalInput"),
        ("wk_t", [MF, 128, 256], F32, "ExternalInput"),
        ("wv_t", [MF, 128, 256], F32, "ExternalInput"),
        ("gl", [2, 1], F32, "ExternalInput"),
        ("wo_t", [MF, 128, D], F32, "ExternalInput"),
        ("rope1", [128, SEQ], F32, "ExternalInput"),
        ("masks", [5, 128, 512], BF16, "ExternalInput"),
        ("ag_in", [D, TPC], BF16, "Internal"),
        ("lng", [NCORE * D, TPC], BF16, "Internal"),
        ("a2a_in", [D, TPC], BF16, "Internal"),
        ("attg", [D, TPC], BF16, "Internal"),
        ("gtmp", [2, 2], F32, "Internal"),
    ],
    "l1f": _COMMON + [
        ("w1f", [IC1, 128, D], F32, "ExternalInput"),
        ("w3f", [IC1, 128, D], F32, "ExternalInput"),
        ("w2f", [MF, 128, HID1P], F32, "ExternalInput"),
        ("w1a", [2, 128, HID1P], F32, "ExternalInput"),
        ("w3a", [2, 128, HID1P], F32, "ExternalInput"),
        ("w2a", [IC1, 128, 256], F32, "ExternalInput"),
        ("s_in", [1, 4], F32, "Internal"),
        ("s_out", [1, 4], F32, "Internal"),
        ("th_dram", [1, 4], F32, "Internal"),
    ],
}

_PHASES = {"l0a": ph_l0a, "l0f": ph_l0f, "l1a": ph_l1a, "l1f": ph_l1f}

_IN_NAMES = {ph: [nm for nm, _, _, k in sp if k == "ExternalInput"]
             for ph, sp in _SPECS.items()}

_PERCORE = {"x_in", "rot_cos", "rot_sin", "offs", "wq_t", "wk_t", "wv_t", "gl", "w1a", "w3a", "w2a"}


def build_phase(ph):
    nc = bass.Bass("TRN2", target_bir_lowering=False)
    specs = list(_SPECS[ph]) + [("f_dram", [D, TPC], F32,
                                 "ExternalOutput" if (DBG_F or DBG_LIGHT) else "Internal")]
    if ph == "l0a" and DBG_LIGHT:
        specs += [("q3dbg", [D, TPC], BF16, "ExternalOutput")]
    if ph == "l1f" and DBG_F:
        specs += [("thdbg", [1, 4], F32, "ExternalOutput"),
                  ("hdbg", [HID1P, TPC], BF16, "ExternalOutput")]
    if ph == "l0a":
        specs += [("ar_out", [2, 2048], F32, "Internal")]
        if DBG_F:
            specs += [("adbg", [128, MF], F32, "ExternalOutput"),
                      ("kdbg", [D, TPC], BF16, "ExternalOutput"),
                      ("vdbg", [D, TPC], BF16, "ExternalOutput"),
                      ("qdbg", [D, TPC], BF16, "ExternalOutput"),
                      ("xndbg", [D, TPC], BF16, "ExternalOutput"),
                      ("abdbg", [3 * 128, TPC], F32, "ExternalOutput"),
                      ("qdbg2", [D, TPC], BF16, "ExternalOutput")]
    io = _mk_io(nc, specs)
    with ExitStack() as stack:
        C = Ctx(nc, stack)
        C.init_consts()
        _PHASES[ph](C, io)
    return nc


def run_phase(ph, H, xs):
    nc = build_phase(ph)
    maps = []
    for c in range(NCORE):
        m = {"x_in": xs[c]}
        for nm in _IN_NAMES[ph]:
            if nm == "x_in":
                continue
            v = H[nm]
            m[nm] = v[c] if nm in _PERCORE else v
        maps.append(m)
    res = run_bass_kernel_spmd(nc, maps, core_ids=list(range(NCORE)))
    if DBG_F or DBG_LIGHT:
        run_phase.f = [res.results[c]["f_dram"] for c in range(NCORE)]
        if ph == "l0a" and not DBG_F:
            run_phase.q3 = [res.results[c]["q3dbg"] for c in range(NCORE)]
    if DBG_F:
        if ph == "l1f":
            run_phase.th = [res.results[c]["thdbg"] for c in range(NCORE)]
            run_phase.h = [res.results[c]["hdbg"] for c in range(NCORE)]
        if ph == "l0a":
            run_phase.ar = [res.results[c]["adbg"] for c in range(NCORE)]
            run_phase.kv = [(res.results[c]["kdbg"], res.results[c]["vdbg"],
                             res.results[c]["qdbg"]) for c in range(NCORE)]
            run_phase.xn = [res.results[c]["xndbg"] for c in range(NCORE)]
            run_phase.ab = [res.results[c]["abdbg"] for c in range(NCORE)]
            run_phase.q2 = [res.results[c]["qdbg2"] for c in range(NCORE)]
    return [res.results[c]["x_out"] for c in range(NCORE)]


def kernel(**inputs):
    H = _prep(inputs)
    xs = H["x0"]
    for ph in ("l0a", "l0f", "l1a", "l1f"):
        xs = run_phase(ph, H, xs)
    out = np.concatenate([np.asarray(x, np.float32).T for x in xs], axis=0)
    return np.ascontiguousarray(out.reshape(2, SEQ, D))



FUSE = True

_ALL_IN = []
for _ph in ("l0a", "l0f", "l1a", "l1f"):
    for _nm in _IN_NAMES[_ph]:
        if _nm not in ("x_in",) and _nm not in _ALL_IN:
            _ALL_IN.append(_nm)


def build_fused():
    nc = bass.Bass("TRN2", target_bir_lowering=False)
    io = {}
    io["x_in"] = nc.dram_tensor("x_in", [D, TPC], F32, kind="ExternalInput")
    io["x_out"] = nc.dram_tensor("x_out", [D, TPC], F32, kind="ExternalOutput")
    seen = {"x_in", "x_out"}
    for ph in ("l0a", "l0f", "l1a", "l1f"):
        for nm, shp, dt, kind in _SPECS[ph]:
            if nm in seen or nm in ("x_in", "x_out", "f_dram"):
                continue
            seen.add(nm)
            io[nm] = nc.dram_tensor(nm, list(shp), dt, kind=kind)
    io["f_dram"] = nc.dram_tensor("f_dram", [D, TPC], F32)
    io["ar_out"] = nc.dram_tensor("ar_out", [2, 2048], F32)
    x1 = nc.dram_tensor("x1d", [D, TPC], F32)
    x2 = nc.dram_tensor("x2d", [D, TPC], F32)
    x3 = nc.dram_tensor("x3d", [D, TPC], F32)
    with ExitStack() as stack:
        C = Ctx(nc, stack)
        C.init_consts()
        ph_l0a(C, {**io, "x_in": io["x_in"], "x_out": x1})
        ph_l0f(C, {**io, "x_in": x1, "x_out": x2})
        ph_l1a(C, {**io, "x_in": x2, "x_out": x3})
        ph_l1f(C, {**io, "x_in": x3, "x_out": io["x_out"]})
    return nc


def kernel(**inputs):
    H = _prep(inputs)
    if FUSE:
        nc = build_fused()
        maps = []
        for c in range(NCORE):
            m = {"x_in": H["x0"][c]}
            for nm in _ALL_IN:
                v = H[nm]
                m[nm] = v[c] if nm in _PERCORE else v
            maps.append(m)
        res = run_bass_kernel_spmd(nc, maps, core_ids=list(range(NCORE)))
        xs = [res.results[c]["x_out"] for c in range(NCORE)]
    else:
        xs = H["x0"]
        for ph in ("l0a", "l0f", "l1a", "l1f"):
            xs = run_phase(ph, H, xs)
    out = np.concatenate([np.asarray(x, np.float32).T for x in xs], axis=0)
    return np.ascontiguousarray(out.reshape(2, SEQ, D))


# version 06 (final)
# speedup vs baseline: 1.0791x; 1.0191x over previous
import numpy as np
import ml_dtypes
from contextlib import ExitStack

import concourse.bass as bass
import concourse.mybir as mybir
from concourse.bass_utils import run_bass_kernel_spmd
from concourse.aot_env import allow_ici_all_to_all

F32 = mybir.dt.float32
BF16 = mybir.dt.bfloat16
U8 = mybir.dt.uint8
I32 = mybir.dt.int32
AF = mybir.ActivationFunctionType
ALU = mybir.AluOpType
AX = mybir.AxisListType.X
BF = ml_dtypes.bfloat16

NCORE = 8
TPC = 1024
D = 2048
MF = 16
SEQ = 4096
NOC_IN = 48
HID0 = 8192
IC0 = 64
HID1 = 5461
HID1P = 5504
IC1 = 43
EPS = 1e-5
RG = [list(range(NCORE))]
HV = [(0, 512), (512, 1024)]



def _slab3(w, pc=128, cc=128):
    K, N = w.shape
    kc, ncc = K // pc, N // cc
    return np.ascontiguousarray(
        w.reshape(kc, pc, ncc, cc).transpose(2, 1, 0, 3).reshape(ncc, pc, kc * cc))


def _prep(inputs):
    H = {}
    x = np.asarray(inputs["L0_attn_X"], np.float32).reshape(NCORE * TPC, D)
    H["x0"] = [np.ascontiguousarray(x[c * TPC:(c + 1) * TPC].T) for c in range(NCORE)]

    perm = np.concatenate([np.arange(0, 1024, 2), np.arange(1, 1024, 2), np.arange(1024, 2048)])
    W_in = np.asarray(inputs["L0_attn_W_in"], np.float32)
    b_in = np.asarray(inputs["L0_attn_b_in"], np.float32)
    W_in_p = np.concatenate([W_in[:, 0:D][:, perm], W_in[:, D:2 * D][:, perm],
                             W_in[:, 2 * D:3 * D][:, perm]], axis=1)
    b_in_p = np.concatenate([b_in[0:D][perm], b_in[D:2 * D][perm], b_in[2 * D:3 * D][perm]])
    W_out_p = np.asarray(inputs["L0_attn_W_out"], np.float32)[perm, :]
    H["w_in_t"] = _slab3(W_in_p)
    H["b_in_t"] = np.ascontiguousarray(b_in_p.reshape(NOC_IN, 128).T)
    H["w_out_t"] = _slab3(W_out_p)
    H["b_out_t"] = np.ascontiguousarray(np.asarray(inputs["L0_attn_b_out"], np.float32).reshape(MF, 128).T)
    H["gamma_t"] = np.ascontiguousarray(np.asarray(inputs["L0_attn_gamma"], np.float32).reshape(MF, 128).T)
    H["beta_t"] = np.ascontiguousarray(np.asarray(inputs["L0_attn_beta"], np.float32).reshape(MF, 128).T)
    invf = 10000.0 ** (-np.arange(0, 1024, 2, dtype=np.float64) / 1024.0)
    H["rot_cos"], H["rot_sin"] = [], []
    for c in range(NCORE):
        pos = np.arange(TPC, dtype=np.float64) + (c % 4) * TPC
        ang = invf[:, None] * pos[None, :]
        H["rot_cos"].append(np.ascontiguousarray(np.cos(ang).astype(np.float32)))
        H["rot_sin"].append(np.ascontiguousarray(np.sin(ang).astype(np.float32)))

    def u8(a):
        a = np.ascontiguousarray(np.asarray(a, np.int32))
        return np.ascontiguousarray(a.view(np.uint8).reshape(a.shape[0], a.shape[1], 4)[:, :, 0])
    H["gw_t"] = _slab3(u8(inputs["L0_ffn_gate_w"]).T)
    H["uw_t"] = _slab3(u8(inputs["L0_ffn_up_w"]).T)
    H["dw_t"] = _slab3(u8(inputs["L0_ffn_down_w"]).T)
    H["gsc_t"] = np.ascontiguousarray(np.asarray(inputs["L0_ffn_gate_scale"], np.float32).reshape(IC0, 128).T)
    H["usc_t"] = np.ascontiguousarray(np.asarray(inputs["L0_ffn_up_scale"], np.float32).reshape(IC0, 128).T)
    H["dsc_t"] = np.ascontiguousarray(np.asarray(inputs["L0_ffn_down_scale"], np.float32).reshape(MF, 128).T)

    wqs = np.asarray(inputs["L1_attn_wq_sem"], np.float32)
    wks = np.asarray(inputs["L1_attn_wk_sem"], np.float32)
    wqg = np.asarray(inputs["L1_attn_wq_geo"], np.float32)
    wkg = np.asarray(inputs["L1_attn_wk_geo"], np.float32)
    wv = np.asarray(inputs["L1_attn_wv"], np.float32)
    gl = np.asarray(inputs["L1_attn_gate_logit"], np.float32)
    H["wq_t"], H["wk_t"], H["wv_t"], H["gl"] = [], [], [], []
    for c in range(NCORE):
        hs = [2 * c, 2 * c + 1]
        q = np.concatenate([np.concatenate([wqg[:, h * 64:h * 64 + 64], wqs[:, h * 64:h * 64 + 64]], 1) for h in hs], 1)
        k = np.concatenate([np.concatenate([wkg[:, h * 64:h * 64 + 64], wks[:, h * 64:h * 64 + 64]], 1) for h in hs], 1)
        v = np.concatenate([wv[:, h * 128:h * 128 + 128] for h in hs], 1)
        H["wq_t"].append(np.ascontiguousarray(q.reshape(MF, 128, 256)))
        H["wk_t"].append(np.ascontiguousarray(k.reshape(MF, 128, 256)))
        H["wv_t"].append(np.ascontiguousarray(v.reshape(MF, 128, 256)))
        H["gl"].append(np.ascontiguousarray(gl[hs].reshape(2, 1)))
    H["wo_t"] = _slab3(np.asarray(inputs["L1_attn_wo"], np.float32))
    invg = 10000.0 ** (-np.arange(32, dtype=np.float64) / 32.0)
    ang = np.arange(SEQ, dtype=np.float64)[None, :] * invg[:, None]
    cg, sg = np.cos(ang), np.sin(ang)
    H["rope1"] = np.ascontiguousarray(np.concatenate([cg, cg, -sg, sg], 0).astype(np.float32))
    kk = np.arange(128)[:, None]
    qq = np.arange(512)[None, :]
    mk = [(kk <= qq - 128 * d_).astype(np.float32) for d_ in range(4)]
    mk.append(np.ones((128, 512), np.float32))
    H["masks"] = np.ascontiguousarray(np.stack(mk).astype(BF))

    w1T = np.zeros((D, HID1P), np.float32); w1T[:, :HID1] = np.asarray(inputs["L1_ffn_w1"], np.float32).T
    w3T = np.zeros((D, HID1P), np.float32); w3T[:, :HID1] = np.asarray(inputs["L1_ffn_w3"], np.float32).T
    w2T = np.zeros((HID1P, D), np.float32); w2T[:HID1, :] = np.asarray(inputs["L1_ffn_w2"], np.float32).T
    H["w1f"] = _slab3(w1T)
    H["w3f"] = _slab3(w3T)
    H["w2f"] = _slab3(w2T)

    H["w1a"] = [np.ascontiguousarray(w1T[256 * c:256 * c + 256].reshape(2, 128, HID1P))
                for c in range(NCORE)]
    H["w3a"] = [np.ascontiguousarray(w3T[256 * c:256 * c + 256].reshape(2, 128, HID1P))
                for c in range(NCORE)]
    H["w2a"] = [np.ascontiguousarray(w2T[:, 256 * c:256 * c + 256].reshape(IC1, 128, 256))
                for c in range(NCORE)]
    H["offs"] = [np.array([[(c // 4) * 2048, 256 * c, (1 - c // 4) * 2048, 2 * c]], np.int32)
                 for c in range(NCORE)]
    return H




_UN = [0]


def _u(nm):
    _UN[0] += 1
    return f"{nm}_{_UN[0]}"


class Prog:
    def __init__(self, nc, stack):
        self.nc, self.stack = nc, stack
        self.h, self.n = {}, {}

    def sem(self, name):
        if name not in self.h:
            self.h[name] = self.stack.enter_context(self.nc.semaphore(name))
            self.n[name] = 0
        return self.h[name]

    def inc(self, inst, name, amt=1):
        inst.then_inc(self.sem(name), amt)
        self.n[name] += amt
        return self.n[name]

    def wait(self, eng, name, val=None):
        eng.wait_ge(self.sem(name), self.n[name] if val is None else val)

    def base(self, *names):
        return {k: self.n.get(k, 0) for k in names}


class Ctx:
    def __init__(self, nc, stack):
        self.nc, self.stack = nc, stack
        self.P = Prog(nc, stack)
        e = stack.enter_context
        self.fscr = e(nc.sbuf_tensor("fscr", [1, 4], F32))
        self.dumr = e(nc.sbuf_tensor("dumr", [1, 256], BF16))
        self.scr_ps = e(nc.psum_tensor("scr_ps_g", [1, 256], F32))
        self.ones1_f32 = e(nc.sbuf_tensor("ones1_f32", [1, 128], F32))
        self.ones128_bf = e(nc.sbuf_tensor("ones128_bf", [128, 1], BF16))
        self.m128_bf = e(nc.sbuf_tensor("m128_bf", [1, 128], BF16))
        self.eps_t = e(nc.sbuf_tensor("eps_t", [1, 1], F32))

    def init_consts(self):
        nc = self.nc
        with nc.Block() as b:
            @b.vector
            def _(v):
                v.memset(self.ones1_f32[:], 1.0)
                v.memset(self.dumr[:], 0.0)
                v.memset(self.ones128_bf[:], 1.0)
                v.memset(self.m128_bf[:], -128.0)
                v.memset(self.eps_t[:], EPS)


def col(t2, i):
    return t2[i].rearrange("p -> p 1")


def slot2(buf, i, w):
    return buf[:, (i % 2) * w:(i % 2 + 1) * w]



def emit_stats(C, src_fn, a_dst, b_dst, mode="plain", apl=None):
    nc, P = C.nc, C.P
    with ExitStack() as sc:
        e = sc.enter_context
        sum_ps = e(nc.psum_tensor(_u("st_sum_ps"), [1, TPC], F32))
        sq_ps = e(nc.psum_tensor(_u("st_sq_ps"), [1, TPC], F32))
        xt = e(nc.sbuf_tensor(_u("st_xt"), [128, 2 * TPC], F32))
        xb = e(nc.sbuf_tensor(_u("st_xb"), [128, 2 * TPC], BF16))
        xq = e(nc.sbuf_tensor(_u("st_xq"), [128, 2 * TPC], BF16))
        mean = xt[0:1, 0:TPC]
        var = xt[0:1, TPC:2 * TPC]
        aa = e(nc.sbuf_tensor(_u("st_aa"), [1, TPC], F32))
        bb = e(nc.sbuf_tensor(_u("st_bb"), [1, TPC], F32))
        if mode == "l0inner":
            sA = e(nc.sbuf_tensor(_u("st_sA"), [1, TPC], F32))
            sB = e(nc.sbuf_tensor(_u("st_sB"), [1, TPC], F32))
        B = P.base("ld", "v", "s", "t")

        with nc.Block() as b:
            @b.sync
            def _(sy):
                for mf in range(MF):
                    if mf >= 2:
                        sy.wait_ge(P.sem("v"), B["v"] + (mf - 2) + 1)
                    P.inc(sy.dma_start(out=slot2(xt, mf, TPC), in_=src_fn(mf)), "ld", 16)

            @b.vector
            def _(v):
                for mf in range(MF):
                    v.wait_ge(P.sem("ld"), B["ld"] + 16 * (mf + 1))
                    if mf >= 2:
                        v.wait_ge(P.sem("s"), B["s"] + (mf - 2) + 1)
                        v.wait_ge(P.sem("t"), B["t"] + (mf - 2) + 1)
                    P.inc(v.tensor_copy(slot2(xb, mf, TPC), slot2(xt, mf, TPC)), "v")

            @b.scalar
            def _(s):
                for mf in range(MF):
                    s.wait_ge(P.sem("v"), B["v"] + mf + 1)
                    if mf >= 2:
                        s.wait_ge(P.sem("t"), B["t"] + (mf - 2) + 1)
                    P.inc(s.activation(slot2(xq, mf, TPC), slot2(xb, mf, TPC), AF.Square), "s")
                P.inc(s.activation(C.fscr[0:1, 2:3], C.fscr[0:1, 2:3], AF.Square), "s")

            @b.tensor
            def _(tn):
                for mf in range(MF):
                    tn.wait_ge(P.sem("s"), B["s"] + mf + 2)
                    st, sp = (mf == 0), (mf == MF - 1)
                    for (h0, h1) in HV:
                        tn.matmul(sum_ps[0:1, h0:h1], C.ones128_bf[:], slot2(xb, mf, TPC)[:, h0:h1],
                                  start=st, stop=sp)
                        last = tn.matmul(sq_ps[0:1, h0:h1], C.ones128_bf[:], slot2(xq, mf, TPC)[:, h0:h1],
                                         start=st, stop=sp)
                    P.inc(last, "t")
                drain(C, tn, "t")

        B = P.base("v", "s")
        if mode == "l0inner":
            with nc.Block() as b:
                @b.vector
                def _(v):
                    P.wait(v, "t")
                    vsync(P, v, v.tensor_copy(sA[:], sum_ps[:]))
                    vsync(P, v, v.tensor_copy(sB[:], sq_ps[:]))
                    vsync(P, v, v.tensor_scalar(mean[:], sA[:], 1.0 / D, None, ALU.mult))
                    vsync(P, v, v.tensor_scalar(sB[:], sB[:], 1.0 / D, None, ALU.mult))
                    vsync(P, v, v.tensor_tensor(var[:], mean[:], mean[:], ALU.mult))
                    P.inc(v.tensor_tensor(var[:], sB[:], var[:], ALU.subtract), "v")
                    v.wait_ge(P.sem("s"), B["s"] + 1)
                    vsync(P, v, v.reciprocal(aa[:], sA[:]))
                    vsync(P, v, v.tensor_copy(sA[:], aa[:]))
                    vsync(P, v, v.tensor_scalar(sB[:], var[:], EPS, None, ALU.add))
                    vsync(P, v, v.reciprocal(sB[:], sB[:]))
                    P.inc(v.tensor_tensor(var[:], var[:], sB[:], ALU.mult), "v")
                    v.wait_ge(P.sem("s"), B["s"] + 2)
                    vsync(P, v, v.reciprocal(var[:], var[:]))
                    vsync(P, v, v.tensor_tensor(aa[:], aa[:], var[:], ALU.mult))
                    P.inc(v.tensor_tensor(bb[:], mean[:], aa[:], ALU.mult), "v")

                @b.scalar
                def _(s):
                    s.wait_ge(P.sem("v"), B["v"] + 1)
                    P.inc(s.activation(sA[:], var[:], AF.Sqrt, bias=C.eps_t[:]), "s")
                    s.wait_ge(P.sem("v"), B["v"] + 2)
                    P.inc(s.activation(var[:], var[:], AF.Sqrt, bias=C.eps_t[:]), "s")
        else:
            with nc.Block() as b:
                @b.vector
                def _(v):
                    P.wait(v, "t")
                    vsync(P, v, v.tensor_copy(mean[:], sum_ps[:]))
                    vsync(P, v, v.tensor_scalar(mean[:], mean[:], 1.0 / D, None, ALU.mult))
                    vsync(P, v, v.tensor_copy(var[:], sq_ps[:]))
                    vsync(P, v, v.tensor_scalar(var[:], var[:], 1.0 / D, None, ALU.mult))
                    vsync(P, v, v.tensor_tensor(aa[:], mean[:], mean[:], ALU.mult))
                    P.inc(v.tensor_tensor(var[:], var[:], aa[:], ALU.subtract), "v")
                    v.wait_ge(P.sem("s"), B["s"] + 1)
                    vsync(P, v, v.reciprocal(var[:], aa[:]))
                    P.inc(v.tensor_tensor(bb[:], mean[:], var[:], ALU.mult), "v")

                @b.scalar
                def _(s):
                    s.wait_ge(P.sem("v"), B["v"] + 1)
                    P.inc(s.activation(aa[:], var[:], AF.Sqrt, bias=C.eps_t[:]), "s")

        if mode == "l0inner":
            if apl is not None:
                emit_bcast(C, [aa, bb, sA], [a_dst, b_dst, apl])
            else:
                emit_bcast(C, [aa, bb], [a_dst, b_dst])
        else:
            emit_bcast(C, [var, bb], [a_dst, b_dst])


def emit_bcast(C, rows, dsts):
    nc, P = C.nc, C.P
    with ExitStack() as sc:
        bc_ps = sc.enter_context(nc.psum_tensor(_u("bc_ps"), [128, TPC], F32))
        B = P.base("v", "t")
        with nc.Block() as b:
            @b.tensor
            def _(tn):
                for i, row in enumerate(rows):
                    if i >= 1:
                        tn.wait_ge(P.sem("v"), B["v"] + i)
                    for (h0, h1) in HV:
                        last = tn.matmul(bc_ps[:, h0:h1], C.ones1_f32[:], row[0:1, h0:h1],
                                         start=True, stop=True)
                    P.inc(last, "t")
                    drain(C, tn, "t")

            @b.vector
            def _(v):
                for i, dst in enumerate(dsts):
                    v.wait_ge(P.sem("t"), B["t"] + 2 * (i + 1))
                    P.inc(v.tensor_copy(dst[:], bc_ps[:]), "v")


def emit_bcast_col(C, scalar_dram_off, dsts_col):
    nc, P = C.nc, C.P
    with nc.Block() as b:
        @b.sync
        def _(sy):
            for (t, off), dst in zip(scalar_dram_off, dsts_col):
                P.inc(sy.dma_start(out=dst, in_=bass.AP(t, off, [[0, 128], [1, 1]])), "ld", 16)


def emit_apply_pre(C, x_fn, dst_big, a_bc, b_bc, gam=None, bet=None, extra_g=None):
    nc, P = C.nc, C.P
    with ExitStack() as sc:
        e = sc.enter_context
        xt = e(nc.sbuf_tensor(_u("ap_xt"), [128, 2 * TPC], F32))
        t1 = e(nc.sbuf_tensor(_u("ap_t1"), [128, 2 * TPC], F32))
        B = P.base("ld", "v", "s")
        VPM = 2

        with nc.Block() as b:
            @b.sync
            def _(sy):
                for mf in range(MF):
                    if mf >= 2:
                        sy.wait_ge(P.sem("v"), B["v"] + VPM * (mf - 2) + 1)
                    P.inc(sy.dma_start(out=slot2(xt, mf, TPC), in_=x_fn(mf)), "ld", 16)

            @b.vector
            def _(v):
                for mf in range(MF):
                    v.wait_ge(P.sem("ld"), B["ld"] + 16 * (mf + 1))
                    if gam is not None and mf >= 2:
                        v.wait_ge(P.sem("s"), B["s"] + (mf - 2) + 1)
                    P.inc(v.tensor_tensor(slot2(t1, mf, TPC), slot2(xt, mf, TPC), a_bc[:], ALU.mult), "v")
                    if gam is None:
                        dst = dst_big[:, mf * TPC:(mf + 1) * TPC]
                        P.inc(v.tensor_tensor(dst, slot2(t1, mf, TPC), b_bc[:], ALU.subtract), "v")
                    else:
                        P.inc(v.tensor_tensor(slot2(t1, mf, TPC), slot2(t1, mf, TPC), b_bc[:], ALU.subtract), "v")

            @b.scalar
            def _(s):
                if gam is None:
                    return
                for mf in range(MF):
                    s.wait_ge(P.sem("v"), B["v"] + VPM * (mf + 1))
                    dst = dst_big[:, mf * TPC:(mf + 1) * TPC]
                    P.inc(s.activation(dst, slot2(t1, mf, TPC), AF.Identity,
                                       scale=gam[:, mf:mf + 1], bias=bet[:, mf:mf + 1]), "s")
                P.inc(s.activation(C.fscr[0:1, 2:3], C.fscr[0:1, 2:3], AF.Square), "s")

            if extra_g is not None:
                @b.gpsimd
                def _(g):
                    for mf in range(MF):
                        if gam is None:
                            g.wait_ge(P.sem("v"), B["v"] + VPM * (mf + 1))
                        else:
                            g.wait_ge(P.sem("s"), B["s"] + mf + 1)
                        extra_g(g, mf)


def emit_apply_post(C, f_fn, x_fn, x_out, a_bc, b_bc):
    nc, P = C.nc, C.P
    with ExitStack() as sc:
        e = sc.enter_context
        ft = e(nc.sbuf_tensor(_u("po_ft"), [128, 2 * TPC], F32))
        xt = e(nc.sbuf_tensor(_u("po_xt"), [128, 2 * TPC], F32))
        xo = e(nc.sbuf_tensor(_u("po_xo"), [128, 2 * TPC], F32))
        B = P.base("ld", "v", "g")
        VPM = 3

        with nc.Block() as b:
            @b.sync
            def _(sy):
                for mf in range(MF):
                    if mf >= 2:
                        sy.wait_ge(P.sem("v"), B["v"] + VPM * (mf - 2) + 3)
                    P.inc(sy.dma_start(out=slot2(ft, mf, TPC), in_=f_fn(mf)), "ld", 16)
                    P.inc(sy.dma_start(out=slot2(xt, mf, TPC), in_=x_fn(mf)), "ld", 16)

            @b.vector
            def _(v):
                for mf in range(MF):
                    v.wait_ge(P.sem("ld"), B["ld"] + 32 * (mf + 1))
                    if mf >= 2:
                        v.wait_ge(P.sem("g"), B["g"] + 16 * (mf - 1))
                    v.tensor_tensor(slot2(ft, mf, TPC), slot2(ft, mf, TPC), a_bc[:], ALU.mult)
                    v.tensor_tensor(slot2(ft, mf, TPC), slot2(ft, mf, TPC), b_bc[:], ALU.subtract)
                    P.inc(v.tensor_tensor(slot2(xo, mf, TPC), slot2(xt, mf, TPC), slot2(ft, mf, TPC),
                                          ALU.add), "v", VPM)

            @b.gpsimd
            def _(g):
                for mf in range(MF):
                    g.wait_ge(P.sem("v"), B["v"] + VPM * (mf + 1))
                    P.inc(g.dma_start(out=x_out[mf * 128:(mf + 1) * 128, :],
                                      in_=slot2(xo, mf, TPC)), "g", 16)


def drain(C, tn, sem):
    return C.P.inc(tn.matmul(C.scr_ps[0:1, 0:256], C.dumr[0:1, 0:1], C.dumr[0:1, 0:256],
                             start=True, stop=True), sem)


def vsync(P, eng, inst):
    t = P.inc(inst, "vx")
    eng.wait_ge(P.sem("vx"), t)


def fence(C, sems=("ld", "g", "cc")):
    nc, P = C.nc, C.P
    with nc.Block() as b:
        @b.vector
        def _(v):
            P.inc(v.memset(C.fscr[0:1, 0:1], 0.0), "fzv")
            P.wait(v, "fzv")

        @b.scalar
        def _(s):
            for nm in sems:
                if nm in P.n and P.n[nm] > 0:
                    P.wait(s, nm)
            P.inc(s.activation(C.fscr[0:1, 1:2], C.fscr[0:1, 1:2], AF.Abs), "fzs")
            P.wait(s, "fzs")



def ph_l0a(C, io):
    nc, P = C.nc, C.P
    x_in, x_out, f_dram = io["x_in"], io["x_out"], io["f_dram"]

    def xsrc(mf):
        return x_in[mf * 128:(mf + 1) * 128, :]

    with ExitStack() as sc:
        e = sc.enter_context
        qb = e(nc.sbuf_tensor("qb", [128, MF * TPC], BF16))
        kb = e(nc.sbuf_tensor("kb", [128, MF * TPC], BF16))
        vb = e(nc.sbuf_tensor("vb", [128, MF * TPC], BF16))
        gam = e(nc.sbuf_tensor("gam", [128, MF], F32))
        bet = e(nc.sbuf_tensor("bet", [128, MF], F32))
        bo = e(nc.sbuf_tensor("bo", [128, MF], F32))
        bin_sb = e(nc.sbuf_tensor("bin_sb", [128, NOC_IN], F32))
        offs_sb = e(nc.sbuf_tensor("offs_sb", [1, 4], I32))
        a0 = e(nc.sbuf_tensor("a0", [128, TPC], F32))
        b0 = e(nc.sbuf_tensor("b0", [128, TPC], F32))
        apl = e(nc.sbuf_tensor("apl", [128, TPC], F32))
        af = e(nc.sbuf_tensor("af_l0a", [128, TPC], F32))
        bfc = e(nc.sbuf_tensor("bf_l0a", [128, TPC], F32))
        A_sb = e(nc.sbuf_tensor("A_sb", [128, MF], F32))
        z128 = e(nc.sbuf_tensor("z128", [128, MF], F32))
        rb = e(nc.gpsimd.register("rb"))
        rb2 = e(nc.gpsimd.register("rb2"))

        def tile(buf, mf):
            return buf[:, mf * TPC:(mf + 1) * TPC]

        with nc.Block() as b:
            @b.sync
            def _(sy):
                P.inc(sy.dma_start(out=gam[:], in_=io["gamma_t"][:]), "ld", 16)
                P.inc(sy.dma_start(out=bet[:], in_=io["beta_t"][:]), "ld", 16)
                P.inc(sy.dma_start(out=bo[:], in_=io["b_out_t"][:]), "ld", 16)
                P.inc(sy.dma_start(out=bin_sb[:], in_=io["b_in_t"][:]), "ld", 16)
                P.inc(sy.dma_start(out=offs_sb[:], in_=io["offs"][:]), "ld", 16)

            @b.vector
            def _(v):
                v.memset(z128[:], 0.0)
        fence(C)

        with ExitStack() as sxn:
            Xn = sxn.enter_context(nc.sbuf_tensor("Xn", [128, MF * TPC], BF16))
            emit_stats(C, xsrc, a0, b0, mode="l0inner", apl=apl)
            fence(C)
            if "abdbg" in io:
                with nc.Block() as b:
                    @b.gpsimd
                    def _(g):
                        P.inc(g.dma_start(out=io["abdbg"][0:128, :], in_=a0[:]), "g", 16)
                        P.inc(g.dma_start(out=io["abdbg"][128:256, :], in_=b0[:]), "g", 16)
                        P.inc(g.dma_start(out=io["abdbg"][256:384, :], in_=apl[:]), "g", 16)
                fence(C, sems=("g",))
            emit_apply_pre(C, xsrc, Xn, a0, b0, gam=gam, bet=bet)
            fence(C)
            if "xndbg" in io:
                with nc.Block() as b:
                    @b.gpsimd
                    def _(g):
                        for mf in range(MF):
                            P.inc(g.dma_start(out=io["xndbg"][mf * 128:(mf + 1) * 128, :],
                                              in_=Xn[:, mf * TPC:(mf + 1) * TPC]), "g", 16)
                fence(C, sems=("g",))

            with ExitStack() as s4:
                e4 = s4.enter_context
                wsl = e4(nc.sbuf_tensor("wsl", [128, 2 * D], F32))
                wbf = e4(nc.sbuf_tensor("wbf", [128, 2 * D], BF16))
                ps = [e4(nc.psum_tensor(f"qkv_ps{i}", [128, TPC], F32)) for i in range(2)]
                B = P.base("ld", "cst", "mm", "mmd", "ev")
                with nc.Block() as b:
                    @b.sync
                    def _(sy):
                        for oc in range(NOC_IN):
                            if oc >= 2:
                                sy.wait_ge(P.sem("cst"), B["cst"] + oc - 1)
                            P.inc(sy.dma_start(out=slot2(wsl, oc, D), in_=io["w_in_t"][oc]), "ld", 16)

                    @b.vector
                    def _(v):
                        for oc in range(NOC_IN):
                            v.wait_ge(P.sem("ld"), B["ld"] + 16 * (oc + 1))
                            if oc >= 2:
                                v.wait_ge(P.sem("mm"), B["mm"] + oc - 1)
                            P.inc(v.tensor_copy(slot2(wbf, oc, D), slot2(wsl, oc, D)), "cst")
                            if oc >= 1:
                                po = oc - 1
                                v.wait_ge(P.sem("mmd"), B["mmd"] + po + 1)
                                dst = (qb if po < 16 else (kb if po < 32 else vb))
                                P.inc(v.tensor_scalar(tile(dst, po % 16), ps[po % 2][:],
                                                      bin_sb[:, po:po + 1], None, ALU.add), "ev")
                        po = NOC_IN - 1
                        v.wait_ge(P.sem("mmd"), B["mmd"] + po + 1)
                        P.inc(v.tensor_scalar(tile(vb, po % 16), ps[po % 2][:],
                                              bin_sb[:, po:po + 1], None, ALU.add), "ev")

                    @b.tensor
                    def _(tn):
                        P.wait(tn, "s")
                        for oc in range(NOC_IN):
                            tn.wait_ge(P.sem("cst"), B["cst"] + oc + 1)
                            if oc >= 2:
                                tn.wait_ge(P.sem("ev"), B["ev"] + oc - 1)
                            for mf in range(MF):
                                for (h0, h1) in HV:
                                    last = tn.matmul(ps[oc % 2][:, h0:h1],
                                                     slot2(wbf, oc, D)[:, mf * 128:(mf + 1) * 128],
                                                     tile(Xn, mf)[:, h0:h1],
                                                     start=(mf == 0), stop=(mf == MF - 1))
                            P.inc(last, "mm")
                            drain(C, tn, "mmd")
                fence(C)

        with ExitStack() as s5:
            e5 = s5.enter_context
            rcs = e5(nc.sbuf_tensor("rcs", [128, 8 * TPC], F32))
            u1 = e5(nc.sbuf_tensor("u1", [128, TPC], F32))
            u2 = e5(nc.sbuf_tensor("u2", [128, TPC], F32))
            u3 = e5(nc.sbuf_tensor("u3", [128, TPC], F32))
            u4 = e5(nc.sbuf_tensor("u4", [128, TPC], F32))
            with nc.Block() as b:
                @b.sync
                def _(sy):
                    for t in range(4):
                        P.inc(sy.dma_start(out=rcs[:, t * TPC:(t + 1) * TPC],
                                           in_=io["rot_cos"][t * 128:(t + 1) * 128, :]), "ld", 16)
                        P.inc(sy.dma_start(out=rcs[:, (4 + t) * TPC:(5 + t) * TPC],
                                           in_=io["rot_sin"][t * 128:(t + 1) * 128, :]), "ld", 16)
            fence(C)
            with nc.Block() as b:
                @b.vector
                def _(v):
                    for buf in (kb, vb):
                        for t in range(4):
                            cosT = rcs[:, t * TPC:(t + 1) * TPC]
                            sinT = rcs[:, (4 + t) * TPC:(5 + t) * TPC]
                            v.tensor_tensor(u1[:], tile(buf, t), cosT, ALU.mult)
                            v.tensor_tensor(u2[:], tile(buf, t + 4), sinT, ALU.mult)
                            v.tensor_tensor(u3[:], tile(buf, t), sinT, ALU.mult)
                            v.tensor_tensor(u4[:], tile(buf, t + 4), cosT, ALU.mult)
                            v.tensor_tensor(tile(buf, t), u1[:], u2[:], ALU.subtract)
                            inst = v.tensor_tensor(tile(buf, t + 4), u3[:], u4[:], ALU.add)
                    P.inc(inst, "v")
            fence(C)

        if "kdbg" in io:
            with nc.Block() as b:
                @b.gpsimd
                def _(g):
                    for mf in range(MF):
                        P.inc(g.dma_start(out=io["kdbg"][mf * 128:(mf + 1) * 128, :],
                                          in_=tile(kb, mf)), "g", 16)
                        P.inc(g.dma_start(out=io["vdbg"][mf * 128:(mf + 1) * 128, :],
                                          in_=tile(vb, mf)), "g", 16)
                        P.inc(g.dma_start(out=io["qdbg"][mf * 128:(mf + 1) * 128, :],
                                          in_=tile(qb, mf)), "g", 16)
            fence(C, sems=("g",))

        with ExitStack() as s6:
            e6 = s6.enter_context
            sqk = e6(nc.sbuf_tensor("sqk", [128, 2 * TPC], BF16))
            sqv = e6(nc.sbuf_tensor("sqv", [128, 2 * TPC], BF16))
            nk_ps = e6(nc.psum_tensor("nk_ps", [1, TPC], F32))
            nv_ps = e6(nc.psum_tensor("nv_ps", [1, TPC], F32))
            nk_sb = e6(nc.sbuf_tensor("nk_sb", [1, TPC], F32))
            nv_sb = e6(nc.sbuf_tensor("nv_sb", [1, TPC], F32))
            W_bc = e6(nc.sbuf_tensor("W_bc", [128, TPC], F32))
            pk = e6(nc.sbuf_tensor("pk", [128, TPC], BF16))
            pw = e6(nc.sbuf_tensor("pw", [128, TPC], F32))
            apart = e6(nc.sbuf_tensor("apart", [128, MF], F32))
            B = P.base("sq", "nmm")
            with nc.Block() as b:
                @b.scalar
                def _(s):
                    P.wait(s, "v")
                    for mf in range(MF):
                        if mf >= 2:
                            s.wait_ge(P.sem("nmm"), B["nmm"] + mf - 1)
                        s.activation(slot2(sqk, mf, TPC), tile(kb, mf), AF.Square)
                        P.inc(s.activation(slot2(sqv, mf, TPC), tile(vb, mf), AF.Square), "sq")
                    P.inc(s.activation(C.fscr[0:1, 2:3], C.fscr[0:1, 2:3], AF.Square), "sq")

                @b.tensor
                def _(tn):
                    for mf in range(MF):
                        tn.wait_ge(P.sem("sq"), B["sq"] + mf + 2)
                        st, sp = (mf == 0), (mf == MF - 1)
                        for (h0, h1) in HV:
                            tn.matmul(nk_ps[0:1, h0:h1], C.ones128_bf[:],
                                      slot2(sqk, mf, TPC)[:, h0:h1], start=st, stop=sp)
                            last = tn.matmul(nv_ps[0:1, h0:h1], C.ones128_bf[:],
                                             slot2(sqv, mf, TPC)[:, h0:h1], start=st, stop=sp)
                        P.inc(last, "nmm")
                    drain(C, tn, "nmm")
            with nc.Block() as b:
                @b.vector
                def _(v):
                    P.wait(v, "nmm")
                    vsync(P, v, v.tensor_copy(nk_sb[:], nk_ps[:]))
                    vsync(P, v, v.tensor_copy(nv_sb[:], nv_ps[:]))
                    v.tensor_tensor(nk_sb[:], nk_sb[:], nv_sb[:], ALU.mult)
            with nc.Block() as b:
                @b.scalar
                def _(s):
                    s.activation(nv_sb[:], nk_sb[:], AF.Sqrt)
            with nc.Block() as b:
                @b.vector
                def _(v):
                    v.reciprocal(nk_sb[:], nv_sb[:])
            emit_bcast(C, [nk_sb], [W_bc])
            with nc.Block() as b:
                @b.vector
                def _(v):
                    for mf in range(MF):
                        v.tensor_tensor(pk[:], tile(kb, mf), tile(vb, mf), ALU.mult)
                        v.tensor_tensor(pw[:], pk[:], W_bc[:], ALU.mult)
                        inst = v.reduce_sum(apart[:, mf:mf + 1], pw[:], axis=AX)
                    P.inc(inst, "v")
            fence(C)
            with nc.Block() as b:
                @b.gpsimd
                def _(g):
                    P.wait(g, "v")
                    g.reg_load(rb, offs_sb[0:1, 0:1])
                    g.reg_load(rb2, offs_sb[0:1, 2:3])
                    P.inc(g.dma_start(out=bass.AP(io["ar_in"], rb, [[16, 128], [1, 16]]),
                                      in_=apart[:]), "g", 16)
                    P.inc(g.dma_start(out=bass.AP(io["ar_in"], rb2, [[16, 128], [1, 16]]),
                                      in_=z128[:]), "g", 16)
                    P.wait(g, "g")
                    P.inc(g.collective_compute(
                        "AllReduce", ALU.add, replica_groups=RG,
                        ins=[io["ar_in"][:]], outs=[io["ar_out"][:]]), "cc")
                    P.wait(g, "cc")
                    P.inc(g.dma_start(out=A_sb[:], in_=bass.AP(io["ar_out"], rb, [[16, 128], [1, 16]])),
                          "g", 16)
            fence(C)
            if "adbg" in io:
                with nc.Block() as b:
                    @b.gpsimd
                    def _(g):
                        P.inc(g.dma_start(out=io["adbg"][:], in_=A_sb[:]), "g", 16)
                fence(C, sems=("g",))
            with nc.Block() as b:
                @b.vector
                def _(v):
                    v.tensor_scalar(A_sb[:], A_sb[:], 0.5, 0.5, ALU.mult, ALU.add)
            with nc.Block() as b:
                @b.vector
                def _(v):
                    for mf in range(MF):
                        inst = v.tensor_scalar(tile(qb, mf), tile(qb, mf), A_sb[:, mf:mf + 1],
                                               None, ALU.mult)
                    P.inc(inst, "v")
            fence(C, sems=())
            if "qdbg2" in io:
                with nc.Block() as b:
                    @b.gpsimd
                    def _(g):
                        for mf in range(MF):
                            P.inc(g.dma_start(out=io["qdbg2"][mf * 128:(mf + 1) * 128, :],
                                              in_=tile(qb, mf)), "g", 16)
                fence(C, sems=("g",))

        with ExitStack() as s8:
            e8 = s8.enter_context
            wsl = e8(nc.sbuf_tensor("wsl8", [128, 2 * D], F32))
            wbf = e8(nc.sbuf_tensor("wbf8", [128, 2 * D], BF16))
            xt = e8(nc.sbuf_tensor("xt8", [128, 2 * TPC], F32))
            fo = e8(nc.sbuf_tensor("fo8", [128, 2 * TPC], F32))
            xr = e8(nc.sbuf_tensor("xr8", [128, TPC], F32))
            fs = e8(nc.sbuf_tensor("fs8", [128, 2 * TPC], F32))
            ghost = e8(nc.psum_tensor("ghost8", [128, TPC], F32))
            ps = [e8(nc.psum_tensor(f"wo_ps{i}", [128, TPC], F32)) for i in range(2)]
            B8l = P.base("ld")["ld"]
            B8g = P.base("g")["g"]
            with nc.Block() as b:
                @b.sync
                def _(sy):
                    P.inc(sy.dma_start(out=slot2(wsl, 0, D), in_=io["w_out_t"][0]), "ld", 16)
                    P.inc(sy.dma_start(out=slot2(xt, 0, TPC), in_=xsrc(0)), "ld", 16)
            for oc in range(MF):
                with nc.Block() as b:
                    @b.sync
                    def _(sy, oc=oc):
                        if oc + 1 < MF:
                            P.inc(sy.dma_start(out=slot2(wsl, oc + 1, D),
                                               in_=io["w_out_t"][oc + 1]), "ld", 16)
                            P.inc(sy.dma_start(out=slot2(xt, oc + 1, TPC),
                                               in_=xsrc(oc + 1)), "ld", 16)
                with nc.Block() as b:
                    @b.vector
                    def _(v, oc=oc):
                        v.wait_ge(P.sem("ld"), B8l + 32 * (oc + 1))
                        P.inc(v.tensor_copy(slot2(wbf, oc, D), slot2(wsl, oc, D)), "cst")
                with nc.Block() as b:
                    @b.tensor
                    def _(tn, oc=oc):
                        P.wait(tn, "cst")
                        P.wait(tn, "v")
                        for qf in range(MF):
                            for (h0, h1) in HV:
                                last = tn.matmul(ps[oc % 2][:, h0:h1],
                                                 slot2(wbf, oc, D)[:, qf * 128:(qf + 1) * 128],
                                                 tile(qb, qf)[:, h0:h1],
                                                 start=(qf == 0), stop=(qf == MF - 1))
                        P.inc(last, "mm")
                with nc.Block() as b:
                    @b.scalar
                    def _(s, oc=oc):
                        P.wait(s, "mm")
                        P.inc(s.activation(slot2(fo, oc, TPC), ps[oc % 2][:], AF.Identity,
                                           bias=bo[:, oc:oc + 1]), "evs")
                with nc.Block() as b:
                    @b.vector
                    def _(v, oc=oc):
                        P.wait(v, "evs")
                        if oc >= 2:
                            v.wait_ge(P.sem("g"), B8g + 16 * (oc - 1))
                        v.tensor_tensor(xr[:], slot2(xt, oc, TPC), apl[:], ALU.mult)
                        P.inc(v.tensor_tensor(slot2(fs, oc, TPC), slot2(fo, oc, TPC), xr[:],
                                              ALU.add), "evv")
                with nc.Block() as b:
                    @b.gpsimd
                    def _(g, oc=oc):
                        P.wait(g, "evv")
                        P.inc(g.dma_start(out=f_dram[oc * 128:(oc + 1) * 128, :],
                                          in_=slot2(fs, oc, TPC)), "g", 16)
            fence(C)

        emit_stats(C, lambda mf: f_dram[mf * 128:(mf + 1) * 128, :], af, bfc)
        fence(C)
        emit_apply_post(C, lambda mf: f_dram[mf * 128:(mf + 1) * 128, :], xsrc, x_out, af, bfc)
        fence(C)
        if "q3dbg" in io:
            with nc.Block() as b:
                @b.gpsimd
                def _(g):
                    for mf in range(MF):
                        P.inc(g.dma_start(out=io["q3dbg"][mf * 128:(mf + 1) * 128, :],
                                          in_=tile(qb, mf)), "g", 16)
                    P.wait(g, "g")



def ph_l0f(C, io):
    nc, P = C.nc, C.P
    x_in, x_out, f_dram = io["x_in"], io["x_out"], io["f_dram"]

    def xsrc(mf):
        return x_in[mf * 128:(mf + 1) * 128, :]

    with ExitStack() as sc:
        e = sc.enter_context
        gsc = e(nc.sbuf_tensor("gsc", [128, IC0], F32))
        usc = e(nc.sbuf_tensor("usc", [128, IC0], F32))
        dsc = e(nc.sbuf_tensor("dsc", [128, MF], F32))
        sch = sc.enter_context(ExitStack())
        h_sb = sch.enter_context(nc.sbuf_tensor("h_sb", [128, IC0 * TPC], BF16))
        hi_bf = sch.enter_context(nc.sbuf_tensor("hi_bf", [1, TPC], BF16))
        lo_bf = sch.enter_context(nc.sbuf_tensor("lo_bf", [1, TPC], BF16))

        def htile(ic):
            return h_sb[:, ic * TPC:(ic + 1) * TPC]

        with nc.Block() as b:
            @b.sync
            def _(sy):
                P.inc(sy.dma_start(out=gsc[:], in_=io["gsc_t"][:]), "ld", 16)
                P.inc(sy.dma_start(out=usc[:], in_=io["usc_t"][:]), "ld", 16)
                P.inc(sy.dma_start(out=dsc[:], in_=io["dsc_t"][:]), "ld", 16)

        with ExitStack() as s12:
            ln1 = s12.enter_context(nc.sbuf_tensor("ln1", [128, MF * TPC], BF16))
            tsum_bf = s12.enter_context(nc.sbuf_tensor("tsum_bf", [1, TPC], BF16))

            def ltile(mf):
                return ln1[:, mf * TPC:(mf + 1) * TPC]

            with ExitStack() as sab:
                a1 = sab.enter_context(nc.sbuf_tensor("a1", [128, TPC], F32))
                b1 = sab.enter_context(nc.sbuf_tensor("b1", [128, TPC], F32))
                emit_stats(C, xsrc, a1, b1)
                fence(C)
                emit_apply_pre(C, xsrc, ln1, a1, b1)
                fence(C)

            with ExitStack() as s3:
                ts_ps = s3.enter_context(nc.psum_tensor("ts_ps", [1, TPC], F32))
                tsum_f = s3.enter_context(nc.sbuf_tensor("tsum_f", [1, TPC], F32))
                with nc.Block() as b:
                    @b.tensor
                    def _(tn):
                        for mf in range(MF):
                            for (h0, h1) in HV:
                                last = tn.matmul(ts_ps[0:1, h0:h1], C.ones128_bf[:],
                                                 ltile(mf)[:, h0:h1],
                                                 start=(mf == 0), stop=(mf == MF - 1))
                        P.inc(last, "t")
                        drain(C, tn, "t")
                with nc.Block() as b:
                    @b.vector
                    def _(v):
                        P.wait(v, "t")
                        vsync(P, v, v.tensor_copy(tsum_f[:], ts_ps[:]))
                        v.tensor_copy(tsum_bf[:], tsum_f[:])
                fence(C, sems=())

            with ExitStack() as s4:
                e4 = s4.enter_context
                g8 = e4(nc.sbuf_tensor("g8", [128, 2 * D], U8))
                u8b = e4(nc.sbuf_tensor("u8b", [128, 2 * D], U8))
                gbf = e4(nc.sbuf_tensor("gbf", [128, 2 * D], BF16))
                ubf = e4(nc.sbuf_tensor("ubf", [128, 2 * D], BF16))
                sl = e4(nc.sbuf_tensor("sl", [128, 2 * TPC], BF16))
                ul = e4(nc.sbuf_tensor("ul", [128, 2 * TPC], BF16))
                ps_g = e4(nc.psum_tensor("ps_g", [128, TPC], F32))
                ps_u = e4(nc.psum_tensor("ps_u", [128, TPC], F32))
                B = P.base("ld", "cst", "mm", "mmd", "evs", "evv")
                with nc.Block() as b:
                    @b.sync
                    def _(sy):
                        for ic in range(IC0):
                            if ic >= 2:
                                sy.wait_ge(P.sem("cst"), B["cst"] + 2 * (ic - 2) + 2)
                            P.inc(sy.dma_start(out=slot2(g8, ic, D), in_=io["gw_t"][ic]), "ld", 16)
                            P.inc(sy.dma_start(out=slot2(u8b, ic, D), in_=io["uw_t"][ic]), "ld", 16)

                    @b.vector
                    def _(v):
                        for ic in range(IC0):
                            v.wait_ge(P.sem("ld"), B["ld"] + 32 * (ic + 1))
                            if ic >= 2:
                                v.wait_ge(P.sem("mm"), B["mm"] + ic - 1)
                            v.tensor_copy(slot2(gbf, ic, D), slot2(g8, ic, D))
                            P.inc(v.tensor_copy(slot2(ubf, ic, D), slot2(u8b, ic, D)), "cst", 2)
                            if ic >= 1:
                                po = ic - 1
                                v.wait_ge(P.sem("mmd"), B["mmd"] + po + 1)
                                v.tensor_scalar(slot2(ul, po, TPC), ps_u[:], usc[:, po:po + 1],
                                                None, ALU.mult)
                                v.wait_ge(P.sem("evs"), B["evs"] + po + 1)
                                P.inc(v.tensor_tensor(htile(po), slot2(sl, po, TPC),
                                                      slot2(ul, po, TPC), ALU.mult), "evv")
                        po = IC0 - 1
                        v.wait_ge(P.sem("mmd"), B["mmd"] + po + 1)
                        v.tensor_scalar(slot2(ul, po, TPC), ps_u[:], usc[:, po:po + 1], None, ALU.mult)
                        v.wait_ge(P.sem("evs"), B["evs"] + po + 1)
                        P.inc(v.tensor_tensor(htile(po), slot2(sl, po, TPC), slot2(ul, po, TPC),
                                              ALU.mult), "evv")

                    @b.scalar
                    def _(s):
                        for ic in range(IC0):
                            s.wait_ge(P.sem("mmd"), B["mmd"] + ic + 1)
                            if ic >= 2:
                                s.wait_ge(P.sem("evv"), B["evv"] + ic - 1)
                            P.inc(s.activation(slot2(sl, ic, TPC), ps_g[:], AF.Silu,
                                               scale=gsc[:, ic:ic + 1]), "evs")

                    @b.tensor
                    def _(tn):
                        for ic in range(IC0):
                            tn.wait_ge(P.sem("cst"), B["cst"] + 2 * (ic + 1))
                            if ic >= 1:
                                tn.wait_ge(P.sem("evs"), B["evs"] + ic)
                                tn.wait_ge(P.sem("evv"), B["evv"] + ic)
                            for (h0, h1) in HV:
                                for hc in range(MF):
                                    tn.matmul(ps_g[:, h0:h1],
                                              slot2(gbf, ic, D)[:, hc * 128:(hc + 1) * 128],
                                              ltile(hc)[:, h0:h1], start=(hc == 0), stop=False)
                                tn.matmul(ps_g[:, h0:h1], C.m128_bf[:], tsum_bf[0:1, h0:h1],
                                          start=False, stop=True)
                                for hc in range(MF):
                                    tn.matmul(ps_u[:, h0:h1],
                                              slot2(ubf, ic, D)[:, hc * 128:(hc + 1) * 128],
                                              ltile(hc)[:, h0:h1], start=(hc == 0), stop=False)
                                last = tn.matmul(ps_u[:, h0:h1], C.m128_bf[:], tsum_bf[0:1, h0:h1],
                                                 start=False, stop=True)
                            P.inc(last, "mm")
                            drain(C, tn, "mmd")
                fence(C, sems=())

        with ExitStack() as s5:
            hs_ps = s5.enter_context(nc.psum_tensor("hs_ps", [1, TPC], F32))
            hs_f = s5.enter_context(nc.sbuf_tensor("hs_f", [1, TPC], F32))
            lo_f = s5.enter_context(nc.sbuf_tensor("lo_f", [1, TPC], F32))
            with nc.Block() as b:
                @b.tensor
                def _(tn):
                    for ic in range(IC0):
                        for (h0, h1) in HV:
                            last = tn.matmul(hs_ps[0:1, h0:h1], C.ones128_bf[:],
                                             h_sb[:, ic * TPC + h0:ic * TPC + h1],
                                             start=(ic == 0), stop=(ic == IC0 - 1))
                    P.inc(last, "t")
                    drain(C, tn, "t")
            with nc.Block() as b:
                @b.vector
                def _(v):
                    P.wait(v, "t")
                    vsync(P, v, v.tensor_copy(hs_f[:], hs_ps[:]))
                    vsync(P, v, v.tensor_copy(hi_bf[:], hs_f[:]))
                    vsync(P, v, v.tensor_tensor(lo_f[:], hs_f[:], hi_bf[:], ALU.subtract))
                    v.tensor_copy(lo_bf[:], lo_f[:])
            fence(C, sems=())

        with ExitStack() as s6:
            e6 = s6.enter_context
            d8 = e6(nc.sbuf_tensor("d8", [128, 2 * HID0], U8))
            dbf = e6(nc.sbuf_tensor("dbf", [128, 2 * HID0], BF16))
            fo = e6(nc.sbuf_tensor("fo_l0f", [128, 2 * TPC], F32))
            ps_d = [e6(nc.psum_tensor(f"ps_d{i}", [128, TPC], F32)) for i in range(2)]
            BDl = P.base("ld")["ld"]
            BDg = P.base("g")["g"]
            with nc.Block() as b:
                @b.sync
                def _(sy):
                    P.inc(sy.dma_start(out=slot2(d8, 0, HID0), in_=io["dw_t"][0]), "ld", 16)
            for dc in range(MF):
                with nc.Block() as b:
                    @b.sync
                    def _(sy, dc=dc):
                        if dc + 1 < MF:
                            P.inc(sy.dma_start(out=slot2(d8, dc + 1, HID0),
                                               in_=io["dw_t"][dc + 1]), "ld", 16)
                with nc.Block() as b:
                    @b.vector
                    def _(v, dc=dc):
                        v.wait_ge(P.sem("ld"), BDl + 16 * (dc + 1))
                        P.inc(v.tensor_copy(slot2(dbf, dc, HID0)[:, 0:HID0 // 2],
                                            slot2(d8, dc, HID0)[:, 0:HID0 // 2]), "cst")

                    @b.scalar
                    def _(s, dc=dc):
                        s.wait_ge(P.sem("ld"), BDl + 16 * (dc + 1))
                        P.inc(s.activation(slot2(dbf, dc, HID0)[:, HID0 // 2:HID0],
                                           slot2(d8, dc, HID0)[:, HID0 // 2:HID0], AF.Identity),
                              "cs2")
                with nc.Block() as b:
                    @b.tensor
                    def _(tn, dc=dc):
                        P.wait(tn, "cst")
                        P.wait(tn, "cs2")
                        for (h0, h1) in HV:
                            for ic in range(IC0):
                                tn.matmul(ps_d[dc % 2][:, h0:h1],
                                          slot2(dbf, dc, HID0)[:, ic * 128:(ic + 1) * 128],
                                          h_sb[:, ic * TPC + h0:ic * TPC + h1],
                                          start=(ic == 0), stop=False)
                            tn.matmul(ps_d[dc % 2][:, h0:h1], C.m128_bf[:], hi_bf[0:1, h0:h1],
                                      start=False, stop=False)
                            last = tn.matmul(ps_d[dc % 2][:, h0:h1], C.m128_bf[:], lo_bf[0:1, h0:h1],
                                             start=False, stop=True)
                        P.inc(last, "mm")
                with nc.Block() as b:
                    @b.scalar
                    def _(s, dc=dc):
                        P.wait(s, "mm")
                        if dc >= 2:
                            s.wait_ge(P.sem("g"), BDg + 16 * (dc - 1))
                        P.inc(s.activation(slot2(fo, dc, TPC), ps_d[dc % 2][:], AF.Identity,
                                           scale=dsc[:, dc:dc + 1]), "evs")
                with nc.Block() as b:
                    @b.gpsimd
                    def _(g, dc=dc):
                        P.wait(g, "evs")
                        P.inc(g.dma_start(out=f_dram[dc * 128:(dc + 1) * 128, :],
                                          in_=slot2(fo, dc, TPC)), "g", 16)
            fence(C)

        sch.close()
        af = e(nc.sbuf_tensor("af_l0f", [128, TPC], F32))
        bfc = e(nc.sbuf_tensor("bf_l0f", [128, TPC], F32))
        emit_stats(C, lambda mf: f_dram[mf * 128:(mf + 1) * 128, :], af, bfc)
        fence(C)
        emit_apply_post(C, lambda mf: f_dram[mf * 128:(mf + 1) * 128, :], xsrc, x_out, af, bfc)
        fence(C)


def ph_l1a(C, io):
    nc, P = C.nc, C.P
    x_in, x_out, f_dram = io["x_in"], io["x_out"], io["f_dram"]
    lng, ag_in = io["lng"], io["ag_in"]
    a2a_in, attg = io["a2a_in"], io["attg"]

    def xsrc(mf):
        return x_in[mf * 128:(mf + 1) * 128, :]

    with ExitStack() as sc:
        e = sc.enter_context
        qcat = e(nc.sbuf_tensor("qcat", [128, 2 * 8192], BF16))
        kcat = e(nc.sbuf_tensor("kcat", [128, 2 * 8192], BF16))
        vtm = e(nc.sbuf_tensor("vtm", [128, 64 * 256], BF16))
        gatesc = e(nc.sbuf_tensor("gatesc", [128, 2], F32))
        sig_sb = e(nc.sbuf_tensor("sig_sb", [128, 4], F32))
        msk_sb = e(nc.sbuf_tensor("msk_sb", [128, 5 * 512], BF16))
        rope_sb = e(nc.sbuf_tensor("rope_sb", [64, SEQ], F32))
        rope2_sb = e(nc.sbuf_tensor("rope2_sb", [64, SEQ], F32))
        gl_sb = e(nc.sbuf_tensor("gl_sb", [2, 1], F32))
        gv_sb = e(nc.sbuf_tensor("gv_sb", [2, 2], F32))

        def qc(hl):
            return qcat[:, hl * 8192:(hl + 1) * 8192]

        def kc(hl):
            return kcat[:, hl * 8192:(hl + 1) * 8192]

        with ExitStack() as s12:
            ln2 = s12.enter_context(nc.sbuf_tensor("ln2", [128, MF * TPC], BF16))
            a2 = s12.enter_context(nc.sbuf_tensor("a2", [128, TPC], F32))
            b2 = s12.enter_context(nc.sbuf_tensor("b2", [128, TPC], F32))
            emit_stats(C, xsrc, a2, b2)
            fence(C)

            def eg(g, mf):
                P.inc(g.dma_start(out=ag_in[mf * 128:(mf + 1) * 128, :],
                                  in_=ln2[:, mf * TPC:(mf + 1) * TPC]), "g", 16)
            emit_apply_pre(C, xsrc, ln2, a2, b2, extra_g=eg)
        with nc.Block() as b:
            @b.sync
            def _(sy):
                P.inc(sy.dma_start(out=gl_sb[:], in_=io["gl"][:]), "ld", 16)
                P.inc(sy.dma_start(out=rope_sb[:], in_=io["rope1"][0:64, :]), "ld", 16)
                P.inc(sy.dma_start(out=rope2_sb[:], in_=io["rope1"][64:128, :]), "ld", 16)
                for d_ in range(5):
                    P.inc(sy.dma_start(out=msk_sb[:, d_ * 512:(d_ + 1) * 512],
                                       in_=io["masks"][d_]), "ld", 16)

            @b.gpsimd
            def _(g):
                P.wait(g, "g")
                P.inc(g.collective_compute("AllGather", ALU.bypass, replica_groups=RG,
                                           ins=[ag_in[:]], outs=[lng[:]]), "cc")
        fence(C, sems=("ld",))
        with nc.Block() as b:
            @b.scalar
            def _(s):
                P.wait(s, "ld")
                P.inc(s.activation(gl_sb[:], gl_sb[:], AF.Sigmoid), "s")
        with nc.Block() as b:
            @b.vector
            def _(v):
                P.wait(v, "s")
                v.tensor_scalar(gv_sb[:, 0:1], gl_sb[:], -0.125, 0.125, ALU.mult, ALU.add)
                P.inc(v.tensor_scalar(gv_sb[:, 1:2], gl_sb[:], 0.125, None, ALU.mult), "v")
        with nc.Block() as b:
            @b.gpsimd
            def _(g):
                P.wait(g, "v")
                P.inc(g.dma_start(out=io["gtmp"][:], in_=gv_sb[:]), "g", 16)
        fence(C, sems=("g",))
        with nc.Block() as b:
            @b.sync
            def _(sy):
                for hl in range(2):
                    P.inc(sy.dma_start(out=gatesc[0:64, hl:hl + 1],
                                       in_=bass.AP(io["gtmp"], 2 * hl, [[0, 64], [1, 1]])), "ld", 16)
                    P.inc(sy.dma_start(out=gatesc[64:128, hl:hl + 1],
                                       in_=bass.AP(io["gtmp"], 2 * hl + 1, [[0, 64], [1, 1]])), "ld", 16)
        fence(C)

        with ExitStack() as s4:
            wqb = s4.enter_context(nc.sbuf_tensor("wqb", [128, MF * 256], BF16))
            wkb = s4.enter_context(nc.sbuf_tensor("wkb", [128, MF * 256], BF16))
            wvb = s4.enter_context(nc.sbuf_tensor("wvb", [128, MF * 256], BF16))
            with ExitStack() as s4f:
                wf = s4f.enter_context(nc.sbuf_tensor("wf_stage", [128, MF * 256], F32))
                for (wt, wb) in ((io["wq_t"], wqb), (io["wk_t"], wkb), (io["wv_t"], wvb)):
                    with nc.Block() as b:
                        @b.sync
                        def _(sy, wt=wt):
                            for m in range(MF):
                                P.inc(sy.dma_start(out=wf[:, m * 256:(m + 1) * 256],
                                                   in_=wt[m]), "ld", 16)
                    fence(C, sems=("ld",))
                    with nc.Block() as b:
                        @b.vector
                        def _(v, wb=wb):
                            P.inc(v.tensor_copy(wb[:], wf[:]), "v")
                    fence(C, sems=())

            with ExitStack() as s5:
                e5 = s5.enter_context
                lng_sb = e5(nc.sbuf_tensor("lng_sb", [128, 2 * MF * 512], BF16))

                def lsl(ts, mf):
                    return lng_sb[:, (ts % 2) * MF * 512 + mf * 512:(ts % 2) * MF * 512 + (mf + 1) * 512]
                swp = e5(nc.sbuf_tensor("swp", [64, 4 * 512], BF16))
                rp1 = e5(nc.sbuf_tensor("rp1", [64, 512], F32))
                rp2 = e5(nc.sbuf_tensor("rp2", [64, 512], F32))
                ps_q = e5(nc.psum_tensor("ps_q", [128, 2 * 512], F32))
                ps_k = e5(nc.psum_tensor("ps_k", [128, 2 * 512], F32))
                ps_v = e5(nc.psum_tensor("ps_v", [128, 2 * 512], F32))
                B = P.base("ld", "mm", "mmd", "vev", "swp", "rop")
                with nc.Block() as b:
                    @b.sync
                    def _(sy):
                        for ts in range(16):
                            if ts >= 2:
                                sy.wait_ge(P.sem("mm"), B["mm"] + ts - 1)
                            r0 = (ts // 2) * 2048
                            c0 = (ts % 2) * 512
                            for mf in range(MF):
                                P.inc(sy.dma_start(
                                    out=lsl(ts, mf),
                                    in_=lng[r0 + mf * 128:r0 + (mf + 1) * 128, c0:c0 + 512]),
                                    "ld", 16)

                    @b.tensor
                    def _(tn):
                        P.wait(tn, "v")
                        P.wait(tn, "s")
                        for ts in range(16):
                            tn.wait_ge(P.sem("ld"), B["ld"] + 256 * (ts + 1))
                            if ts >= 1:
                                tn.wait_ge(P.sem("vev"), B["vev"] + 5 * ts)
                            for hl in range(2):
                                for (wb, pp) in ((wqb, ps_q), (wkb, ps_k)):
                                    for mf in range(MF):
                                        last = tn.matmul(
                                            pp[:, hl * 512:(hl + 1) * 512],
                                            wb[:, mf * 256 + hl * 128:mf * 256 + (hl + 1) * 128],
                                            lsl(ts, mf),
                                            start=(mf == 0), stop=(mf == MF - 1))
                            for tc in range(4):
                                for mf in range(MF):
                                    last = tn.matmul(
                                        ps_v[:, (tc % 2) * 256 + (tc // 2) * 512:
                                             (tc % 2) * 256 + (tc // 2) * 512 + 256],
                                        lsl(ts, mf)[:, tc * 128:(tc + 1) * 128],
                                        wvb[:, mf * 256:(mf + 1) * 256],
                                        start=(mf == 0), stop=(mf == MF - 1))
                            P.inc(last, "mm")
                            drain(C, tn, "mmd")

                    @b.vector
                    def _(v):
                        for ts in range(16):
                            v.wait_ge(P.sem("mmd"), B["mmd"] + ts + 1)
                            span = slice(ts * 512, (ts + 1) * 512)
                            for hl in range(2):
                                P.inc(v.tensor_scalar(qc(hl)[:, span], ps_q[:, hl * 512:(hl + 1) * 512],
                                                      gatesc[:, hl:hl + 1], None, ALU.mult), "vev")
                                P.inc(v.tensor_copy(kc(hl)[:, span], ps_k[:, hl * 512:(hl + 1) * 512]),
                                      "vev")
                            P.inc(v.tensor_copy(vtm[:, ts * 1024:(ts + 1) * 1024], ps_v[:]), "vev")
                            v.wait_ge(P.sem("swp"), B["swp"] + 128 * (ts + 1))
                            for gi, (buf, hl) in enumerate(((qcat, 0), (qcat, 1), (kcat, 0), (kcat, 1))):
                                geo = (qc(hl) if buf is qcat else kc(hl))[0:64, span]
                                v.tensor_tensor(rp1[:], geo, rope_sb[:, (ts % 8) * 512:(ts % 8 + 1) * 512],
                                                ALU.mult)
                                v.tensor_tensor(rp2[:], swp[:, gi * 512:(gi + 1) * 512],
                                                rope2_sb[:, (ts % 8) * 512:(ts % 8 + 1) * 512],
                                                ALU.mult)
                                P.inc(v.tensor_tensor(geo, rp1[:], rp2[:], ALU.add), "rop")

                    @b.gpsimd
                    def _(g):
                        for ts in range(16):
                            g.wait_ge(P.sem("vev"), B["vev"] + 5 * ts + 4)
                            if ts >= 1:
                                g.wait_ge(P.sem("rop"), B["rop"] + 4 * ts)
                            span = slice(ts * 512, (ts + 1) * 512)
                            for gi, (buf_hl) in enumerate(((0, 0), (0, 1), (1, 0), (1, 1))):
                                isq, hl = buf_hl
                                src = (qc(hl) if isq == 0 else kc(hl))
                                P.inc(g.dma_start(out=swp[0:32, gi * 512:gi * 512 + 512],
                                                  in_=src[32:64, span]), "swp", 16)
                                P.inc(g.dma_start(out=swp[32:64, gi * 512:gi * 512 + 512],
                                                  in_=src[0:32, span]), "swp", 16)
                fence(C)

        with ExitStack() as s6:
            e6 = s6.enter_context
            sel = e6(nc.sbuf_tensor("sel", [128, 2], BF16))
            sqq = e6(nc.sbuf_tensor("sqq", [128, SEQ], BF16))
            ps_n = e6(nc.psum_tensor("ps_n", [2, 512], F32))
            nsb = e6(nc.sbuf_tensor("nsb", [2, 512], F32))
            rmx = e6(nc.sbuf_tensor("rmx", [2, 8], F32))
            rq = e6(nc.sbuf_tensor("rq", [2, 1], F32))
            rk = e6(nc.sbuf_tensor("rk", [2, 1], F32))
            ps_d1 = e6(nc.psum_tensor("sig_d1", [1, 1], F32))
            ps_bc = e6(nc.psum_tensor("ps_bc", [128, 1], F32))
            sgn = e6(nc.sbuf_tensor("sgn", [1, 1], F32))
            with nc.Block() as b:
                @b.vector
                def _(v):
                    v.memset(sel[:, 0:2], 0.0)
                    v.memset(sel[0:64, 0:1], 1.0)
                    P.inc(v.memset(sel[64:128, 1:2], 1.0), "v")
            for hl in range(2):
                for bb_ in range(2):
                    for (srcbuf, rdst) in ((qc(hl), rq), (kc(hl), rk)):
                        with nc.Block() as b:
                            @b.scalar
                            def _(s, srcbuf=srcbuf, bb_=bb_):
                                P.inc(s.activation(sqq[:], srcbuf[:, bb_ * 4096:(bb_ + 1) * 4096],
                                                   AF.Square), "s")
                                P.inc(s.activation(C.fscr[0:1, 2:3], C.fscr[0:1, 2:3], AF.Square), "s")
                        for ch in range(8):
                            with nc.Block() as b:
                                @b.tensor
                                def _(tn, ch=ch):
                                    P.wait(tn, "v")
                                    P.wait(tn, "s")
                                    P.inc(tn.matmul(ps_n[:], sel[:], sqq[:, ch * 512:(ch + 1) * 512],
                                                    start=True, stop=True), "t")
                                    drain(C, tn, "t")
                            with nc.Block() as b:
                                @b.vector
                                def _(v, ch=ch):
                                    P.wait(v, "t")
                                    vsync(P, v, v.tensor_copy(nsb[:], ps_n[:]))
                                    v.reduce_max(rmx[:, ch:ch + 1], nsb[:], axis=AX)
                        with nc.Block() as b:
                            @b.vector
                            def _(v, rdst=rdst):
                                P.inc(v.reduce_max(rdst[:], rmx[:], axis=AX), "v")
                        with nc.Block() as b:
                            @b.scalar
                            def _(s, rdst=rdst):
                                P.wait(s, "v")
                                P.inc(s.activation(rdst[:], rdst[:], AF.Sqrt), "s")
                    with nc.Block() as b:
                        @b.tensor
                        def _(tn):
                            P.wait(tn, "s")
                            P.inc(tn.matmul(ps_d1[:], rq[:], rk[:], start=True, stop=True), "t")
                            drain(C, tn, "t")
                    with nc.Block() as b:
                        @b.vector
                        def _(v):
                            P.wait(v, "t")
                            vsync(P, v, v.tensor_copy(sgn[:], ps_d1[:]))
                            P.inc(v.tensor_scalar(sgn[:], sgn[:], -1.0, None, ALU.mult), "v")
                    with nc.Block() as b:
                        @b.tensor
                        def _(tn):
                            P.wait(tn, "v")
                            P.inc(tn.matmul(ps_bc[:], C.ones1_f32[:], sgn[:], start=True, stop=True),
                                  "t")
                            drain(C, tn, "t")
                    with nc.Block() as b:
                        @b.vector
                        def _(v, hl=hl, bb_=bb_):
                            P.wait(v, "t")
                            P.inc(v.tensor_copy(sig_sb[:, bb_ * 2 + hl:bb_ * 2 + hl + 1], ps_bc[:]),
                                  "v")
            fence(C, sems=())

        with ExitStack() as s7:
            e7 = s7.enter_context
            ps_sc = [e7(nc.psum_tensor(f"ps_sc{i}", [128, 512], F32)) for i in range(3)]
            ps_av = e7(nc.psum_tensor("ps_av", [128, 512], F32))
            ps_sum = e7(nc.psum_tensor("ps_sum", [1, 512], F32))
            ps_rb = e7(nc.psum_tensor("ps_rb", [128, 512], F32))
            p_t = e7(nc.sbuf_tensor("p_t", [128, 3 * 512], BF16))
            att_t = e7(nc.sbuf_tensor("att_t", [128, 512], BF16))
            sumf = e7(nc.sbuf_tensor("sumf", [1, 512], F32))
            sumr = e7(nc.sbuf_tensor("sumr", [1, 512], F32))
            rbf = e7(nc.sbuf_tensor("rbf", [128, 512], F32))
            for bb_ in range(2):
                for hl in range(2):
                    for sm in range(8):
                        nj = 4 * sm + 4
                        gs = bb_ * 8 + sm
                        qspan = slice(bb_ * 4096 + sm * 512, bb_ * 4096 + (sm + 1) * 512)
                        B = P.base("tsc", "tav", "tsm", "exp", "msk", "vfi", "trb", "g")

                        def kcol(j):
                            return bb_ * 4096 + j * 128

                        def vcol(j):
                            return (32 * bb_ + j) * 256 + hl * 128

                        def pts(j):
                            return p_t[:, (j % 3) * 512:(j % 3 + 1) * 512]

                        with nc.Block() as blk:
                            @blk.tensor
                            def _(tn, hl=hl, bb_=bb_, nj=nj, qspan=qspan, B=B,
                                  kcol=kcol, vcol=vcol, pts=pts):
                                P.wait(tn, "rop")
                                P.wait(tn, "vev")
                                for j in range(nj):
                                    if j >= 3:
                                        tn.wait_ge(P.sem("exp"), B["exp"] + j - 2)
                                    P.inc(tn.matmul(ps_sc[j % 3][:],
                                                    kc(hl)[:, kcol(j):kcol(j) + 128],
                                                    qc(hl)[:, qspan],
                                                    start=True, stop=True), "tsc")
                                    if j == nj - 1:
                                        drain(C, tn, "tsc")
                                    if j >= 2:
                                        jj = j - 2
                                        tn.wait_ge(P.sem("msk"), B["msk"] + jj + 1)
                                        P.inc(tn.matmul(ps_av[:], vtm[:, vcol(jj):vcol(jj) + 128],
                                                        pts(jj), start=(jj == 0), stop=False), "tav")
                                        P.inc(tn.matmul(ps_sum[:], C.ones128_bf[:],
                                                        pts(jj), start=(jj == 0), stop=False), "tsm")
                                for jj in (nj - 2, nj - 1):
                                    tn.wait_ge(P.sem("msk"), B["msk"] + jj + 1)
                                    P.inc(tn.matmul(ps_av[:], vtm[:, vcol(jj):vcol(jj) + 128],
                                                    pts(jj), start=(jj == 0), stop=(jj == nj - 1)),
                                          "tav")
                                    P.inc(tn.matmul(ps_sum[:], C.ones128_bf[:],
                                                    pts(jj), start=(jj == 0), stop=(jj == nj - 1)),
                                          "tsm")
                                drain(C, tn, "tsm")
                                tn.wait_ge(P.sem("vfi"), B["vfi"] + 1)
                                P.inc(tn.matmul(ps_rb[:], C.ones1_f32[:], sumr[:],
                                                start=True, stop=True), "trb")
                                drain(C, tn, "trb")

                            @blk.scalar
                            def _(s, hl=hl, bb_=bb_, nj=nj, B=B, pts=pts):
                                P.wait(s, "v")
                                for j in range(nj):
                                    s.wait_ge(P.sem("tsc"), B["tsc"] + min(j + 2, nj + 1))
                                    if j >= 3:
                                        s.wait_ge(P.sem("tav"), B["tav"] + j - 2)
                                    P.inc(s.activation(pts(j), ps_sc[j % 3][:], AF.Exp,
                                                       bias=sig_sb[:, bb_ * 2 + hl:bb_ * 2 + hl + 1]),
                                          "exp")
                                P.inc(s.activation(C.fscr[0:1, 0:4], C.fscr[0:1, 0:4], AF.Square),
                                      "exp")

                            @blk.vector
                            def _(v, nj=nj, B=B, pts=pts):
                                for j in range(nj):
                                    d_ = j - (nj - 4) if j >= nj - 4 else 4
                                    v.wait_ge(P.sem("exp"), B["exp"] + min(j + 2, nj + 1))
                                    if j >= 3:
                                        v.wait_ge(P.sem("tav"), B["tav"] + j - 2)
                                    P.inc(v.tensor_tensor(pts(j), pts(j),
                                                          msk_sb[:, d_ * 512:(d_ + 1) * 512],
                                                          ALU.mult), "msk")
                                v.wait_ge(P.sem("tsm"), B["tsm"] + nj + 1)
                                vsync(P, v, v.tensor_copy(sumf[:], ps_sum[:]))
                                P.inc(v.reciprocal(sumr[:], sumf[:]), "vfi")
                                v.wait_ge(P.sem("trb"), B["trb"] + 2)
                                v.tensor_copy(rbf[:], ps_rb[:])
                                P.wait(v, "g")
                                P.inc(v.tensor_tensor(att_t[:], ps_av[:], rbf[:], ALU.mult), "vfi")

                            @blk.gpsimd
                            def _(g, hl=hl, gs=gs, B=B):
                                g.wait_ge(P.sem("vfi"), B["vfi"] + 2)
                                dd, half = gs // 2, gs % 2
                                P.inc(g.dma_start(
                                    out=a2a_in[dd * 256 + hl * 128:dd * 256 + (hl + 1) * 128,
                                               half * 512:(half + 1) * 512],
                                    in_=att_t[:]), "g", 16)
            fence(C)

        with nc.Block() as b:
            @b.gpsimd
            def _(g):
                with allow_ici_all_to_all():
                    P.inc(g.collective_compute("AllToAll", ALU.bypass, replica_groups=RG,
                                               ins=[a2a_in[:]], outs=[attg[:]]), "cc")
        fence(C)

        with ExitStack() as s9:
            e9 = s9.enter_context
            att_sb = e9(nc.sbuf_tensor("att_sb", [128, MF * TPC], BF16))
            wsl = e9(nc.sbuf_tensor("wsl9", [128, 2 * D], F32))
            wbf = e9(nc.sbuf_tensor("wbf9", [128, 2 * D], BF16))
            fo = e9(nc.sbuf_tensor("fo9", [128, 2 * TPC], F32))
            ps = [e9(nc.psum_tensor(f"wo9_ps{i}", [128, TPC], F32)) for i in range(2)]
            with nc.Block() as b:
                @b.sync
                def _(sy):
                    for mf in range(MF):
                        P.inc(sy.dma_start(out=att_sb[:, mf * TPC:(mf + 1) * TPC],
                                           in_=attg[mf * 128:(mf + 1) * 128, :]), "ld", 16)
            fence(C, sems=("ld",))
            BWl = P.base("ld")["ld"]
            BWg = P.base("g")["g"]
            with nc.Block() as b:
                @b.sync
                def _(sy):
                    P.inc(sy.dma_start(out=slot2(wsl, 0, D), in_=io["wo_t"][0]), "ld", 16)
            for oc in range(MF):
                with nc.Block() as b:
                    @b.sync
                    def _(sy, oc=oc):
                        if oc + 1 < MF:
                            P.inc(sy.dma_start(out=slot2(wsl, oc + 1, D),
                                               in_=io["wo_t"][oc + 1]), "ld", 16)
                with nc.Block() as b:
                    @b.vector
                    def _(v, oc=oc):
                        v.wait_ge(P.sem("ld"), BWl + 16 * (oc + 1))
                        P.inc(v.tensor_copy(slot2(wbf, oc, D), slot2(wsl, oc, D)), "cst")
                with nc.Block() as b:
                    @b.tensor
                    def _(tn, oc=oc):
                        P.wait(tn, "cst")
                        P.wait(tn, "v")
                        for hv in range(MF):
                            for (h0, h1) in HV:
                                last = tn.matmul(ps[oc % 2][:, h0:h1],
                                                 slot2(wbf, oc, D)[:, hv * 128:(hv + 1) * 128],
                                                 att_sb[:, hv * TPC + h0:hv * TPC + h1],
                                                 start=(hv == 0), stop=(hv == MF - 1))
                        P.inc(last, "mm")
                with nc.Block() as b:
                    @b.scalar
                    def _(s, oc=oc):
                        P.wait(s, "mm")
                        if oc >= 2:
                            s.wait_ge(P.sem("g"), BWg + 16 * (oc - 1))
                        P.inc(s.activation(slot2(fo, oc, TPC), ps[oc % 2][:], AF.Identity), "evs")
                with nc.Block() as b:
                    @b.gpsimd
                    def _(g, oc=oc):
                        P.wait(g, "evs")
                        P.inc(g.dma_start(out=f_dram[oc * 128:(oc + 1) * 128, :],
                                          in_=slot2(fo, oc, TPC)), "g", 16)
            fence(C)

        af = e(nc.sbuf_tensor("af_l1a", [128, TPC], F32))
        bfc = e(nc.sbuf_tensor("bf_l1a", [128, TPC], F32))
        emit_stats(C, lambda mf: f_dram[mf * 128:(mf + 1) * 128, :], af, bfc)
        fence(C)
        emit_apply_post(C, lambda mf: f_dram[mf * 128:(mf + 1) * 128, :], xsrc, x_out, af, bfc)
        fence(C)


def ph_l1f(C, io):
    nc, P = C.nc, C.P
    x_in, x_out, f_dram = io["x_in"], io["x_out"], io["f_dram"]

    def xsrc(mf):
        return x_in[mf * 128:(mf + 1) * 128, :]

    with ExitStack() as sc:
        e = sc.enter_context
        h_sb = e(nc.sbuf_tensor("h1_sb", [128, IC1 * TPC], BF16))
        thp = e(nc.sbuf_tensor("thp", [128, 3], F32))
        thm = e(nc.sbuf_tensor("thm", [128, 3], F32))
        s23 = e(nc.sbuf_tensor("s23", [128, 1], F32))

        def htile(ic):
            return h_sb[:, ic * TPC:(ic + 1) * TPC]

        with ExitStack() as s0:
            e0 = s0.enter_context
            wst = e0(nc.sbuf_tensor("q_wst", [128, 2 * 5504], F32))
            acc = e0(nc.sbuf_tensor("q_acc", [128, 48], F32))
            accb = e0(nc.sbuf_tensor("q_accb", [128, 48], BF16))
            accb2 = e0(nc.sbuf_tensor("q_accb2", [128, 48], BF16))
            abs_ps = e0(nc.psum_tensor("abs_ps", [1, 48], F32))
            part = e0(nc.sbuf_tensor("q_part", [1, 4], F32))
            ssb = e0(nc.sbuf_tensor("q_ssb", [1, 4], F32))
            jobs = ([("w1a", dcl, HID1P) for dcl in range(2)]
                    + [("w3a", dcl, HID1P) for dcl in range(2)]
                    + [("w2a", icp, 256) for icp in range(IC1)])
            NJOB = len(jobs)
            B = P.base("g", "qab", "qmm")
            with nc.Block() as b:
                @b.gpsimd
                def _(g):
                    for i, (nm, idx, wid) in enumerate(jobs):
                        if i >= 2:
                            g.wait_ge(P.sem("qab"), B["qab"] + i - 1)
                        P.inc(g.dma_start(out=slot2(wst, i, 5504)[:, 0:wid],
                                          in_=io[nm][idx]), "g", 16)

                @b.vector
                def _(v):
                    for i, (nm, idx, wid) in enumerate(jobs):
                        v.wait_ge(P.sem("g"), B["g"] + 16 * (i + 1))
                        P.inc(v.reduce_sum(acc[:, i:i + 1], slot2(wst, i, 5504)[:, 0:wid],
                                           axis=AX, apply_absolute_value=True), "qab")
                    vsync(P, v, v.tensor_copy(accb[:], acc[:]))
                    vsync(P, v, v.tensor_tensor(acc[:], acc[:], accb[:], ALU.subtract))
                    P.inc(v.tensor_copy(accb2[:], acc[:]), "qab")
            with nc.Block() as b:
                @b.tensor
                def _(tn):
                    P.wait(tn, "qab")
                    tn.matmul(abs_ps[0:1, 0:NJOB], C.ones128_bf[:], accb[:, 0:NJOB],
                              start=True, stop=False)
                    P.inc(tn.matmul(abs_ps[0:1, 0:NJOB], C.ones128_bf[:], accb2[:, 0:NJOB],
                                    start=False, stop=True), "qmm")
                    drain(C, tn, "qmm")
            with nc.Block() as b:
                @b.vector
                def _(v):
                    P.wait(v, "qmm")
                    vsync(P, v, v.memset(part[:], 0.0))
                    v.reduce_sum(part[0:1, 0:1], abs_ps[0:1, 0:2], axis=AX)
                    v.reduce_sum(part[0:1, 1:2], abs_ps[0:1, 2:4], axis=AX)
                    P.inc(v.reduce_sum(part[0:1, 2:3], abs_ps[0:1, 4:4 + IC1], axis=AX), "v")
            fence(C, sems=())
            with nc.Block() as b:
                @b.gpsimd
                def _(g):
                    P.wait(g, "v")
                    P.inc(g.dma_start(out=io["s_in"][:], in_=part[:]), "g", 16)
                    P.wait(g, "g")
                    P.inc(g.collective_compute("AllReduce", ALU.add, replica_groups=RG,
                                               ins=[io["s_in"][:]], outs=[io["s_out"][:]]), "cc")
                    P.wait(g, "cc")
                    P.inc(g.dma_start(out=ssb[:], in_=io["s_out"][:]), "g", 16)
            fence(C)
            with nc.Block() as b:
                @b.vector
                def _(v):
                    vsync(P, v, v.tensor_scalar(ssb[:], ssb[:], 0.5 / float(HID1 * D), None, ALU.mult))
                    P.inc(v.tensor_tensor(ssb[0:1, 3:4], ssb[0:1, 1:2], ssb[0:1, 2:3], ALU.mult),
                          "v")
            with nc.Block() as b:
                @b.gpsimd
                def _(g):
                    P.wait(g, "v")
                    P.inc(g.dma_start(out=io["th_dram"][:], in_=ssb[:]), "g", 16)
                    if "thdbg" in io:
                        P.inc(g.dma_start(out=io["thdbg"][:], in_=ssb[:]), "g", 16)
            fence(C, sems=("g",))
            with nc.Block() as b:
                @b.sync
                def _(sy):
                    for w in range(3):
                        P.inc(sy.dma_start(out=thp[:, w:w + 1],
                                           in_=bass.AP(io["th_dram"], w, [[0, 128], [1, 1]])), "ld", 16)
                    P.inc(sy.dma_start(out=s23[:],
                                       in_=bass.AP(io["th_dram"], 3, [[0, 128], [1, 1]])), "ld", 16)
            fence(C, sems=("ld",))
            with nc.Block() as b:
                @b.vector
                def _(v):
                    P.inc(v.tensor_scalar(thm[:], thp[:], -1.0, None, ALU.mult), "v")
            fence(C, sems=())

        with ExitStack() as s12:
            ln3 = s12.enter_context(nc.sbuf_tensor("ln3", [128, MF * TPC], BF16))
            a3 = s12.enter_context(nc.sbuf_tensor("a3", [128, TPC], F32))
            b3 = s12.enter_context(nc.sbuf_tensor("b3", [128, TPC], F32))

            def ltile(mf):
                return ln3[:, mf * TPC:(mf + 1) * TPC]

            emit_stats(C, xsrc, a3, b3)
            fence(C)
            emit_apply_pre(C, xsrc, ln3, a3, b3)
            fence(C)

            with ExitStack() as s3:
                e3 = s3.enter_context
                w1s = e3(nc.sbuf_tensor("w1s_sl", [128, 2 * D], F32))
                w3s = e3(nc.sbuf_tensor("w3s_sl", [128, 2 * D], F32))
                t1b = e3(nc.sbuf_tensor("t1b", [128, 2 * D], BF16))
                t3b = e3(nc.sbuf_tensor("t3b", [128, 2 * D], BF16))
                sgA = e3(nc.sbuf_tensor("sgA", [128, D], BF16))
                sgB = e3(nc.sbuf_tensor("sgB", [128, D], BF16))
                sl = e3(nc.sbuf_tensor("sl_l1f", [128, 2 * TPC], BF16))
                ul = e3(nc.sbuf_tensor("ul_l1f", [128, 2 * TPC], BF16))
                ps_1 = e3(nc.psum_tensor("ps_1", [128, TPC], F32))
                ps_3 = e3(nc.psum_tensor("ps_3", [128, TPC], F32))
                scr_ps = e3(nc.psum_tensor("scr_ps_l1f3", [1, 128], F32))
                B = P.base("ld", "sg", "qtt", "mm", "mmd", "evs", "evv")
                with nc.Block() as b:
                    @b.sync
                    def _(sy):
                        for ic in range(IC1):
                            if ic >= 2:
                                sy.wait_ge(P.sem("sg"), B["sg"] + 2 * (ic - 2) + 2)
                            P.inc(sy.dma_start(out=slot2(w1s, ic, D), in_=io["w1f"][ic]), "ld", 16)
                            P.inc(sy.dma_start(out=slot2(w3s, ic, D), in_=io["w3f"][ic]), "ld", 16)

                    @b.scalar
                    def _(s):
                        P.wait(s, "v")
                        for ic in range(IC1):
                            s.wait_ge(P.sem("ld"), B["ld"] + 32 * (ic + 1))
                            if ic >= 1:
                                s.wait_ge(P.sem("qtt"), B["qtt"] + 2 * ic)
                            s.activation(sgA[:], slot2(w1s, ic, D), AF.Sign, bias=thm[:, 0:1])
                            P.inc(s.activation(sgB[:], slot2(w1s, ic, D), AF.Sign, bias=thp[:, 0:1]),
                                  "sg")
                            s.wait_ge(P.sem("qtt"), B["qtt"] + 2 * ic + 1)
                            s.activation(sgA[:], slot2(w3s, ic, D), AF.Sign, bias=thm[:, 1:2])
                            P.inc(s.activation(sgB[:], slot2(w3s, ic, D), AF.Sign, bias=thp[:, 1:2]),
                                  "sg")
                            if ic >= 1:
                                po = ic - 1
                                s.wait_ge(P.sem("mmd"), B["mmd"] + po + 1)
                                if po >= 2:
                                    s.wait_ge(P.sem("evv"), B["evv"] + po - 1)
                                P.inc(s.activation(slot2(sl, po, TPC), ps_1[:], AF.Silu,
                                                   scale=thp[:, 0:1]), "evs")
                        po = IC1 - 1
                        s.wait_ge(P.sem("mmd"), B["mmd"] + po + 1)
                        s.wait_ge(P.sem("evv"), B["evv"] + po - 1)
                        P.inc(s.activation(slot2(sl, po, TPC), ps_1[:], AF.Silu,
                                           scale=thp[:, 0:1]), "evs")

                    @b.vector
                    def _(v):
                        for ic in range(IC1):
                            v.wait_ge(P.sem("sg"), B["sg"] + 2 * ic + 1)
                            if ic >= 2:
                                v.wait_ge(P.sem("mm"), B["mm"] + ic - 1)
                            P.inc(v.tensor_tensor(slot2(t1b, ic, D), sgA[:], sgB[:], ALU.add), "qtt")
                            v.wait_ge(P.sem("sg"), B["sg"] + 2 * ic + 2)
                            P.inc(v.tensor_tensor(slot2(t3b, ic, D), sgA[:], sgB[:], ALU.add), "qtt")
                            if ic >= 1:
                                po = ic - 1
                                v.wait_ge(P.sem("mmd"), B["mmd"] + po + 1)
                                v.tensor_copy(slot2(ul, po, TPC), ps_3[:])
                                v.wait_ge(P.sem("evs"), B["evs"] + po + 1)
                                P.inc(v.tensor_tensor(htile(po), slot2(sl, po, TPC),
                                                      slot2(ul, po, TPC), ALU.mult), "evv")
                        po = IC1 - 1
                        v.wait_ge(P.sem("mmd"), B["mmd"] + po + 1)
                        v.tensor_copy(slot2(ul, po, TPC), ps_3[:])
                        v.wait_ge(P.sem("evs"), B["evs"] + po + 1)
                        P.inc(v.tensor_tensor(htile(po), slot2(sl, po, TPC), slot2(ul, po, TPC),
                                              ALU.mult), "evv")

                    @b.tensor
                    def _(tn):
                        for ic in range(IC1):
                            tn.wait_ge(P.sem("qtt"), B["qtt"] + 2 * ic + 2)
                            if ic >= 1:
                                tn.wait_ge(P.sem("evs"), B["evs"] + ic)
                                tn.wait_ge(P.sem("evv"), B["evv"] + ic)
                            for (h0, h1) in HV:
                                for dc in range(MF):
                                    tn.matmul(ps_1[:, h0:h1],
                                              slot2(t1b, ic, D)[:, dc * 128:(dc + 1) * 128],
                                              ltile(dc)[:, h0:h1], start=(dc == 0), stop=(dc == MF - 1))
                                for dc in range(MF):
                                    last = tn.matmul(ps_3[:, h0:h1],
                                                     slot2(t3b, ic, D)[:, dc * 128:(dc + 1) * 128],
                                                     ltile(dc)[:, h0:h1], start=(dc == 0),
                                                     stop=(dc == MF - 1))
                            P.inc(last, "mm")
                            P.inc(tn.matmul(scr_ps[:], C.ones128_bf[:],
                                            slot2(t1b, ic, D)[:, 0:128],
                                            start=True, stop=True), "mmd")
                fence(C, sems=())

        if "hdbg" in io:
            with nc.Block() as b:
                @b.gpsimd
                def _(g):
                    for ic in range(IC1):
                        P.inc(g.dma_start(out=io["hdbg"][ic * 128:(ic + 1) * 128, :],
                                          in_=htile(ic)), "g", 16)
            fence(C, sems=("g",))

        with ExitStack() as s4:
            e4 = s4.enter_context
            w2s = e4(nc.sbuf_tensor("w2s_sl", [128, 2 * HID1P], F32))
            t2b = e4(nc.sbuf_tensor("t2b", [128, 2 * HID1P], BF16))
            sgA2 = e4(nc.sbuf_tensor("sgA2", [128, HID1P], BF16))
            sgB2 = e4(nc.sbuf_tensor("sgB2", [128, HID1P], BF16))
            fo = e4(nc.sbuf_tensor("fo_l1f", [128, 2 * TPC], F32))
            ps_d = [e4(nc.psum_tensor(f"l1f_psd{i}", [128, TPC], F32)) for i in range(2)]
            BFl = P.base("ld")["ld"]
            BFg = P.base("g")["g"]
            with nc.Block() as b:
                @b.sync
                def _(sy):
                    P.inc(sy.dma_start(out=slot2(w2s, 0, HID1P), in_=io["w2f"][0]), "ld", 16)
            for dc in range(MF):
                with nc.Block() as b:
                    @b.sync
                    def _(sy, dc=dc):
                        if dc + 1 < MF:
                            P.inc(sy.dma_start(out=slot2(w2s, dc + 1, HID1P),
                                               in_=io["w2f"][dc + 1]), "ld", 16)
                with nc.Block() as b:
                    @b.scalar
                    def _(s, dc=dc):
                        s.wait_ge(P.sem("ld"), BFl + 16 * (dc + 1))
                        s.activation(sgA2[:], slot2(w2s, dc, HID1P), AF.Sign, bias=thm[:, 2:3])
                        P.inc(s.activation(sgB2[:], slot2(w2s, dc, HID1P), AF.Sign, bias=thp[:, 2:3]),
                              "sg")
                with nc.Block() as b:
                    @b.vector
                    def _(v, dc=dc):
                        P.wait(v, "sg")
                        P.inc(v.tensor_tensor(slot2(t2b, dc, HID1P), sgA2[:], sgB2[:], ALU.add), "qtt")
                with nc.Block() as b:
                    @b.tensor
                    def _(tn, dc=dc):
                        P.wait(tn, "qtt")
                        for (h0, h1) in HV:
                            for ic in range(IC1):
                                last = tn.matmul(ps_d[dc % 2][:, h0:h1],
                                                 slot2(t2b, dc, HID1P)[:, ic * 128:(ic + 1) * 128],
                                                 h_sb[:, ic * TPC + h0:ic * TPC + h1],
                                                 start=(ic == 0), stop=(ic == IC1 - 1))
                        P.inc(last, "mm")
                with nc.Block() as b:
                    @b.scalar
                    def _(s, dc=dc):
                        P.wait(s, "mm")
                        if dc >= 2:
                            s.wait_ge(P.sem("g"), BFg + 16 * (dc - 1))
                        P.inc(s.activation(slot2(fo, dc, TPC), ps_d[dc % 2][:], AF.Identity,
                                           scale=s23[:, 0:1]), "evs")
                with nc.Block() as b:
                    @b.gpsimd
                    def _(g, dc=dc):
                        P.wait(g, "evs")
                        P.inc(g.dma_start(out=f_dram[dc * 128:(dc + 1) * 128, :],
                                          in_=slot2(fo, dc, TPC)), "g", 16)
            fence(C)

        af = e(nc.sbuf_tensor("af_l1f", [128, TPC], F32))
        bfc = e(nc.sbuf_tensor("bf_l1f", [128, TPC], F32))
        emit_stats(C, lambda mf: f_dram[mf * 128:(mf + 1) * 128, :], af, bfc)
        fence(C)
        emit_apply_post(C, lambda mf: f_dram[mf * 128:(mf + 1) * 128, :], xsrc, x_out, af, bfc)
        fence(C)



def _mk_io(nc, specs):
    io = {}
    for nm, shp, dt, kind in specs:
        io[nm] = nc.dram_tensor(nm, list(shp), dt, kind=kind)
    return io


DBG_F = False
DBG_LIGHT = False

_COMMON = [("x_in", [D, TPC], F32, "ExternalInput"),
           ("x_out", [D, TPC], F32, "ExternalOutput")]

_SPECS = {
    "l0a": _COMMON + [
        ("w_in_t", [NOC_IN, 128, D], F32, "ExternalInput"),
        ("b_in_t", [128, NOC_IN], F32, "ExternalInput"),
        ("w_out_t", [MF, 128, D], F32, "ExternalInput"),
        ("b_out_t", [128, MF], F32, "ExternalInput"),
        ("gamma_t", [128, MF], F32, "ExternalInput"),
        ("beta_t", [128, MF], F32, "ExternalInput"),
        ("rot_cos", [512, TPC], F32, "ExternalInput"),
        ("rot_sin", [512, TPC], F32, "ExternalInput"),
        ("offs", [1, 4], I32, "ExternalInput"),
        ("ar_in", [2, 2048], F32, "Internal"),
    ],
    "l0f": _COMMON + [
        ("gw_t", [IC0, 128, D], U8, "ExternalInput"),
        ("uw_t", [IC0, 128, D], U8, "ExternalInput"),
        ("dw_t", [MF, 128, HID0], U8, "ExternalInput"),
        ("gsc_t", [128, IC0], F32, "ExternalInput"),
        ("usc_t", [128, IC0], F32, "ExternalInput"),
        ("dsc_t", [128, MF], F32, "ExternalInput"),
    ],
    "l1a": _COMMON + [
        ("wq_t", [MF, 128, 256], F32, "ExternalInput"),
        ("wk_t", [MF, 128, 256], F32, "ExternalInput"),
        ("wv_t", [MF, 128, 256], F32, "ExternalInput"),
        ("gl", [2, 1], F32, "ExternalInput"),
        ("wo_t", [MF, 128, D], F32, "ExternalInput"),
        ("rope1", [128, SEQ], F32, "ExternalInput"),
        ("masks", [5, 128, 512], BF16, "ExternalInput"),
        ("ag_in", [D, TPC], BF16, "Internal"),
        ("lng", [NCORE * D, TPC], BF16, "Internal"),
        ("a2a_in", [D, TPC], BF16, "Internal"),
        ("attg", [D, TPC], BF16, "Internal"),
        ("gtmp", [2, 2], F32, "Internal"),
    ],
    "l1f": _COMMON + [
        ("w1f", [IC1, 128, D], F32, "ExternalInput"),
        ("w3f", [IC1, 128, D], F32, "ExternalInput"),
        ("w2f", [MF, 128, HID1P], F32, "ExternalInput"),
        ("w1a", [2, 128, HID1P], F32, "ExternalInput"),
        ("w3a", [2, 128, HID1P], F32, "ExternalInput"),
        ("w2a", [IC1, 128, 256], F32, "ExternalInput"),
        ("s_in", [1, 4], F32, "Internal"),
        ("s_out", [1, 4], F32, "Internal"),
        ("th_dram", [1, 4], F32, "Internal"),
    ],
}

_PHASES = {"l0a": ph_l0a, "l0f": ph_l0f, "l1a": ph_l1a, "l1f": ph_l1f}

_IN_NAMES = {ph: [nm for nm, _, _, k in sp if k == "ExternalInput"]
             for ph, sp in _SPECS.items()}

_PERCORE = {"x_in", "rot_cos", "rot_sin", "offs", "wq_t", "wk_t", "wv_t", "gl", "w1a", "w3a", "w2a"}


def build_phase(ph):
    nc = bass.Bass("TRN2", target_bir_lowering=False)
    specs = list(_SPECS[ph]) + [("f_dram", [D, TPC], F32,
                                 "ExternalOutput" if (DBG_F or DBG_LIGHT) else "Internal")]
    if ph == "l0a" and DBG_LIGHT:
        specs += [("q3dbg", [D, TPC], BF16, "ExternalOutput")]
    if ph == "l1f" and DBG_F:
        specs += [("thdbg", [1, 4], F32, "ExternalOutput"),
                  ("hdbg", [HID1P, TPC], BF16, "ExternalOutput")]
    if ph == "l0a":
        specs += [("ar_out", [2, 2048], F32, "Internal")]
        if DBG_F:
            specs += [("adbg", [128, MF], F32, "ExternalOutput"),
                      ("kdbg", [D, TPC], BF16, "ExternalOutput"),
                      ("vdbg", [D, TPC], BF16, "ExternalOutput"),
                      ("qdbg", [D, TPC], BF16, "ExternalOutput"),
                      ("xndbg", [D, TPC], BF16, "ExternalOutput"),
                      ("abdbg", [3 * 128, TPC], F32, "ExternalOutput"),
                      ("qdbg2", [D, TPC], BF16, "ExternalOutput")]
    io = _mk_io(nc, specs)
    with ExitStack() as stack:
        C = Ctx(nc, stack)
        C.init_consts()
        _PHASES[ph](C, io)
    return nc


def run_phase(ph, H, xs):
    nc = build_phase(ph)
    maps = []
    for c in range(NCORE):
        m = {"x_in": xs[c]}
        for nm in _IN_NAMES[ph]:
            if nm == "x_in":
                continue
            v = H[nm]
            m[nm] = v[c] if nm in _PERCORE else v
        maps.append(m)
    res = run_bass_kernel_spmd(nc, maps, core_ids=list(range(NCORE)))
    if DBG_F or DBG_LIGHT:
        run_phase.f = [res.results[c]["f_dram"] for c in range(NCORE)]
        if ph == "l0a" and not DBG_F:
            run_phase.q3 = [res.results[c]["q3dbg"] for c in range(NCORE)]
    if DBG_F:
        if ph == "l1f":
            run_phase.th = [res.results[c]["thdbg"] for c in range(NCORE)]
            run_phase.h = [res.results[c]["hdbg"] for c in range(NCORE)]
        if ph == "l0a":
            run_phase.ar = [res.results[c]["adbg"] for c in range(NCORE)]
            run_phase.kv = [(res.results[c]["kdbg"], res.results[c]["vdbg"],
                             res.results[c]["qdbg"]) for c in range(NCORE)]
            run_phase.xn = [res.results[c]["xndbg"] for c in range(NCORE)]
            run_phase.ab = [res.results[c]["abdbg"] for c in range(NCORE)]
            run_phase.q2 = [res.results[c]["qdbg2"] for c in range(NCORE)]
    return [res.results[c]["x_out"] for c in range(NCORE)]


def kernel(**inputs):
    H = _prep(inputs)
    xs = H["x0"]
    for ph in ("l0a", "l0f", "l1a", "l1f"):
        xs = run_phase(ph, H, xs)
    out = np.concatenate([np.asarray(x, np.float32).T for x in xs], axis=0)
    return np.ascontiguousarray(out.reshape(2, SEQ, D))



FUSE = True

_ALL_IN = []
for _ph in ("l0a", "l0f", "l1a", "l1f"):
    for _nm in _IN_NAMES[_ph]:
        if _nm not in ("x_in",) and _nm not in _ALL_IN:
            _ALL_IN.append(_nm)


def build_fused():
    nc = bass.Bass("TRN2", target_bir_lowering=False)
    io = {}
    io["x_in"] = nc.dram_tensor("x_in", [D, TPC], F32, kind="ExternalInput")
    io["x_out"] = nc.dram_tensor("x_out", [D, TPC], F32, kind="ExternalOutput")
    seen = {"x_in", "x_out"}
    for ph in ("l0a", "l0f", "l1a", "l1f"):
        for nm, shp, dt, kind in _SPECS[ph]:
            if nm in seen or nm in ("x_in", "x_out", "f_dram"):
                continue
            seen.add(nm)
            io[nm] = nc.dram_tensor(nm, list(shp), dt, kind=kind)
    io["f_dram"] = nc.dram_tensor("f_dram", [D, TPC], F32)
    io["ar_out"] = nc.dram_tensor("ar_out", [2, 2048], F32)
    x1 = nc.dram_tensor("x1d", [D, TPC], F32)
    x2 = nc.dram_tensor("x2d", [D, TPC], F32)
    x3 = nc.dram_tensor("x3d", [D, TPC], F32)
    with ExitStack() as stack:
        C = Ctx(nc, stack)
        C.init_consts()
        ph_l0a(C, {**io, "x_in": io["x_in"], "x_out": x1})
        ph_l0f(C, {**io, "x_in": x1, "x_out": x2})
        ph_l1a(C, {**io, "x_in": x2, "x_out": x3})
        ph_l1f(C, {**io, "x_in": x3, "x_out": io["x_out"]})
    return nc


def kernel(**inputs):
    H = _prep(inputs)
    if FUSE:
        nc = build_fused()
        maps = []
        for c in range(NCORE):
            m = {"x_in": H["x0"][c]}
            for nm in _ALL_IN:
                v = H[nm]
                m[nm] = v[c] if nm in _PERCORE else v
            maps.append(m)
        res = run_bass_kernel_spmd(nc, maps, core_ids=list(range(NCORE)))
        xs = [res.results[c]["x_out"] for c in range(NCORE)]
    else:
        xs = H["x0"]
        for ph in ("l0a", "l0f", "l1a", "l1f"):
            xs = run_phase(ph, H, xs)
    out = np.concatenate([np.asarray(x, np.float32).T for x in xs], axis=0)
    return np.ascontiguousarray(out.reshape(2, SEQ, D))
